# version 01
# baseline (speedup 1.0000x reference)
import numpy as np
from contextlib import ExitStack

import concourse.bass as bass
import concourse.mybir as mybir
from concourse.bass_utils import run_bass_kernel_spmd

F32 = mybir.dt.float32
BF16 = mybir.dt.bfloat16
AF = mybir.ActivationFunctionType
ALU = mybir.AluOpType
AX = mybir.AxisListType

MAGIC = float(np.float32(1.5 * 2 ** 23))
S = 4096
R = 2304
OWN = 2048
Q0 = S - R
D = 1024
CC = 8
NT = R // 128
ONT = OWN // 128
DFF = 4096
EPS = 1e-5
QCH = [(384 * i, 384) for i in range(6)]
PI = float(np.float32(np.pi))

ENGS = ("v", "s", "t", "g", "y")
NYL = 6
NGL = 4


class Prog:
    def __init__(self):
        self.ops = {e: [] for e in ENGS}
        self.cnt = {}
        self.lastw = {e: [] for e in ENGS}
        self.sem_h = {}
        self.rr = {"y": 0, "g": 0}

    def wait(self, e, sem, n):
        assert n <= self.cnt[sem], f"wait {sem}>={n}, emitted {self.cnt[sem]}"
        if n > 0:
            self.ops[e].append(
                lambda eng, sem=sem, n=n: eng.wait_ge(self.sem_h[sem], n))

    def waitd(self, e, tok):
        if tok is not None:
            self.wait(e, tok[0], tok[1])

    def _op(self, e, fn, inc, reads, writes):
        if e == "v" and any(r is w for r in reads for w in self.lastw["v"]):
            self.ops["v"].append(lambda eng: eng.drain())
        self.lastw[e] = list(writes)
        if inc is not None:
            self.cnt[inc] = self.cnt.get(inc, 0) + 1

        def run(eng, fn=fn, inc=inc):
            inst = fn(eng)
            if inc is not None:
                inst.then_inc(self.sem_h[inc], 1)
        self.ops[e].append(run)
        return self.cnt.get(inc)

    def _dma(self, e, fn):
        nl = NYL if e == "y" else NGL
        lane = f"d{e}{self.rr[e] % nl}"
        self.rr[e] += 1
        prev = self.cnt[lane]
        if prev > 0:
            self.ops[e].append(
                lambda eng, lane=lane, prev=prev: eng.wait_ge(
                    self.sem_h[lane], prev))
        self.cnt[lane] = prev + 16
        n = self.cnt[lane]

        def run(eng, fn=fn, lane=lane):
            fn(eng).then_inc(self.sem_h[lane], 16)
        self.ops[e].append(run)
        return (lane, n)

    def dma_done_all(self, e):
        nl = NYL if e == "y" else NGL
        return [(f"d{e}{i}", self.cnt[f"d{e}{i}"]) for i in range(nl)
                if self.cnt[f"d{e}{i}"] > 0]

    def v(self, fn, inc=None, reads=(), writes=()):
        return self._op("v", fn, inc, reads, writes)

    def s(self, fn, inc=None):
        return self._op("s", fn, inc, (), ())

    def t(self, fn, inc=None):
        return self._op("t", fn, inc, (), ())

    def g(self, fn):
        return self._dma("g", fn)

    def y(self, fn):
        return self._dma("y", fn)

    def barrier(self):
        gt = self.dma_done_all("g")
        yt = self.dma_done_all("y")
        for e in ("v", "s", "t", "g", "y"):
            for sem in ("v", "s", "t"):
                self.wait(e, sem, self.cnt[sem])
            for sem, n in gt + yt:
                self.wait(e, sem, n)


def build_program(upto="full"):
    nc = bass.Bass("TRN2", target_bir_lowering=False)
    P = Prog()

    xrot = nc.dram_tensor("xrot", [S, D], F32, kind="ExternalInput")
    aWqT = nc.dram_tensor("aWqT", [CC, 128, D], F32, kind="ExternalInput")
    aWkvT = nc.dram_tensor("aWkvT", [CC, 128, 2 * D], F32, kind="ExternalInput")
    aWpT = nc.dram_tensor("aWpT", [CC, 128, D], F32, kind="ExternalInput")
    gT = nc.dram_tensor("gT", [CC, 128, DFF], F32, kind="ExternalInput")
    uT = nc.dram_tensor("uT", [CC, 128, DFF], F32, kind="ExternalInput")
    dT = nc.dram_tensor("dT", [32, 128, D], F32, kind="ExternalInput")
    bWq = nc.dram_tensor("bWq", [CC, 128, D], F32, kind="ExternalInput")
    bWkd = nc.dram_tensor("bWkd", [CC, 128, 512], F32, kind="ExternalInput")
    bWv = nc.dram_tensor("bWv", [CC, 128, 256], F32, kind="ExternalInput")
    bWo = nc.dram_tensor("bWo", [CC, 128, D], F32, kind="ExternalInput")
    bWf1 = nc.dram_tensor("bWf1", [CC, 128, DFF], F32, kind="ExternalInput")
    bWf2 = nc.dram_tensor("bWf2", [32, 128, D], F32, kind="ExternalInput")
    brows = nc.dram_tensor("brows", [8, D], F32, kind="ExternalInput")
    bkvrow = nc.dram_tensor("bkvrow", [2, 512], F32, kind="ExternalInput")
    bqcol = nc.dram_tensor("bqcol", [128, 8], F32, kind="ExternalInput")
    bkdcol = nc.dram_tensor("bkdcol", [128, 4], F32, kind="ExternalInput")
    bf1col = nc.dram_tensor("bf1col", [128, 32], F32, kind="ExternalInput")
    masks_d = nc.dram_tensor("masks", [16, 128, 384], F32, kind="ExternalInput")
    posarr = nc.dram_tensor("posarr", [1, R], F32, kind="ExternalInput")
    thsg = nc.dram_tensor("thsg", [128, 2], F32, kind="ExternalInput")
    ident = nc.dram_tensor("ident", [128, 128], F32, kind="ExternalInput")
    perm_d = nc.dram_tensor("perm", [128, 128], F32, kind="ExternalInput")
    cpool_d = nc.dram_tensor("cpool", [128, 16], F32, kind="ExternalInput")

    out = nc.dram_tensor("out", [OWN, D], F32, kind="ExternalOutput")

    scr_x2 = nc.dram_tensor("scr_x2", [R, D], F32, kind="Internal")
    scr_x3 = nc.dram_tensor("scr_x3", [R, D], F32, kind="Internal")
    scr_y = nc.dram_tensor("scr_y", [R, D], F32, kind="Internal")
    scr_xb = nc.dram_tensor("scr_xb", [OWN, D], F32, kind="Internal")
    scr_x4 = nc.dram_tensor("scr_x4", [OWN, D], F32, kind="Internal")
    scr_xnq1 = nc.dram_tensor("scr_xnq1", [NT, 128, D], BF16, kind="Internal")
    scr_xnq2 = nc.dram_tensor("scr_xnq2", [ONT, 128, D], BF16, kind="Internal")
    scr_n8h1 = nc.dram_tensor("scr_n8h1", [NT, 128, DFF], BF16, kind="Internal")
    scr_n8h2 = nc.dram_tensor("scr_n8h2", [ONT, 128, DFF], BF16,
                              kind="Internal")
    scr_tern = [nc.dram_tensor(f"scr_tern{i}", [128, 32768], BF16,
                               kind="Internal") for i in range(3)]
    scr_h2q = nc.dram_tensor("scr_h2q", [8, 128, CC * 256], BF16,
                             kind="Internal")
    scr_h1q = nc.dram_tensor("scr_h1q", [128, CC * R], BF16, kind="Internal")

    cvals = []

    def ccol(v):
        v = float(np.float32(v))
        if v not in cvals:
            cvals.append(v)
            assert len(cvals) <= 16
        return cvals.index(v)

    with ExitStack() as TOP:
        def sbuf(stk, name, shape, dtype):
            return stk.enter_context(nc.sbuf_tensor(name, shape, dtype))

        def psum(stk, name, shape, dtype=F32):
            return stk.enter_context(nc.psum_tensor(name, shape, dtype))

        sems = ["v", "s", "t"] + [f"dy{i}" for i in range(NYL)] + \
            [f"dg{i}" for i in range(NGL)]
        for snm in sems:
            P.sem_h[snm] = TOP.enter_context(nc.semaphore("sem_" + snm))
            P.cnt[snm] = 0

        cp = sbuf(TOP, "cp", [128, 16], F32)
        id32 = sbuf(TOP, "id32", [128, 128], F32)
        id16 = sbuf(TOP, "id16", [128, 128], BF16)
        ones16 = sbuf(TOP, "ones16", [128, 128], BF16)
        ones32 = sbuf(TOP, "ones32", [128, 128], F32)
        perm32 = sbuf(TOP, "perm32", [128, 128], F32)
        mcol = sbuf(TOP, "mcol", [128, 8], F32)
        srec1 = sbuf(TOP, "srec1", [128, NT], F32)
        dsc1 = sbuf(TOP, "dsc1", [128, NT], F32)
        srec2 = sbuf(TOP, "srec2", [128, ONT], F32)
        dsc2 = sbuf(TOP, "dsc2", [128, ONT], F32)

        def cap(v):
            return cp[:, ccol(v):ccol(v) + 1]

        for t in [P.y(lambda e: e.dma_start(cp[:, :], cpool_d[:, :])),
                  P.y(lambda e: e.dma_start(id32[:, :], ident[:, :])),
                  P.y(lambda e: e.dma_start(perm32[:, :], perm_d[:, :]))]:
            P.waitd("v", t)
        P.v(lambda e: e.memset(ones16[:, :], 1.0), writes=[ones16])
        P.v(lambda e: e.memset(ones32[:, :], 1.0), writes=[ones32])
        P.v(lambda e: e.tensor_copy(id16[:, :], id32[:, :]), inc="v",
            reads=[id32], writes=[id16])

        def ln_stats(stt, xt, sq, mul):
            P.wait("s", "v", P.cnt["v"])
            P.s(lambda e: e.activation(sq[:, :], xt[:, :], AF.Square,
                                       accum_out=stt[:, 1:2]), inc="s")
            P.v(lambda e: e.reduce_sum(stt[:, 0:1], xt[:, :], AX.X),
                reads=[xt], writes=[stt])
            P.wait("v", "s", P.cnt["s"])
            P.v(lambda e: e.tensor_scalar(stt[:, 2:3], stt[:, 0:1], 1.0 / D,
                                          None, ALU.mult), reads=[stt],
                writes=[stt])
            P.v(lambda e: e.tensor_scalar(stt[:, 3:4], stt[:, 1:2], 1.0 / D,
                                          None, ALU.mult), reads=[stt],
                writes=[stt])
            P.v(lambda e: e.scalar_tensor_tensor(stt[:, 4:5], stt[:, 2:3],
                                                 stt[:, 2:3], stt[:, 3:4],
                                                 ALU.mult, ALU.subtract),
                reads=[stt], writes=[stt], inc="v")
            P.wait("s", "v", P.cnt["v"])
            P.s(lambda e: e.activation(stt[:, 5:6], stt[:, 4:5], AF.Sqrt,
                                       bias=cap(EPS), scale=-1.0), inc="s")
            P.wait("v", "s", P.cnt["s"])
            P.v(lambda e: e.reciprocal(stt[:, 6:7], stt[:, 5:6]),
                reads=[stt], writes=[stt])
            P.v(lambda e: e.tensor_scalar(stt[:, 7:8], stt[:, 6:7], mul, None,
                                          ALU.mult), reads=[stt], writes=[stt])

        def quant_chunks(dst, dst_off, src, n, w, finalmul, tmp,
                         lo=-128.0, hi=127.0):
            for j in range(n):
                b = tmp[j % 2]
                P.v(lambda e, j=j, b=b: e.tensor_scalar(
                    b[:, 0:w], src[:, j * w:(j + 1) * w], 16.0, MAGIC,
                    ALU.mult, ALU.add), reads=[src], writes=[b])
                P.v(lambda e, b=b: e.tensor_scalar(
                    b[:, 0:w], b[:, 0:w], MAGIC, hi, ALU.subtract, ALU.min),
                    reads=[b], writes=[b])
                P.v(lambda e, j=j, b=b: e.tensor_scalar(
                    dst[:, dst_off + j * w:dst_off + (j + 1) * w], b[:, 0:w],
                    lo, finalmul, ALU.max, ALU.mult),
                    reads=[b], writes=[dst], inc="v")

        def lnq_tile(stt, xo, sq, lnx, n8, aq, srec, nt):
            ln_stats(stt, xo, sq, 1.0)
            P.v(lambda e: e.tensor_scalar(
                lnx[:, :], xo[:, :], stt[:, 2:3], stt[:, 6:7],
                ALU.subtract, ALU.mult), reads=[xo, stt], writes=[lnx])
            P.v(lambda e: e.tensor_reduce(
                aq[:, 0:1], lnx[:, :], AX.X, ALU.max,
                apply_absolute_value=True), reads=[lnx], writes=[aq])
            P.v(lambda e: e.tensor_scalar(
                aq[:, 1:2], aq[:, 0:1], 1e-5, 1.0 / 127.0, ALU.max, ALU.mult),
                reads=[aq], writes=[aq])
            P.v(lambda e, nt=nt: e.tensor_copy(
                srec[:, nt:nt + 1], aq[:, 1:2]), reads=[aq], writes=[srec])
            P.v(lambda e: e.reciprocal(aq[:, 2:3], aq[:, 1:2]),
                reads=[aq], writes=[aq])
            P.v(lambda e: e.tensor_scalar(
                lnx[:, :], lnx[:, :], aq[:, 2:3], MAGIC, ALU.mult, ALU.add),
                reads=[lnx, aq], writes=[lnx])
            P.v(lambda e: e.tensor_scalar(
                n8[:, :], lnx[:, :], MAGIC, None, ALU.subtract),
                reads=[lnx], writes=[n8], inc="v")

        def transpose_1024(n8, pst, sevac_prev):
            P.wait("t", "v", P.cnt["v"])
            if sevac_prev:
                P.wait("t", "s", sevac_prev)
            for cc in range(CC):
                P.t(lambda e, cc=cc: e.transpose(
                    pst[:, 128 * cc:128 * cc + 128],
                    n8[:, 128 * cc:128 * cc + 128], id16[:, :]),
                    inc=("t" if cc == CC - 1 else None))
            P.wait("s", "t", P.cnt["t"])

        with ExitStack() as SC23:
            zq = sbuf(SC23, "zq", [128, CC * R], BF16)
            with ExitStack() as SC12:
                xqf = sbuf(SC12, "xqf", [128, CC * S], BF16)
                with ExitStack() as PH1:
                    xin = [sbuf(PH1, f"xin{i}", [128, D], F32) for i in range(2)]
                    sqt = [sbuf(PH1, f"sqt{i}", [128, D], F32) for i in range(2)]
                    lnt = [sbuf(PH1, f"lnt{i}", [128, D], F32) for i in range(2)]
                    qtm = [sbuf(PH1, f"qtm{i}", [128, D], BF16) for i in range(2)]
                    stt_ = [sbuf(PH1, f"stt{i}", [128, 8], F32) for i in range(2)]
                    pstr = [psum(PH1, f"pstr{i}", [128, 1024], BF16)
                            for i in range(2)]
                    sevac = [0, 0]
                    tdone = [0, 0]
                    for i in range(32):
                        q = i % 2
                        xt, sq, ln, qt, stt = (xin[q], sqt[q], lnt[q], qtm[q],
                                               stt_[q])
                        ps = pstr[q]
                        if tdone[q]:
                            P.wait("y", "t", tdone[q])
                        tk = P.y(lambda e, xt=xt, r0=128 * i: e.dma_start(
                            xt[:, :], xrot[r0:r0 + 128, :]))
                        P.waitd("v", tk)
                        P.waitd("s", tk)
                        ln_stats(stt, xt, sq, 16.0)
                        P.v(lambda e, ln=ln, xt=xt, stt=stt: e.tensor_scalar(
                            ln[:, :], xt[:, :], stt[:, 2:3], stt[:, 7:8],
                            ALU.subtract, ALU.mult), reads=[xt, stt],
                            writes=[ln])
                        P.v(lambda e, ln=ln: e.tensor_scalar(
                            ln[:, :], ln[:, :], MAGIC, MAGIC, ALU.add,
                            ALU.subtract), reads=[ln], writes=[ln])
                        P.v(lambda e, qt=qt, ln=ln: e.tensor_scalar(
                            qt[:, :], ln[:, :], 127.0, -128.0, ALU.min,
                            ALU.max), reads=[ln], writes=[qt], inc="v")
                        transpose_1024(qt, ps, sevac[q])
                        tdone[q] = P.cnt["t"]
                        for cc in range(CC):
                            P.s(lambda e, cc=cc, i=i, ps=ps: e.copy(
                                xqf[:, cc * S + 128 * i:cc * S + 128 * i + 128],
                                ps[:, 128 * cc:128 * cc + 128]),
                                inc=("s" if cc == CC - 1 else None))
                        sevac[q] = P.cnt["s"]

                with ExitStack() as PH2:
                    qq = sbuf(PH2, "qq", [128, CC * R], BF16)
                    P.barrier()
                    with ExitStack() as QW:
                        wqf = sbuf(QW, "wqf", [128, CC * D], F32)
                        wq16f = sbuf(QW, "wq16f", [128, CC * D], BF16)
                        wtmp = [sbuf(QW, f"wtmpA{i}", [128, 512], F32)
                                for i in range(2)]
                        evq = [sbuf(QW, f"evqA{i}", [128, 512], F32)
                               for i in range(2)]
                        ps_w = [psum(QW, f"ps_wA{i}", [128, 512])
                                for i in range(2)]
                        for t in [P.y(lambda e, cc=cc: e.dma_start(
                                wqf[:, cc * D:(cc + 1) * D], aWqT[cc, :, :]))
                                for cc in range(CC)]:
                            P.waitd("v", t)
                        quant_chunks(wq16f, 0, wqf, CC * 2, 512, 1.0 / 256.0,
                                     wtmp)
                        P.wait("t", "v", P.cnt["v"])
                        bankv = [0, 0]
                        idx = 0
                        for oc in range(CC):
                            for (c0, cw) in QCH:
                                bk = idx % 2
                                idx += 1
                                pw = ps_w[bk]
                                if bankv[bk]:
                                    P.wait("t", "v", bankv[bk])
                                for cc in range(CC):
                                    P.t(lambda e, pw=pw, cc=cc, oc=oc, c0=c0,
                                        cw=cw: e.matmul(
                                            pw[:, 0:cw],
                                            wq16f[:, cc * D + 128 * oc:
                                                  cc * D + 128 * oc + 128],
                                            xqf[:, cc * S + Q0 + c0:
                                                cc * S + Q0 + c0 + cw],
                                            start=(cc == 0),
                                            stop=(cc == CC - 1)),
                                        inc=("t" if cc == CC - 1 else None))
                                P.wait("v", "t", P.cnt["t"])
                                ev = evq[bk]
                                P.v(lambda e, pw=pw, ev=ev, cw=cw:
                                    e.tensor_scalar(
                                        ev[:, 0:cw], pw[:, 0:cw], 16.0, MAGIC,
                                        ALU.mult, ALU.add), writes=[ev],
                                    inc="v")
                                bankv[bk] = P.cnt["v"]
                                P.v(lambda e, ev=ev, cw=cw: e.tensor_scalar(
                                    ev[:, 0:cw], ev[:, 0:cw], MAGIC, 127.0,
                                    ALU.subtract, ALU.min), reads=[ev],
                                    writes=[ev])
                                P.v(lambda e, ev=ev, oc=oc, c0=c0, cw=cw:
                                    e.tensor_scalar(
                                        qq[:, oc * R + c0:oc * R + c0 + cw],
                                        ev[:, 0:cw], -128.0, 1.0,
                                        ALU.max, ALU.mult),
                                    reads=[ev], writes=[qq], inc="v")

                    if upto in ("xqf", "qq1"):
                        dump = xqf if upto == "xqf" else qq
                        P.wait("g", "s", P.cnt["s"])
                        P.wait("g", "v", P.cnt["v"])
                        for j in range(16):
                            P.g(lambda e, j=j, dump=dump: e.dma_start(
                                out[128 * j:128 * j + 128, :],
                                dump[:, j * D:(j + 1) * D]))
                        for sem, n in P.dma_done_all("g"):
                            P.wait("g", sem, n)
                        emit_blocks(nc, P)
                        return nc, cvals

                    P.barrier()
                    ebuf = sbuf(PH2, "ebuf", [128, 32 * 384], BF16)
                    kpair = sbuf(PH2, "kpair", [128, S], BF16)
                    vpair = sbuf(PH2, "vpair", [128, 32 * 128], BF16)
                    wldf = sbuf(PH2, "wldf", [128, CC * 128], F32)
                    wp16 = sbuf(PH2, "wp16", [128, CC * 128], BF16)
                    wtmp2 = [sbuf(PH2, f"wtmpB{i}", [128, 512], F32)
                             for i in range(2)]
                    evp = [sbuf(PH2, f"evpB{i}", [128, 512], F32)
                           for i in range(2)]
                    q8c = [sbuf(PH2, f"q8c{i}", [128, 384], F32)
                           for i in range(2)]
                    zrf = sbuf(PH2, "zrf", [128, 384], F32)
                    zr16 = sbuf(PH2, "zr16", [128, 384], BF16)
                    p16f = [sbuf(PH2, f"p16f{i}", [128, 384], BF16)
                            for i in range(2)]
                    p16 = [sbuf(PH2, f"p16{i}", [128, 384], BF16)
                           for i in range(2)]
                    ps_s = [psum(PH2, f"ps_s{i}", [128, 384]) for i in range(2)]
                    ps_z = psum(PH2, "ps_z", [128, 384])
                    ps_avA = psum(PH2, "ps_avA", [128, 384])
                    ps_avB = psum(PH2, "ps_avB", [128, 384])
                    ps_w2 = [psum(PH2, f"ps_wB{i}", [128, 512])
                             for i in range(2)]

                    sid = [0, 0]
                    zrv_last = 0
                    p16v_last = 0
                    av_free = {0: 0, 64: 0}
                    bankv2 = [0, 0]
                    wldf_free_v = 0

                    def kv_cols(col0, store, finalmul, wid):
                        nonlocal wldf_free_v, bankv2
                        toks = []
                        if wldf_free_v:
                            P.wait("y", "v", wldf_free_v)
                        for cc in range(CC):
                            toks.append(P.y(lambda e, cc=cc: e.dma_start(
                                wldf[:, cc * 128:cc * 128 + 128],
                                aWkvT[cc, :, col0:col0 + 128])))
                        for t in toks:
                            P.waitd("v", t)
                        quant_chunks(wp16, 0, wldf, CC, 128, 1.0 / 256.0,
                                     wtmp2)
                        wldf_free_v = P.cnt["v"]
                        P.wait("t", "v", P.cnt["v"])
                        if wid == 512:
                            for nch in range(8):
                                bk = nch % 2
                                pw = ps_w2[bk]
                                if bankv2[bk]:
                                    P.wait("t", "v", bankv2[bk])
                                for cc in range(CC):
                                    P.t(lambda e, pw=pw, cc=cc, nch=nch:
                                        e.matmul(
                                            pw[:, :],
                                            wp16[:, 128 * cc:128 * cc + 128],
                                            xqf[:, cc * S + 512 * nch:
                                                cc * S + 512 * nch + 512],
                                            start=(cc == 0),
                                            stop=(cc == CC - 1)),
                                        inc=("t" if cc == CC - 1 else None))
                                P.wait("v", "t", P.cnt["t"])
                                ev = evp[bk]
                                P.v(lambda e, pw=pw, ev=ev: e.tensor_scalar(
                                    ev[:, :], pw[:, :], 16.0, MAGIC, ALU.mult,
                                    ALU.add), writes=[ev], inc="v")
                                bankv2[bk] = P.cnt["v"]
                                P.v(lambda e, ev=ev: e.tensor_scalar(
                                    ev[:, :], ev[:, :], MAGIC, 127.0,
                                    ALU.subtract, ALU.min), reads=[ev],
                                    writes=[ev])
                                P.v(lambda e, ev=ev, nch=nch, store=store,
                                    finalmul=finalmul: e.tensor_scalar(
                                        store[:, 512 * nch:512 * nch + 512],
                                        ev[:, :], -128.0, finalmul,
                                        ALU.max, ALU.mult),
                                    reads=[ev], writes=[store], inc="v")
                        else:
                            for ntl in range(32):
                                bk = ntl % 2
                                pw = ps_w2[bk]
                                if bankv2[bk]:
                                    P.wait("t", "v", bankv2[bk])
                                for cc in range(CC):
                                    P.t(lambda e, pw=pw, cc=cc, ntl=ntl:
                                        e.matmul(
                                            pw[:, 0:128],
                                            xqf[:, cc * S + 128 * ntl:
                                                cc * S + 128 * ntl + 128],
                                            wp16[:, 128 * cc:128 * cc + 128],
                                            start=(cc == 0),
                                            stop=(cc == CC - 1)),
                                        inc=("t" if cc == CC - 1 else None))
                                P.wait("v", "t", P.cnt["t"])
                                ev = evp[bk]
                                P.v(lambda e, pw=pw, ev=ev: e.tensor_scalar(
                                    ev[:, 0:128], pw[:, 0:128], 16.0, MAGIC,
                                    ALU.mult, ALU.add), writes=[ev], inc="v")
                                bankv2[bk] = P.cnt["v"]
                                P.v(lambda e, ev=ev: e.tensor_scalar(
                                    ev[:, 0:128], ev[:, 0:128], MAGIC, 127.0,
                                    ALU.subtract, ALU.min), reads=[ev],
                                    writes=[ev])
                                P.v(lambda e, ev=ev, ntl=ntl, store=store,
                                    finalmul=finalmul: e.tensor_scalar(
                                        store[:, 128 * ntl:128 * ntl + 128],
                                        ev[:, 0:128], -128.0, finalmul,
                                        ALU.max, ALU.mult),
                                    reads=[ev], writes=[store], inc="v")

                    for hp in range(CC):
                        kv_cols(128 * hp, kpair, 1.0 / 128.0, 512)
                        if upto == "kp0" and hp == 0:
                            P.wait("g", "s", P.cnt["s"])
                            P.wait("g", "v", P.cnt["v"])
                            for j in range(4):
                                P.g(lambda e, j=j: e.dma_start(
                                    out[128 * j:128 * j + 128, :],
                                    kpair[:, j * D:(j + 1) * D]))
                            for sem, n in P.dma_done_all("g"):
                                P.wait("g", sem, n)
                            emit_blocks(nc, P)
                            return nc, cvals
                        kv_cols(D + 128 * hp, vpair, 1.0 / 16.0, 128)
                        if upto == "kpv" and hp == 0:
                            P.wait("g", "s", P.cnt["s"])
                            P.wait("g", "v", P.cnt["v"])
                            for j in range(4):
                                P.g(lambda e, j=j: e.dma_start(
                                    out[128 * j:128 * j + 128, :],
                                    kpair[:, j * D:(j + 1) * D]))
                            for sem, n in P.dma_done_all("g"):
                                P.wait("g", sem, n)
                            emit_blocks(nc, P)
                            return nc, cvals
                        kv_done_v = P.cnt["v"]
                        for hh in range(2):
                            base = 64 * hh
                            ps_av = ps_avA if hh == 0 else ps_avB
                            avlo, avhi = base, base + 64
                            for (c0, cw) in QCH:
                                P.wait("t", "v", kv_done_v)
                                if p16v_last:
                                    P.wait("s", "v", p16v_last)
                                for kt in range(32):
                                    pss = ps_s[kt % 2]
                                    if sid[kt % 2]:
                                        P.wait("t", "s", sid[kt % 2])
                                    P.t(lambda e, pss=pss, kt=kt, base=base,
                                        hp=hp, c0=c0, cw=cw: e.matmul(
                                            pss[:, 0:cw],
                                            kpair[base:base + 64,
                                                  128 * kt:128 * kt + 128],
                                            qq[base:base + 64,
                                               hp * R + c0:hp * R + c0 + cw],
                                            start=True, stop=True,
                                            tile_position=(base, 0)), inc="t")
                                    P.wait("s", "t", P.cnt["t"])
                                    qc = q8c[kt % 2]
                                    P.s(lambda e, qc=qc, pss=pss, cw=cw:
                                        e.activation(
                                            qc[:, 0:cw], pss[:, 0:cw],
                                            AF.Identity, bias=cap(MAGIC),
                                            scale=1.0), inc="s")
                                    sid[kt % 2] = P.cnt["s"]
                                    P.s(lambda e, qc=qc, kt=kt, cw=cw:
                                        e.activation(
                                            ebuf[:, 384 * kt:384 * kt + cw],
                                            qc[:, 0:cw], AF.Exp,
                                            bias=cap(-(8.0 + MAGIC / 16.0)),
                                            scale=1.0 / 16.0), inc="s")
                                exp_done = P.cnt["s"]
                                P.wait("t", "s", exp_done)
                                if zrv_last:
                                    P.wait("t", "v", zrv_last)
                                for kt in range(32):
                                    P.t(lambda e, kt=kt, cw=cw: e.matmul(
                                        ps_z[:, 0:cw], ones16[:, :],
                                        ebuf[:, 384 * kt:384 * kt + cw],
                                        start=(kt == 0), stop=(kt == 31)),
                                        inc=("t" if kt == 31 else None))
                                P.wait("v", "t", P.cnt["t"])
                                P.v(lambda e, cw=cw: e.reciprocal(
                                    zrf[:, 0:cw], ps_z[:, 0:cw]),
                                    writes=[zrf], inc="v")
                                zrv_last = P.cnt["v"]
                                P.v(lambda e, cw=cw: e.tensor_scalar(
                                    zr16[:, 0:cw], zrf[:, 0:cw], 16.0, None,
                                    ALU.mult), reads=[zrf], writes=[zr16],
                                    inc="v")
                                if av_free[base]:
                                    P.wait("t", "v", av_free[base])
                                for kt in range(32):
                                    pf, pq = p16f[kt % 2], p16[kt % 2]
                                    P.v(lambda e, pf=pf, kt=kt, cw=cw:
                                        e.tensor_tensor(
                                            pf[:, 0:cw],
                                            ebuf[:, 384 * kt:384 * kt + cw],
                                            zr16[:, 0:cw], ALU.mult),
                                        reads=[ebuf, zr16], writes=[pf])
                                    P.v(lambda e, pf=pf, pq=pq, cw=cw:
                                        e.tensor_scalar(
                                            pq[:, 0:cw], pf[:, 0:cw], MAGIC,
                                            MAGIC, ALU.add, ALU.subtract),
                                        reads=[pf], writes=[pq], inc="v")
                                    P.wait("t", "v", P.cnt["v"])
                                    P.t(lambda e, pq=pq, kt=kt, base=base,
                                        cw=cw, ps_av=ps_av, avlo=avlo,
                                        avhi=avhi: e.matmul(
                                            ps_av[avlo:avhi, 0:cw],
                                            vpair[:, 128 * kt + base:
                                                  128 * kt + base + 64],
                                            pq[:, 0:cw], start=(kt == 0),
                                            stop=(kt == 31),
                                            tile_position=(0, avlo)),
                                        inc=("t" if kt == 31 else None))
                                p16v_last = P.cnt["v"]
                                P.wait("v", "t", P.cnt["t"])
                                ev = evp[0]
                                P.v(lambda e, ps_av=ps_av, ev=ev, avlo=avlo,
                                    avhi=avhi, cw=cw: e.tensor_scalar(
                                        ev[avlo:avhi, 0:cw],
                                        ps_av[avlo:avhi, 0:cw], MAGIC, MAGIC,
                                        ALU.add, ALU.subtract),
                                    writes=[ev], inc="v")
                                av_free[base] = P.cnt["v"]
                                P.v(lambda e, ev=ev, hp=hp, c0=c0, cw=cw,
                                    avlo=avlo, avhi=avhi: e.tensor_scalar(
                                        zq[avlo:avhi,
                                           hp * R + c0:hp * R + c0 + cw],
                                        ev[avlo:avhi, 0:cw], 127.0, -128.0,
                                        ALU.min, ALU.max),
                                    reads=[ev], writes=[zq], inc="v")
                    if upto == "kpa" and False:
                        P.wait("g", "s", P.cnt["s"])
                        P.wait("g", "v", P.cnt["v"])
                        for j in range(4):
                            P.g(lambda e, j=j: e.dma_start(
                                out[128 * j:128 * j + 128, :],
                                kpair[:, j * D:(j + 1) * D]))
                        for sem, n in P.dma_done_all("g"):
                            P.wait("g", sem, n)
                        emit_blocks(nc, P)
                        return nc, cvals
                    if upto in ("zq1", "kp1", "eb1"):
                        dump = {"zq1": zq, "kp1": kpair, "eb1": ebuf}[upto]
                        P.wait("g", "s", P.cnt["s"])
                        P.wait("g", "v", P.cnt["v"])
                        for j in range(min(16, dump.shape[1] // D)):
                            P.g(lambda e, j=j, dump=dump: e.dma_start(
                                out[128 * j:128 * j + 128, :],
                                dump[:, j * D:(j + 1) * D]))
                        for sem, n in P.dma_done_all("g"):
                            P.wait("g", sem, n)
                        emit_blocks(nc, P)
                        return nc, cvals

            with ExitStack() as PH3:
                P.barrier()
                wpf = sbuf(PH3, "wpf", [128, CC * D], F32)
                wp16p = sbuf(PH3, "wp16p", [128, CC * D], BF16)
                wtmp3 = [sbuf(PH3, f"w3tmp{i}", [128, 512], F32)
                         for i in range(2)]
                bprep = sbuf(PH3, "bprep", [128, D], F32)
                brow = sbuf(PH3, "brow", [1, 8 * D], F32)
                x1t = [sbuf(PH3, f"x1t{i}", [128, D], F32) for i in range(2)]
                x2t = [sbuf(PH3, f"x2t{i}", [128, D], F32) for i in range(2)]
                sqt3 = [sbuf(PH3, f"sqt3{i}", [128, D], F32) for i in range(2)]
                lnt3 = [sbuf(PH3, f"lnt3{i}", [128, D], F32) for i in range(2)]
                n8t = [sbuf(PH3, f"n8t{i}", [128, D], BF16) for i in range(2)]
                xnqt = [sbuf(PH3, f"xnqt{i}", [128, D], BF16) for i in range(2)]
                stt3 = [sbuf(PH3, f"stt3{i}", [128, 8], F32) for i in range(2)]
                aqs = [sbuf(PH3, f"aqs{i}", [128, 4], F32) for i in range(2)]
                ps_o = [psum(PH3, f"ps_o{i}", [128, 512]) for i in range(2)]
                ps_b3 = psum(PH3, "ps_b3", [128, 1024])
                ps_t3 = [psum(PH3, f"ps_t3{i}", [128, 1024], BF16)
                         for i in range(2)]

                toks = [P.y(lambda e, cc=cc: e.dma_start(
                    wpf[:, cc * D:(cc + 1) * D], aWpT[cc, :, :]))
                    for cc in range(CC)]
                tbr = P.y(lambda e: e.dma_start(
                    brow[0:1, :], brows[:, :].rearrange("r d -> () (r d)")))
                for t in toks:
                    P.waitd("v", t)
                quant_chunks(wp16p, 0, wpf, CC * 2, 512, 1.0 / 256.0, wtmp3)
                P.waitd("t", tbr)
                P.wait("t", "v", P.cnt["v"])
                for half in range(2):
                    P.t(lambda e, half=half: e.matmul(
                        ps_b3[:, 512 * half:512 * half + 512], ones32[0:1, :],
                        brow[0:1, 5 * D + 512 * half:5 * D + 512 * half + 512],
                        start=True, stop=True),
                        inc=("t" if half else None))
                P.wait("v", "t", P.cnt["t"])
                P.v(lambda e: e.tensor_copy(bprep[:, :], ps_b3[:, :]),
                    writes=[bprep], inc="v")

                x2done = [0, 0]
                x1_used = [0, 0]
                sevac3 = [0, 0]
                x2g = [None, None]
                xqg = [None, None]
                for nt in range(NT):
                    q = nt % 2
                    xa, xb_ = x1t[q], x2t[q]
                    sq3, ln3, n8, xq3, stt, aq = (sqt3[q], lnt3[q], n8t[q],
                                                  xnqt[q], stt3[q], aqs[q])
                    if x1_used[q]:
                        P.wait("y", "v", x1_used[q])
                    tk1 = P.y(lambda e, xa=xa, nt=nt: e.dma_start(
                        xa[:, :], xrot[Q0 + 128 * nt:Q0 + 128 * nt + 128, :]))
                    for oh in range(2):
                        po = ps_o[oh]
                        if x2done[oh]:
                            P.wait("t", "v", x2done[oh])
                        for cc in range(CC):
                            P.t(lambda e, po=po, cc=cc, nt=nt, oh=oh: e.matmul(
                                po[:, :],
                                zq[:, cc * R + 128 * nt:cc * R + 128 * nt + 128],
                                wp16p[:, cc * D + 512 * oh:
                                      cc * D + 512 * oh + 512],
                                start=(cc == 0), stop=(cc == CC - 1)),
                                inc=("t" if cc == CC - 1 else None))
                    P.wait("v", "t", P.cnt["t"])
                    P.waitd("v", tk1)
                    if x2g[q] is not None:
                        P.waitd("v", x2g[q])
                    for oh in range(2):
                        po = ps_o[oh]
                        P.v(lambda e, po=po, xb_=xb_, oh=oh:
                            e.scalar_tensor_tensor(
                                xb_[:, 512 * oh:512 * oh + 512], po[:, :], 1.0,
                                bprep[:, 512 * oh:512 * oh + 512],
                                ALU.mult, ALU.add),
                            reads=[bprep], writes=[xb_], inc="v")
                        x2done[oh] = P.cnt["v"]
                    P.v(lambda e, xb_=xb_, xa=xa: e.tensor_add(
                        xb_[:, :], xb_[:, :], xa[:, :]),
                        reads=[xb_, xa], writes=[xb_], inc="v")
                    x1_used[q] = P.cnt["v"]
                    x2_v = P.cnt["v"]
                    P.wait("g", "v", x2_v)
                    x2g[q] = P.g(lambda e, xb_=xb_, nt=nt: e.dma_start(
                        scr_x2[128 * nt:128 * nt + 128, :], xb_[:, :]))
                    lnq_tile(stt, xb_, sq3, ln3, n8, aq, srec1, nt)
                    ps3 = ps_t3[q]
                    transpose_1024(n8, ps3, sevac3[q])
                    if xqg[q] is not None:
                        P.waitd("s", xqg[q])
                    P.s(lambda e, xq3=xq3, ps3=ps3: e.copy(xq3[:, :],
                                                           ps3[:, :]),
                        inc="s")
                    sevac3[q] = P.cnt["s"]
                    P.wait("g", "s", P.cnt["s"])
                    xqg[q] = P.g(lambda e, xq3=xq3, nt=nt: e.dma_start(
                        scr_xnq1[nt, :, :], xq3[:, :]))

                if upto == "x2":
                    finish_from_scratch(nc, P, scr_x2, out, PH3)
                    return nc, cvals

        def ffn_A(nt_count, scr_xnq, scr_n8h, srec, dsc, stk, tg=''):
            P.barrier()
            terng_sb = sbuf(stk, "terng_sb" + tg, [128, 32768], BF16)
            ternu_sb = sbuf(stk, "ternu_sb" + tg, [128, 32768], BF16)
            xnt = [sbuf(stk, f"xnt{i}" + tg, [128, D], BF16) for i in range(2)]
            sigt = sbuf(stk, "sigt" + tg, [128, 512], F32)
            tt1 = sbuf(stk, "tt1" + tg, [128, 512], F32)
            prod = sbuf(stk, "prod" + tg, [128, DFF], F32)
            n8h = sbuf(stk, "n8h" + tg, [128, DFF], BF16)
            n8hT = [sbuf(stk, f"n8hT{i}" + tg, [128, DFF], BF16) for i in range(2)]
            gsc = sbuf(stk, "gsc" + tg, [128, 2], F32)
            pmv = sbuf(stk, "pmv" + tg, [128, 4], F32)
            ps_g = [psum(stk, f"ps_g{i}" + tg, [128, 512]) for i in range(2)]
            ps_u = [psum(stk, f"ps_u{i}" + tg, [128, 512]) for i in range(2)]
            ps_t4 = psum(stk, "ps_t4" + tg, [128, 1024], BF16)
            for t in ([P.y(lambda e, j=j: e.dma_start(
                    terng_sb[:, 4096 * j:4096 * (j + 1)],
                    scr_tern[0][:, 4096 * j:4096 * (j + 1)]))
                    for j in range(8)] +
                    [P.y(lambda e, j=j: e.dma_start(
                        ternu_sb[:, 4096 * j:4096 * (j + 1)],
                        scr_tern[1][:, 4096 * j:4096 * (j + 1)]))
                     for j in range(8)]):
                P.waitd("t", t)
            xused = [0, 0]
            bank_g = [0, 0]
            bank_u = [0, 0]
            sevac4 = 0
            n8g = [None, None]
            sig_last = 0
            for nt in range(nt_count):
                q = nt % 2
                xn = xnt[q]
                if xused[q]:
                    P.wait("y", "t", xused[q])
                xtok = P.y(lambda e, xn=xn, nt=nt: e.dma_start(
                    xn[:, :], scr_xnq[nt, :, :]))
                P.waitd("t", xtok)
                P.waitd("v", xtok)
                if sig_last:
                    P.wait("v", "s", sig_last)
                P.v(lambda e, nt=nt: e.tensor_scalar(
                    gsc[:, 0:1], srec[:, nt:nt + 1], mcol[:, 0:1], None,
                    ALU.mult), reads=[srec, mcol], writes=[gsc])
                P.v(lambda e, nt=nt: e.tensor_scalar(
                    gsc[:, 1:2], srec[:, nt:nt + 1], mcol[:, 2:3], None,
                    ALU.mult), reads=[srec, mcol], writes=[gsc], inc="v")
                P.wait("s", "v", P.cnt["v"])
                for oc in range(CC):
                    bk = oc % 2
                    pg, pu = ps_g[bk], ps_u[bk]
                    if bank_g[bk]:
                        P.wait("t", "v", bank_g[bk])
                    for cc in range(CC):
                        P.t(lambda e, pg=pg, cc=cc, oc=oc, xn=xn: e.matmul(
                            pg[:, :], xn[:, 128 * cc:128 * cc + 128],
                            terng_sb[:, cc * DFF + 512 * oc:
                                     cc * DFF + 512 * oc + 512],
                            start=(cc == 0), stop=(cc == CC - 1)),
                            inc=("t" if cc == CC - 1 else None))
                    tg_done = P.cnt["t"]
                    if bank_u[bk]:
                        P.wait("t", "v", bank_u[bk])
                    for cc in range(CC):
                        P.t(lambda e, pu=pu, cc=cc, oc=oc, xn=xn: e.matmul(
                            pu[:, :], xn[:, 128 * cc:128 * cc + 128],
                            ternu_sb[:, cc * DFF + 512 * oc:
                                     cc * DFF + 512 * oc + 512],
                            start=(cc == 0), stop=(cc == CC - 1)),
                            inc=("t" if cc == CC - 1 else None))
                    tu_done = P.cnt["t"]
                    P.wait("s", "t", tg_done)
                    P.s(lambda e, pg=pg: e.activation(
                        sigt[:, :], pg[:, :], AF.Sigmoid,
                        scale=gsc[:, 0:1]), inc="s")
                    sig_last = P.cnt["s"]
                    P.wait("v", "s", P.cnt["s"])
                    P.wait("v", "t", tu_done)
                    P.v(lambda e, pg=pg: e.scalar_tensor_tensor(
                        tt1[:, :], pg[:, :], gsc[:, 0:1], sigt[:, :],
                        ALU.mult, ALU.mult), reads=[sigt, gsc],
                        writes=[tt1], inc="v")
                    bank_g[bk] = P.cnt["v"]
                    P.v(lambda e, pu=pu, oc=oc: e.scalar_tensor_tensor(
                        prod[:, 512 * oc:512 * oc + 512], pu[:, :],
                        gsc[:, 1:2], tt1[:, :], ALU.mult, ALU.mult),
                        reads=[tt1, gsc], writes=[prod], inc="v")
                    bank_u[bk] = P.cnt["v"]
                P.v(lambda e: e.tensor_reduce(
                    pmv[:, 0:1], prod[:, :], AX.X, ALU.max,
                    apply_absolute_value=True), reads=[prod], writes=[pmv])
                P.v(lambda e: e.tensor_scalar(
                    pmv[:, 1:2], pmv[:, 0:1], 1e-5, 1.0 / 127.0,
                    ALU.max, ALU.mult), reads=[pmv], writes=[pmv])
                P.v(lambda e, nt=nt: e.tensor_scalar(
                    dsc[:, nt:nt + 1], pmv[:, 1:2], mcol[:, 4:5], None,
                    ALU.mult), reads=[pmv, mcol], writes=[dsc])
                P.v(lambda e: e.reciprocal(pmv[:, 2:3], pmv[:, 1:2]),
                    reads=[pmv], writes=[pmv])
                P.v(lambda e: e.tensor_scalar(
                    prod[:, :], prod[:, :], pmv[:, 2:3], MAGIC,
                    ALU.mult, ALU.add), reads=[prod, pmv], writes=[prod])
                P.v(lambda e: e.tensor_scalar(
                    n8h[:, :], prod[:, :], MAGIC, None, ALU.subtract),
                    reads=[prod], writes=[n8h], inc="v")
                P.wait("t", "v", P.cnt["v"])
                no = n8hT[q]
                if n8g[q] is not None:
                    P.waitd("s", n8g[q])
                for grp in range(4):
                    if sevac4:
                        P.wait("t", "s", sevac4)
                    for cc in range(8):
                        P.t(lambda e, grp=grp, cc=cc: e.transpose(
                            ps_t4[:, 128 * cc:128 * cc + 128],
                            n8h[:, 1024 * grp + 128 * cc:
                                1024 * grp + 128 * cc + 128],
                            id16[:, :]),
                            inc=("t" if cc == 7 else None))
                    P.wait("s", "t", P.cnt["t"])
                    P.s(lambda e, grp=grp, no=no: e.copy(
                        no[:, 1024 * grp:1024 * (grp + 1)], ps_t4[:, :]),
                        inc="s")
                    sevac4 = P.cnt["s"]
                xused[q] = P.cnt["t"]
                P.wait("g", "s", P.cnt["s"])
                n8g[q] = P.g(lambda e, no=no, nt=nt: e.dma_start(
                    scr_n8h[nt, :, :], no[:, :]))

        def ffn_B(nt_count, scr_n8h, dsc, x_in_scr, x_out, stk, post_fn, tg=''):
            P.barrier()
            ternd_sb = sbuf(stk, "ternd_sb" + tg, [128, 32768], BF16)
            nht = [sbuf(stk, f"nht{i}" + tg, [128, DFF], BF16) for i in range(2)]
            xit = [sbuf(stk, f"xit{i}" + tg, [128, D], F32) for i in range(2)]
            xot = [sbuf(stk, f"xot{i}" + tg, [128, D], F32) for i in range(2)]
            ps_d = [psum(stk, f"ps_d{i}" + tg, [128, 512]) for i in range(2)]
            for t in [P.y(lambda e, j=j: e.dma_start(
                    ternd_sb[:, 4096 * j:4096 * (j + 1)],
                    scr_tern[2][:, 4096 * j:4096 * (j + 1)]))
                    for j in range(8)]:
                P.waitd("t", t)
            nused = [0, 0]
            xofree = [0, 0]
            bank_d = [0, 0]
            for nt in range(nt_count):
                q = nt % 2
                nh, xi, xo = nht[q], xit[q], xot[q]
                if nused[q]:
                    P.wait("y", "t", nused[q])
                tn = P.y(lambda e, nh=nh, nt=nt: e.dma_start(
                    nh[:, :], scr_n8h[nt, :, :]))
                if xofree[q]:
                    P.wait("y", "v", xofree[q])
                tx = P.y(lambda e, xi=xi, nt=nt: e.dma_start(
                    xi[:, :], x_in_scr[128 * nt:128 * nt + 128, :]))
                P.waitd("t", tn)
                for oh in range(2):
                    pd = ps_d[oh]
                    if bank_d[oh]:
                        P.wait("t", "v", bank_d[oh])
                    for ic in range(32):
                        P.t(lambda e, pd=pd, ic=ic, oh=oh, nh=nh: e.matmul(
                            pd[:, :], nh[:, 128 * ic:128 * ic + 128],
                            ternd_sb[:, ic * D + 512 * oh:
                                     ic * D + 512 * oh + 512],
                            start=(ic == 0), stop=(ic == 31)),
                            inc=("t" if ic == 31 else None))
                nused[q] = P.cnt["t"]
                P.wait("v", "t", P.cnt["t"])
                P.waitd("v", tx)
                for oh in range(2):
                    pd = ps_d[oh]
                    P.v(lambda e, pd=pd, xo=xo, xi=xi, oh=oh, nt=nt:
                        e.scalar_tensor_tensor(
                            xo[:, 512 * oh:512 * oh + 512], pd[:, :],
                            dsc[:, nt:nt + 1],
                            xi[:, 512 * oh:512 * oh + 512],
                            ALU.mult, ALU.add),
                        reads=[xi, dsc], writes=[xo], inc="v")
                    bank_d[oh] = P.cnt["v"]
                xo_v = P.cnt["v"]
                P.wait("g", "v", xo_v)
                P.g(lambda e, xo=xo, nt=nt: e.dma_start(
                    x_out[128 * nt:128 * nt + 128, :], xo[:, :]))
                P.wait("s", "v", xo_v)
                post_fn(nt, xo, q)
                xofree[q] = P.cnt["v"]

        with ExitStack() as PT:
            P.barrier()
            wbig = sbuf(PT, "wbig", [128, 32768], F32)
            tstg = [sbuf(PT, f"tstg{i}", [128, 4096], BF16) for i in range(2)]
            wtmp4 = [sbuf(PT, f"w4tmp{i}", [128, 4096], F32) for i in range(2)]
            acol = sbuf(PT, "acol", [128, 1], F32)
            mtiny = sbuf(PT, "mtiny", [1, 16], F32)
            ps_m = psum(PT, "ps_m", [128, 128])
            ps_mb = psum(PT, "ps_mb", [128, 2])
            wbig_free = 0
            for wi, (src3, nch3) in enumerate([(gT, CC), (uT, CC), (dT, 32)]):
                wd = 32768 // nch3
                if wbig_free:
                    P.wait("y", "v", wbig_free)
                toks = [P.y(lambda e, j=j, src3=src3, wd=wd: e.dma_start(
                    wbig[:, j * wd:(j + 1) * wd], src3[j, :, :]))
                    for j in range(nch3)]
                for t in toks:
                    P.waitd("v", t)
                P.v(lambda e: e.reduce_sum(acol[:, 0:1], wbig[:, :], AX.X,
                                           apply_absolute_value=True),
                    reads=[wbig], writes=[acol], inc="v")
                P.wait("t", "v", P.cnt["v"])
                P.t(lambda e: e.transpose(ps_m[0:1, 0:128], acol[:, 0:1],
                                          id32[:, :]), inc="t")
                P.wait("v", "t", P.cnt["t"])
                P.v(lambda e, wi=wi: e.reduce_sum(
                    mtiny[0:1, 4 * wi:4 * wi + 1], ps_m[0:1, 0:128], AX.X),
                    writes=[mtiny], inc="v")
                P.v(lambda e, wi=wi: e.tensor_scalar(
                    mtiny[0:1, 4 * wi + 1:4 * wi + 2],
                    mtiny[0:1, 4 * wi:4 * wi + 1],
                    1.0 / 4194304.0, 1e-5, ALU.mult, ALU.max),
                    reads=[mtiny], writes=[mtiny])
                P.v(lambda e, wi=wi: e.reciprocal(
                    mtiny[0:1, 4 * wi + 2:4 * wi + 3],
                    mtiny[0:1, 4 * wi + 1:4 * wi + 2]),
                    reads=[mtiny], writes=[mtiny], inc="v")
                P.wait("t", "v", P.cnt["v"])
                P.t(lambda e, wi=wi: e.matmul(
                    ps_mb[:, 0:2], ones32[0:1, :],
                    mtiny[0:1, 4 * wi + 1:4 * wi + 3],
                    start=True, stop=True), inc="t")
                P.wait("v", "t", P.cnt["t"])
                P.v(lambda e, wi=wi: e.tensor_copy(
                    mcol[:, 2 * wi:2 * wi + 2], ps_mb[:, 0:2]),
                    writes=[mcol], inc="v")
                stok = [None, None]
                for j in range(8):
                    b = tstg[j % 2]
                    wt = wtmp4[j % 2]
                    if stok[j % 2] is not None:
                        P.waitd("v", stok[j % 2])
                    P.v(lambda e, j=j, wt=wt, wi=wi: e.tensor_scalar(
                        wt[:, :], wbig[:, 4096 * j:4096 * (j + 1)],
                        mcol[:, 2 * wi + 1:2 * wi + 2], MAGIC,
                        ALU.mult, ALU.add), reads=[wbig, mcol], writes=[wt])
                    P.v(lambda e, wt=wt: e.tensor_scalar(
                        wt[:, :], wt[:, :], MAGIC, 1.0, ALU.subtract, ALU.min),
                        reads=[wt], writes=[wt])
                    P.v(lambda e, b=b, wt=wt: e.tensor_scalar(
                        b[:, :], wt[:, :], -1.0, 1.0, ALU.max, ALU.mult),
                        reads=[wt], writes=[b], inc="v")
                    P.wait("g", "v", P.cnt["v"])
                    stok[j % 2] = P.g(lambda e, b=b, j=j, wi=wi: e.dma_start(
                        scr_tern[wi][:, 4096 * j:4096 * (j + 1)], b[:, :]))
                wbig_free = P.cnt["v"]

        with ExitStack() as P4A:
            ffn_A(NT, scr_xnq1, scr_n8h1, srec1, dsc1, P4A)

        with ExitStack() as S5:
            bf2rep = sbuf(S5, "bf2rep", [128, D], F32)
            bf1c = sbuf(S5, "bf1c", [128, 32], F32)
            A2 = S5.enter_context(ExitStack())
            g1rep = sbuf(A2, "g1rep", [128, D], F32)
            b1rep = sbuf(A2, "b1rep", [128, D], F32)
            g2rep = sbuf(A2, "g2rep", [128, D], F32)
            b2rep = sbuf(A2, "b2rep", [128, D], F32)
            borep = sbuf(A2, "borep", [128, D], F32)
            with ExitStack() as BRP:
                P.barrier()
                brow4 = sbuf(BRP, "brow4", [1, 8 * D], F32)
                ps_bb = psum(BRP, "ps_bb", [128, 512])
                tbr = P.y(lambda e: e.dma_start(
                    brow4[0:1, :], brows[:, :].rearrange("r d -> () (r d)")))
                P.waitd("t", tbr)
                for rep, row in [(g1rep, 1), (b1rep, 2), (g2rep, 3),
                                 (b2rep, 4), (borep, 0), (bf2rep, 6)]:
                    for half in range(2):
                        P.t(lambda e, rep=rep, row=row, half=half: e.matmul(
                            ps_bb[:, :], ones32[0:1, :],
                            brow4[0:1, row * D + 512 * half:
                                  row * D + 512 * half + 512],
                            start=True, stop=True), inc="t")
                        P.wait("v", "t", P.cnt["t"])
                        P.v(lambda e, rep=rep, half=half: e.tensor_copy(
                            rep[:, 512 * half:512 * half + 512], ps_bb[:, :]),
                            writes=[rep], inc="v")
                        P.wait("t", "v", P.cnt["v"])

            with ExitStack() as P4B:
                sq5 = [sbuf(P4B, f"sq5{i}", [128, D], F32) for i in range(2)]
                yt5 = [sbuf(P4B, f"yt5{i}", [128, D], F32) for i in range(2)]
                h1t = [sbuf(P4B, f"h1t{i}", [128, D], F32) for i in range(2)]
                h1b = [sbuf(P4B, f"h1b{i}", [128, D], BF16) for i in range(2)]
                st5 = [sbuf(P4B, f"st5{i}", [128, 8], F32) for i in range(2)]
                st6 = [sbuf(P4B, f"st6{i}", [128, 8], F32) for i in range(2)]
                h1o = [sbuf(P4B, f"h1o{i}", [128, D], BF16)
                       for i in range(2)]
                ps_t5 = [psum(P4B, f"ps_t5{i}", [128, 1024], BF16)
                         for i in range(2)]
                sevac5 = [0, 0]
                yg = [None, None]
                h1g = [None, None]

                def post1(nt, xo, q):
                    sq, yt, ht, hb = sq5[q], yt5[q], h1t[q], h1b[q]
                    stt, stt2 = st5[q], st6[q]
                    ln_stats(stt, xo, sq, 1.0)
                    if yg[q] is not None:
                        P.waitd("v", yg[q])
                    P.v(lambda e, yt=yt, xo=xo, stt=stt: e.tensor_scalar(
                        yt[:, :], xo[:, :], stt[:, 2:3], stt[:, 6:7],
                        ALU.subtract, ALU.mult), reads=[xo, stt],
                        writes=[yt], inc="v")
                    P.wait("g", "v", P.cnt["v"])
                    yg[q] = P.g(lambda e, yt=yt, nt=nt: e.dma_start(
                        scr_y[128 * nt:128 * nt + 128, :], yt[:, :]))
                    ln_stats(stt2, yt, sq, 1.0)
                    P.v(lambda e, ht=ht, yt=yt, stt2=stt2: e.tensor_scalar(
                        ht[:, :], yt[:, :], stt2[:, 2:3], stt2[:, 6:7],
                        ALU.subtract, ALU.mult), reads=[yt, stt2],
                        writes=[ht])
                    P.v(lambda e, ht=ht: e.tensor_tensor(
                        ht[:, :], ht[:, :], g1rep[:, :], ALU.mult),
                        reads=[ht, g1rep], writes=[ht])
                    P.v(lambda e, hb=hb, ht=ht: e.tensor_tensor(
                        hb[:, :], ht[:, :], b1rep[:, :], ALU.add),
                        reads=[ht, b1rep], writes=[hb], inc="v")
                    pst = ps_t5[q]
                    transpose_1024(hb, pst, sevac5[q])
                    ho = h1o[q]
                    if h1g[q] is not None:
                        for tkh in h1g[q]:
                            P.waitd("s", tkh)
                    P.s(lambda e, ho=ho, pst=pst: e.copy(ho[:, :], pst[:, :]),
                        inc="s")
                    sevac5[q] = P.cnt["s"]
                    P.wait("g", "s", P.cnt["s"])
                    h1g[q] = [P.g(lambda e, ho=ho, cc=cc, nt=nt: e.dma_start(
                        scr_h1q[:, cc * R + 128 * nt:
                                cc * R + 128 * nt + 128],
                        ho[:, 128 * cc:128 * cc + 128])) for cc in range(CC)]

                ffn_B(NT, scr_n8h1, dsc1, scr_x2, scr_x3, P4B, post1)

            if upto == "x3":
                with ExitStack() as FIN:
                    finish_from_scratch(nc, P, scr_x3, out, FIN)
                return nc, cvals
            if upto == "yy1":
                with ExitStack() as FIN:
                    finish_from_scratch(nc, P, scr_y, out, FIN, off=256)
                return nc, cvals
            if upto in ("h11", "h17"):
                with ExitStack() as FIN:
                    htmp = sbuf(FIN, "htmp", [128, D], F32)
                    for sem, n_ in P.dma_done_all("g"):
                        P.wait("y", sem, n_)
                    base = 7 * R if upto == "h17" else 0
                    nj = 2 if upto == "h17" else 16
                    prev = None
                    for j in range(nj):
                        if prev is not None:
                            P.waitd("g", prev)
                        tk_ = P.g(lambda e, j=j: e.dma_start(
                            htmp[:, :],
                            scr_h1q[:, base + j * D:base + (j + 1) * D]))
                        P.waitd("y", tk_)
                        prev = P.y(lambda e, j=j: e.dma_start(
                            out[128 * j:128 * j + 128, :], htmp[:, :]))
                    for sem, n_ in P.dma_done_all("y"):
                        P.wait("y", sem, n_)
                    emit_blocks(nc, P)
                return nc, cvals

            with ExitStack() as PH5:
                P.barrier()
                zf = sbuf(PH5, "zf", [128, CC * OWN], BF16)
                with ExitStack() as A25:
                    masksb = sbuf(A25, "masksb", [128, 16 * 384], BF16)
                    qbf = sbuf(A25, "qbf", [128, CC * OWN], BF16)
                    kdf = sbuf(A25, "kdf", [128, 4 * R], BF16)
                    vtm = sbuf(A25, "vtm", [128, NT * 256], BF16)
                    mtok = [P.g(lambda e, qt=qt: e.dma_start(
                        masksb[:, 384 * qt:384 * qt + 384],
                        masks_d[qt, :, :])) for qt in range(16)]
                    with ExitStack() as RP2:
                        cosF = sbuf(RP2, "cosF", [128, R], F32)
                        sinFS = sbuf(RP2, "sinFS", [128, R], F32)
                        bqc = sbuf(RP2, "bqc", [128, 8], F32)
                        bkc = sbuf(RP2, "bkc", [128, 4], F32)
                        bvrep = sbuf(RP2, "bvrep", [128, 256], F32)
                        RP3 = RP2.enter_context(ExitStack())
                        thcol = sbuf(RP3, "thcol", [128, 2], F32)
                        posr = sbuf(RP3, "posr", [1, R], F32)
                        bvrow = sbuf(RP3, "bvrow", [1, 512], F32)
                        frq = [sbuf(RP3, f"frq{i}", [128, 384], F32)
                               for i in range(4)]
                        ps_r5 = psum(RP3, "ps_r5", [128, 384])
                        for t in [P.y(lambda e: e.dma_start(thcol[:, :],
                                                            thsg[:, :])),
                                  P.y(lambda e: e.dma_start(posr[0:1, :],
                                                            posarr[:, :])),
                                  P.y(lambda e: e.dma_start(bqc[:, :],
                                                            bqcol[:, :])),
                                  P.y(lambda e: e.dma_start(bkc[:, :],
                                                            bkdcol[:, :])),
                                  P.y(lambda e: e.dma_start(bf1c[:, :],
                                                            bf1col[:, :])),
                                  P.y(lambda e: e.dma_start(bvrow[0:1, :],
                                                            bkvrow[1:2, :]))]:
                            P.waitd("v", t)
                            P.waitd("t", t)
                            P.waitd("s", t)
                        P.t(lambda e: e.matmul(ps_r5[:, 0:256], ones32[0:1, :],
                                               bvrow[0:1, 0:256], start=True,
                                               stop=True), inc="t")
                        P.wait("v", "t", P.cnt["t"])
                        P.v(lambda e: e.tensor_copy(bvrep[:, :],
                                                    ps_r5[:, 0:256]),
                            writes=[bvrep], inc="v")
                        P.wait("t", "v", P.cnt["v"])
                        for ch in range(6):
                            P.t(lambda e, ch=ch: e.matmul(
                                ps_r5[:, :], ones32[0:1, :],
                                posr[0:1, 384 * ch:384 * ch + 384],
                                start=True, stop=True), inc="t")
                            P.wait("v", "t", P.cnt["t"])
                            f0, f1, f2, f3 = frq
                            P.v(lambda e: e.tensor_scalar(
                                f0[:, :], ps_r5[:, :], thcol[:, 0:1], None,
                                ALU.mult), reads=[thcol], writes=[f0],
                                inc="v")
                            P.wait("t", "v", P.cnt["v"])
                            P.v(lambda e: e.tensor_scalar(
                                f1[:, :], f0[:, :],
                                float(1.0 / (2 * np.pi)), MAGIC,
                                ALU.mult, ALU.add), reads=[f0], writes=[f1])
                            P.v(lambda e: e.tensor_scalar(
                                f1[:, :], f1[:, :], MAGIC, None,
                                ALU.subtract), reads=[f1], writes=[f1])
                            P.v(lambda e: e.scalar_tensor_tensor(
                                f2[:, :], f1[:, :], float(-2 * np.pi),
                                f0[:, :], ALU.mult, ALU.add),
                                reads=[f1, f0], writes=[f2], inc="v")
                            P.v(lambda e: e.tensor_scalar(
                                f3[:, :], f0[:, :],
                                float(1.0 / (2 * np.pi)), 0.25,
                                ALU.mult, ALU.add), reads=[f0], writes=[f3])
                            P.v(lambda e: e.tensor_scalar(
                                f3[:, :], f3[:, :], MAGIC, MAGIC, ALU.add,
                                ALU.subtract), reads=[f3], writes=[f3])
                            P.v(lambda e: e.scalar_tensor_tensor(
                                f3[:, :], f3[:, :], float(-2 * np.pi),
                                f0[:, :], ALU.mult, ALU.add),
                                reads=[f3, f0], writes=[f3], inc="v")
                            P.wait("s", "v", P.cnt["v"])
                            P.s(lambda e, ch=ch: e.activation(
                                sinFS[:, 384 * ch:384 * ch + 384], f2[:, :],
                                AF.Sin), inc="s")
                            P.s(lambda e, ch=ch: e.activation(
                                cosF[:, 384 * ch:384 * ch + 384], f3[:, :],
                                AF.Sin, bias=cap(PI / 2), scale=1.0),
                                inc="s")
                            P.wait("v", "s", P.cnt["s"])
                            P.v(lambda e, ch=ch: e.tensor_scalar(
                                sinFS[:, 384 * ch:384 * ch + 384],
                                sinFS[:, 384 * ch:384 * ch + 384],
                                thcol[:, 1:2], None, ALU.mult),
                                reads=[sinFS, thcol], writes=[sinFS],
                                inc="v")

                        RP3.close()
                        for eng in ("y", "g"):
                            for sem in ("v", "s", "t"):
                                P.wait(eng, sem, P.cnt[sem])
                        if upto == "h27":
                            htmp = sbuf(RP2, "htmp", [128, D], F32)
                            P.wait("g", "v", P.cnt["v"])
                            P.wait("g", "s", P.cnt["s"])
                            prev = None
                            for j in range(2):
                                if prev is not None:
                                    P.waitd("g", prev)
                                tk_ = P.g(lambda e, j=j: e.dma_start(
                                    htmp[:, :],
                                    scr_h1q[:, 7 * R + j * D:
                                            7 * R + (j + 1) * D]))
                                P.waitd("y", tk_)
                                prev = P.y(lambda e, j=j: e.dma_start(
                                    out[128 * j:128 * j + 128, :],
                                    htmp[:, :]))
                            for sem, n_ in P.dma_done_all("y"):
                                P.wait("y", sem, n_)
                            emit_blocks(nc, P)
                            return nc, cvals
                        dbgq = sbuf(RP2, "dbgq", [128, 1024], F32) \
                            if upto in ("qp1", "hw1") else None
                        if upto == "rp1":
                            P.wait("g", "s", P.cnt["s"])
                            P.wait("g", "v", P.cnt["v"])
                            for j in range(2):
                                P.g(lambda e, j=j: e.dma_start(
                                    out[128 * j:128 * j + 128, :],
                                    cosF[:, j * D:(j + 1) * D]))
                                P.g(lambda e, j=j: e.dma_start(
                                    out[256 + 128 * j:256 + 128 * j + 128, :],
                                    sinFS[:, j * D:(j + 1) * D]))
                            for sem, n_ in P.dma_done_all("g"):
                                P.wait("g", sem, n_)
                            emit_blocks(nc, P)
                            return nc, cvals
                        with ExitStack() as QKV:
                            hwin = [sbuf(QKV, f"hwin{i}", [128, CC * 512],
                                         BF16) for i in range(2)]
                            qf32 = [sbuf(QKV, f"qf32{i}", [128, 512], F32)
                                    for i in range(2)]
                            qsw = [sbuf(QKV, f"qsw{i}", [128, 512], F32)
                                   for i in range(2)]
                            rt1 = sbuf(QKV, "rt1", [128, 512], F32)
                            rt2 = sbuf(QKV, "rt2", [128, 512], F32)
                            ps_q = [psum(QKV, f"ps_q{i}", [128, 512])
                                    for i in range(2)]
                            ps_sw = [psum(QKV, f"ps_sw{i}", [128, 512])
                                     for i in range(2)]
                            hw_used = [0, 0]

                            def qk_rope(n_oc, wsb, wwid, bias_ap, dst,
                                        dst_cols, chs, src_off, esc=1.0):
                                bank5 = [0, 0]
                                sevq = [0, 0]
                                for chi, (c0, cw) in enumerate(chs):
                                    hb_ = chi % 2
                                    hw = hwin[hb_]
                                    if hw_used[hb_]:
                                        P.wait("y", "t", hw_used[hb_])
                                    htoks = [P.y(lambda e, cc=cc, c0=c0,
                                                 cw=cw, hw=hw:
                                                 e.dma_start(
                                                     hw[:, cc * cw:
                                                        (cc + 1) * cw],
                                                     scr_h1q[:, cc * R +
                                                             src_off + c0:
                                                             cc * R +
                                                             src_off + c0 +
                                                             cw]))
                                             for cc in range(CC)]
                                    for t in htoks:
                                        P.waitd("t", t)
                                    for oc in range(n_oc):
                                        bk = (chi * n_oc + oc) % 2
                                        pq, psw = ps_q[bk], ps_sw[bk]
                                        qf, qs = qf32[bk], qsw[bk]
                                        if bank5[bk]:
                                            P.wait("t", "v", bank5[bk])
                                            P.wait("t", "s", sevq[bk])
                                        for cc in range(CC):
                                            P.t(lambda e, pq=pq, cc=cc,
                                                oc=oc, cw=cw, wsb=wsb,
                                                wwid=wwid, hw=hw: e.matmul(
                                                    pq[:, 0:cw],
                                                    wsb[:, cc * wwid +
                                                        128 * oc:
                                                        cc * wwid +
                                                        128 * oc + 128],
                                                    hw[:, cc * cw:
                                                       cc * cw + cw],
                                                    start=(cc == 0),
                                                    stop=(cc == CC - 1)),
                                                inc=("t" if cc == CC - 1
                                                     else None))
                                        P.wait("s", "t", P.cnt["t"])
                                        P.s(lambda e, qf=qf, pq=pq, cw=cw,
                                            oc=oc, bias_ap=bias_ap, esc=esc:
                                            e.activation(
                                                qf[:, 0:cw], pq[:, 0:cw],
                                                AF.Identity,
                                                bias=bias_ap[:, oc:oc + 1],
                                                scale=esc), inc="s")
                                        if dbgq is not None and oc == 0 \
                                                and chi == 0 and \
                                                dst_cols == OWN:
                                            P.wait("v", "s", P.cnt["s"])
                                            if upto == "qp1":
                                                P.v(lambda e, qf=qf, cw=cw:
                                                    e.tensor_copy(
                                                        dbgq[:, 0:cw],
                                                        qf[:, 0:cw]),
                                                    writes=[dbgq], inc="v")
                                            else:
                                                P.v(lambda e, hw=hw:
                                                    e.tensor_copy(
                                                        dbgq[:, 0:512],
                                                        hw[:, 2 * 512:
                                                           2 * 512 + 512]),
                                                    writes=[dbgq])
                                                P.v(lambda e, hw=hw:
                                                    e.tensor_copy(
                                                        dbgq[:, 512:1024],
                                                        hw[:, 7 * 512:
                                                           7 * 512 + 512]),
                                                    writes=[dbgq], inc="v")
                                        sevq[bk] = P.cnt["s"]
                                        P.wait("t", "s", P.cnt["s"])
                                        P.t(lambda e, psw=psw, qf=qf, cw=cw:
                                            e.matmul(
                                                psw[:, 0:cw], perm32[:, :],
                                                qf[:, 0:cw], start=True,
                                                stop=True), inc="t")
                                        P.wait("s", "t", P.cnt["t"])
                                        P.s(lambda e, qs=qs, psw=psw, cw=cw:
                                            e.copy(qs[:, 0:cw],
                                                   psw[:, 0:cw]), inc="s")
                                        P.wait("v", "s", P.cnt["s"])
                                        P.v(lambda e, qf=qf, c0=c0, cw=cw:
                                            e.tensor_tensor(
                                                rt1[:, 0:cw], qf[:, 0:cw],
                                                cosF[:, src_off + c0:
                                                     src_off + c0 + cw],
                                                ALU.mult),
                                            reads=[qf, cosF], writes=[rt1])
                                        P.v(lambda e, qs=qs, c0=c0, cw=cw:
                                            e.tensor_tensor(
                                                rt2[:, 0:cw], qs[:, 0:cw],
                                                sinFS[:, src_off + c0:
                                                      src_off + c0 + cw],
                                                ALU.mult),
                                            reads=[qs, sinFS], writes=[rt2])
                                        P.v(lambda e, oc=oc, c0=c0, cw=cw,
                                            dst=dst, dst_cols=dst_cols:
                                            e.tensor_tensor(
                                                dst[:, oc * dst_cols + c0:
                                                    oc * dst_cols + c0 + cw],
                                                rt1[:, 0:cw], rt2[:, 0:cw],
                                                ALU.add),
                                            reads=[rt1, rt2], writes=[dst],
                                            inc="v")
                                        bank5[bk] = P.cnt["v"]
                                    hw_used[hb_] = P.cnt["t"]

                            OCH = [(512 * i, 512) for i in range(4)]
                            with ExitStack() as WQS:
                                wq_b = sbuf(WQS, "wq_b", [128, CC * D], BF16)
                                for t in [P.g(lambda e, cc=cc: e.dma_start(
                                        wq_b[:, cc * D:(cc + 1) * D],
                                        bWq[cc, :, :])) for cc in range(CC)]:
                                    P.waitd("t", t)
                                qk_rope(CC, wq_b, D, bqc, qbf, OWN, OCH, 256)
                                P.barrier()
                            with ExitStack() as WKS:
                                wkd_b = sbuf(WKS, "wkd_b", [128, CC * 512],
                                             BF16)
                                for t in [P.g(lambda e, cc=cc: e.dma_start(
                                        wkd_b[:, cc * 512:(cc + 1) * 512],
                                        bWkd[cc, :, :])) for cc in range(CC)]:
                                    P.waitd("t", t)
                                qk_rope(4, wkd_b, 512, bkc, kdf, R, QCH, 0,
                                        esc=0.125)
                                P.barrier()
                            wv_b = sbuf(QKV, "wv_b", [128, CC * 256], BF16)
                            for t in [P.g(lambda e, cc=cc: e.dma_start(
                                    wv_b[:, cc * 256:(cc + 1) * 256],
                                    bWv[cc, :, :])) for cc in range(CC)]:
                                P.waitd("t", t)
                            bank5v = [0, 0]
                            for nt in range(NT):
                                hb_ = nt % 2
                                hw = hwin[hb_]
                                if hw_used[hb_]:
                                    P.wait("y", "t", hw_used[hb_])
                                htoks = [P.y(lambda e, cc=cc, nt=nt, hw=hw:
                                             e.dma_start(
                                                 hw[:, cc * 128:
                                                    (cc + 1) * 128],
                                                 scr_h1q[:, cc * R +
                                                         128 * nt:
                                                         cc * R +
                                                         128 * nt + 128]))
                                         for cc in range(CC)]
                                for t in htoks:
                                    P.waitd("t", t)
                                bk = nt % 2
                                pq = ps_q[bk]
                                if bank5v[bk]:
                                    P.wait("t", "v", bank5v[bk])
                                for cc in range(CC):
                                    P.t(lambda e, pq=pq, cc=cc, hw=hw:
                                        e.matmul(
                                            pq[:, 0:256],
                                            hw[:, cc * 128:cc * 128 + 128],
                                            wv_b[:, cc * 256:
                                                 (cc + 1) * 256],
                                            start=(cc == 0),
                                            stop=(cc == CC - 1)),
                                        inc=("t" if cc == CC - 1 else None))
                                hw_used[hb_] = P.cnt["t"]
                                P.wait("v", "t", P.cnt["t"])
                                P.v(lambda e, pq=pq, nt=nt: e.tensor_tensor(
                                    vtm[:, 256 * nt:256 * nt + 256],
                                    pq[:, 0:256], bvrep[:, :], ALU.add),
                                    reads=[bvrep], writes=[vtm], inc="v")
                                bank5v[bk] = P.cnt["v"]

                    def dump16(t_, n=16):
                        P.wait("g", "s", P.cnt["s"])
                        P.wait("g", "v", P.cnt["v"])
                        for j in range(min(n, t_.shape[1] // D)):
                            P.g(lambda e, j=j, t_=t_: e.dma_start(
                                out[128 * j:128 * j + 128, :],
                                t_[:, j * D:(j + 1) * D]))
                        for sem, n_ in P.dma_done_all("g"):
                            P.wait("g", sem, n_)
                        emit_blocks(nc, P)

                    if upto == "qb1":
                        dump16(qbf)
                        return nc, cvals
                    if upto in ("qp1", "hw1"):
                        dump16(dbgq, 1)
                        return nc, cvals
                    if upto == "kd1":
                        dump16(kdf)
                        return nc, cvals
                    if upto == "vt1":
                        dump16(vtm)
                        return nc, cvals

                    with ExitStack() as ATT:
                        P.barrier()
                        for t in mtok:
                            P.waitd("v", t)
                        dbgs = sbuf(ATT, "dbgs", [128, 2048], F32) \
                            if upto == "sm1" else None
                        smt = sbuf(ATT, "smt", [128, 4 * 384], F32)
                        pex = sbuf(ATT, "pex", [128, 4 * 384], F32)
                        pb = sbuf(ATT, "pb", [128, 4 * 384], BF16)
                        pbt = sbuf(ATT, "pbt", [128, 12 * 128], BF16)
                        nmx = sbuf(ATT, "nmx", [128, 4], F32)
                        zz = sbuf(ATT, "zz", [128, 8], F32)
                        ps_sc = psum(ATT, "ps_sc", [128, 4 * 512])
                        ps_pt = psum(ATT, "ps_pt", [128, 12 * 128], BF16)
                        ps_z1 = psum(ATT, "ps_z1", [128, 128])
                        ps_z2 = psum(ATT, "ps_z2", [128, 128])
                        sc_free_v = 0
                        pt_free_v = 0
                        z_free_v = 0
                        for g in range(4):
                            kb = 64 * g
                            for qt in range(16):
                                if sc_free_v:
                                    P.wait("t", "v", sc_free_v)
                                for hq in range(4):
                                    h = 4 * g + hq
                                    qb = 64 * (h % 2)
                                    P.t(lambda e, h=h, hq=hq, qt=qt, qb=qb,
                                        g=g: e.matmul(
                                            ps_sc[:, 512 * hq:
                                                  512 * hq + 384],
                                            qbf[qb:qb + 64,
                                                (h // 2) * OWN + 128 * qt:
                                                (h // 2) * OWN +
                                                128 * qt + 128],
                                            kdf[qb:qb + 64,
                                                g * R + 128 * qt:
                                                g * R + 128 * qt + 384],
                                            start=True, stop=True,
                                            tile_position=(qb, 0)),
                                        inc=("t" if hq == 3 else None))
                                P.wait("v", "t", P.cnt["t"])
                                m3 = masksb[:, 384 * qt:384 * qt + 384]
                                P.v(lambda e, m3=m3: e.tensor_tensor(
                                    smt[:, :].rearrange(
                                        "p (h w) -> p h w", h=4),
                                    ps_sc[:, :].rearrange(
                                        "p (h w) -> p h w", h=4)[:, :, 0:384],
                                    m3.rearrange("p (o w) -> p o w", o=1)
                                    .broadcast_to((128, 4, 384)), ALU.add),
                                    reads=[masksb], writes=[smt], inc="v")
                                if dbgs is not None and g == 0 and qt == 0:
                                    P.v(lambda e: e.tensor_copy(
                                        dbgs[:, 0:1536], smt[:, :]),
                                        reads=[smt], writes=[dbgs], inc="v")
                                sc_free_v = P.cnt["v"]
                                P.v(lambda e: e.tensor_reduce(
                                    nmx[:, :],
                                    smt[:, :].rearrange(
                                        "p (h w) -> p h w", h=4),
                                    AX.X, ALU.max, negate=True),
                                    reads=[smt], writes=[nmx], inc="v")
                                P.wait("s", "v", P.cnt["v"])
                                for hq in range(4):
                                    P.s(lambda e, hq=hq: e.activation(
                                        pex[:, 384 * hq:384 * hq + 384],
                                        smt[:, 384 * hq:384 * hq + 384],
                                        AF.Exp, bias=nmx[:, hq:hq + 1],
                                        scale=1.0,
                                        accum_out=zz[:, hq:hq + 1]),
                                        inc=("s" if hq == 3 else None))
                                P.wait("v", "s", P.cnt["s"])
                                P.v(lambda e: e.reciprocal(zz[:, 4:8],
                                                           zz[:, 0:4]),
                                    writes=[zz], inc="v")
                                for hq in range(4):
                                    P.v(lambda e, hq=hq: e.tensor_scalar(
                                        pb[:, 384 * hq:384 * hq + 384],
                                        pex[:, 384 * hq:384 * hq + 384],
                                        zz[:, 4 + hq:5 + hq], None,
                                        ALU.mult),
                                        reads=[pex, zz], writes=[pb],
                                        inc=("v" if hq == 3 else None))
                                P.wait("t", "v", P.cnt["v"])
                                if pt_free_v:
                                    P.wait("t", "v", pt_free_v)
                                for blk in range(12):
                                    P.t(lambda e, blk=blk: e.transpose(
                                        ps_pt[:, 128 * blk:128 * blk + 128],
                                        pb[:, 128 * blk:128 * blk + 128],
                                        id16[:, :]),
                                        inc=("t" if blk == 11 else None))
                                P.wait("v", "t", P.cnt["t"])
                                P.v(lambda e: e.tensor_copy(pbt[:, :],
                                                            ps_pt[:, :]),
                                    writes=[pbt], inc="v")
                                pt_free_v = P.cnt["v"]
                                P.wait("t", "v", P.cnt["v"])
                                if z_free_v:
                                    P.wait("t", "v", z_free_v)
                                for rnd in range(2):
                                    for sub in range(2):
                                        hq = 2 * rnd + sub
                                        psz = ps_z1 if sub == 0 else ps_z2
                                        zlo = 64 * (hq % 2)
                                        for j in range(3):
                                            P.t(lambda e, hq=hq, j=j, qt=qt,
                                                kb=kb, psz=psz, zlo=zlo:
                                                e.matmul(
                                                    psz[zlo:zlo + 64, 0:128],
                                                    vtm[:, 256 * (qt + j) +
                                                        kb:
                                                        256 * (qt + j) +
                                                        kb + 64],
                                                    pbt[:, 384 * hq +
                                                        128 * j:
                                                        384 * hq +
                                                        128 * j + 128],
                                                    start=(j == 0),
                                                    stop=(j == 2),
                                                    tile_position=(0, zlo)),
                                                inc=("t" if j == 2
                                                     else None))
                                    P.wait("v", "t", P.cnt["t"])
                                    for sub in range(2):
                                        hq = 2 * rnd + sub
                                        psz = ps_z1 if sub == 0 else ps_z2
                                        zlo = 64 * (hq % 2)
                                        h = 4 * g + hq
                                        P.v(lambda e, psz=psz, zlo=zlo, h=h,
                                            qt=qt: e.tensor_copy(
                                                zf[zlo:zlo + 64,
                                                   (h // 2) * OWN +
                                                   128 * qt:
                                                   (h // 2) * OWN +
                                                   128 * qt + 128],
                                                psz[zlo:zlo + 64, 0:128]),
                                            writes=[zf],
                                            inc=("v" if sub == 1
                                                 else None))
                                    z_free_v = P.cnt["v"]
                                    if rnd == 0:
                                        P.wait("t", "v", z_free_v)
                        if upto == "sm1":
                            P.wait("g", "s", P.cnt["s"])
                            P.wait("g", "v", P.cnt["v"])
                            for j in range(2):
                                P.g(lambda e, j=j: e.dma_start(
                                    out[128 * j:128 * j + 128, :],
                                    dbgs[:, j * D:(j + 1) * D]))
                            for sem, n_ in P.dma_done_all("g"):
                                P.wait("g", sem, n_)
                            emit_blocks(nc, P)
                            return nc, cvals

                if upto == "zf1":
                    P.wait("g", "s", P.cnt["s"])
                    P.wait("g", "v", P.cnt["v"])
                    for j in range(16):
                        P.g(lambda e, j=j: e.dma_start(
                            out[128 * j:128 * j + 128, :],
                            zf[:, j * D:(j + 1) * D]))
                    for sem, n_ in P.dma_done_all("g"):
                        P.wait("g", sem, n_)
                    emit_blocks(nc, P)
                    return nc, cvals

                with ExitStack() as PRJ:
                    P.barrier()
                    wo_b = sbuf(PRJ, "wo_b", [128, CC * D], BF16)
                    yt = [sbuf(PRJ, f"yt6{i}", [128, D], F32)
                          for i in range(2)]
                    xbt = [sbuf(PRJ, f"xbt{i}", [128, D], F32)
                           for i in range(2)]
                    sq6 = [sbuf(PRJ, f"sq6{i}", [128, D], F32)
                           for i in range(2)]
                    h2t = [sbuf(PRJ, f"h2t{i}", [128, D], F32)
                           for i in range(2)]
                    h2b = [sbuf(PRJ, f"h2b{i}", [128, D], BF16)
                           for i in range(2)]
                    h2o = [sbuf(PRJ, f"h2o{i}", [128, D], BF16)
                           for i in range(2)]
                    st7 = [sbuf(PRJ, f"st7{i}", [128, 8], F32)
                           for i in range(2)]
                    ps_o5 = [psum(PRJ, f"ps_o5{i}", [128, 512])
                             for i in range(2)]
                    ps_t6 = [psum(PRJ, f"ps_t6{i}", [128, 1024], BF16)
                             for i in range(2)]
                    for t in [P.g(lambda e, cc=cc: e.dma_start(
                            wo_b[:, cc * D:(cc + 1) * D], bWo[cc, :, :]))
                            for cc in range(CC)]:
                        P.waitd("t", t)
                    bank6 = [0, 0]
                    sev6 = [0, 0]
                    h2g = [None, None]
                    xbused = [0, 0]
                    for nt in range(ONT):
                        q = nt % 2
                        ya, xb_ = yt[q], xbt[q]
                        if xbused[q]:
                            P.wait("y", "v", xbused[q])
                        tk = P.y(lambda e, ya=ya, nt=nt: e.dma_start(
                            ya[:, :], scr_y[256 + 128 * nt:
                                            256 + 128 * nt + 128, :]))
                        for oh in range(2):
                            po = ps_o5[oh]
                            if bank6[oh]:
                                P.wait("t", "v", bank6[oh])
                            for cc in range(CC):
                                P.t(lambda e, po=po, cc=cc, nt=nt, oh=oh:
                                    e.matmul(
                                        po[:, :],
                                        zf[:, cc * OWN + 128 * nt:
                                           cc * OWN + 128 * nt + 128],
                                        wo_b[:, cc * D + 512 * oh:
                                             cc * D + 512 * oh + 512],
                                        start=(cc == 0),
                                        stop=(cc == CC - 1)),
                                    inc=("t" if cc == CC - 1 else None))
                        P.wait("v", "t", P.cnt["t"])
                        P.waitd("v", tk)
                        for oh in range(2):
                            po = ps_o5[oh]
                            P.v(lambda e, po=po, xb_=xb_, oh=oh:
                                e.scalar_tensor_tensor(
                                    xb_[:, 512 * oh:512 * oh + 512],
                                    po[:, :], 1.0,
                                    borep[:, 512 * oh:512 * oh + 512],
                                    ALU.mult, ALU.add),
                                reads=[borep], writes=[xb_], inc="v")
                            bank6[oh] = P.cnt["v"]
                        P.v(lambda e, xb_=xb_, ya=ya: e.tensor_add(
                            xb_[:, :], xb_[:, :], ya[:, :]),
                            reads=[xb_, ya], writes=[xb_], inc="v")
                        P.wait("g", "v", P.cnt["v"])
                        P.g(lambda e, xb_=xb_, nt=nt: e.dma_start(
                            scr_xb[128 * nt:128 * nt + 128, :], xb_[:, :]))
                        stt = st7[q]
                        ln_stats(stt, xb_, sq6[q], 1.0)
                        P.v(lambda e, q=q, xb_=xb_, stt=stt: e.tensor_scalar(
                            h2t[q][:, :], xb_[:, :], stt[:, 2:3],
                            stt[:, 6:7], ALU.subtract, ALU.mult),
                            reads=[xb_, stt], writes=[h2t[q]])
                        P.v(lambda e, q=q: e.tensor_tensor(
                            h2t[q][:, :], h2t[q][:, :], g2rep[:, :],
                            ALU.mult),
                            reads=[h2t[q], g2rep], writes=[h2t[q]])
                        P.v(lambda e, q=q: e.tensor_tensor(
                            h2b[q][:, :], h2t[q][:, :], b2rep[:, :],
                            ALU.add),
                            reads=[h2t[q], b2rep], writes=[h2b[q]],
                            inc="v")
                        xbused[q] = P.cnt["v"]
                        pst = ps_t6[q]
                        transpose_1024(h2b[q], pst, sev6[q])
                        if h2g[q] is not None:
                            P.waitd("s", h2g[q])
                        P.s(lambda e, q=q, pst=pst: e.copy(
                            h2o[q][:, :], pst[:, :]), inc="s")
                        sev6[q] = P.cnt["s"]
                        P.wait("g", "s", P.cnt["s"])
                        h2g[q] = P.g(lambda e, q=q, nt=nt: e.dma_start(
                            scr_h2q[nt // 2, :, :]
                            .rearrange("p (c w) -> p c w", c=CC)
                            [:, :, 128 * (nt % 2):128 * (nt % 2) + 128],
                            h2o[q][:, :].rearrange(
                                "p (c w) -> p c w", c=CC)))

            A2.close()
            if upto == "xb":
                with ExitStack() as FIN:
                    finish_from_scratch(nc, P, scr_xb, out, FIN, off=0)
                return nc, cvals

            with ExitStack() as MLP:
                P.barrier()
                wf1_b = sbuf(MLP, "wf1_b", [128, CC * DFF], BF16)
                wf2_b = sbuf(MLP, "wf2_b", [128, 32 * D], BF16)
                h2qs = [sbuf(MLP, f"h2qs{i}", [128, CC * 256], BF16)
                        for i in range(2)]
                gact = sbuf(MLP, "gact", [128, 32 * 256], BF16)
                x3t7 = [sbuf(MLP, f"x3t7{i}", [128, D], F32)
                        for i in range(2)]
                xbt7 = [sbuf(MLP, f"xbt7{i}", [128, D], F32)
                        for i in range(2)]
                x4t = [sbuf(MLP, f"x4t{i}", [128, D], F32) for i in range(2)]
                sq7 = [sbuf(MLP, f"sq7{i}", [128, D], F32) for i in range(2)]
                ln7 = [sbuf(MLP, f"ln7{i}", [128, D], F32) for i in range(2)]
                n87 = [sbuf(MLP, f"n87{i}", [128, D], BF16)
                       for i in range(2)]
                xq7 = [sbuf(MLP, f"xq7{i}", [128, D], BF16)
                       for i in range(2)]
                st8 = [sbuf(MLP, f"st8{i}", [128, 8], F32) for i in range(2)]
                aq7 = [sbuf(MLP, f"aq7{i}", [128, 4], F32) for i in range(2)]
                ps_f1 = [psum(MLP, f"ps_f1{i}", [128, 256])
                         for i in range(2)]
                ps_f2 = [psum(MLP, f"ps_f2{i}", [128, 512])
                         for i in range(2)]
                ps_t7 = [psum(MLP, f"ps_t7{i}", [128, 1024], BF16)
                         for i in range(2)]
                wtk = [P.g(lambda e, cc=cc: e.dma_start(
                    wf1_b[:, cc * DFF:(cc + 1) * DFF], bWf1[cc, :, :]))
                    for cc in range(CC)]
                wtk += [P.g(lambda e, ic=ic: e.dma_start(
                    wf2_b[:, ic * D:(ic + 1) * D], bWf2[ic, :, :]))
                    for ic in range(32)]
                for t in wtk:
                    P.waitd("t", t)
                h2used = [0, 0]
                bank7 = [0, 0]
                bank8 = [0, 0]
                sev7 = [0, 0]
                xq7g = [None, None]
                gact_tfree = 0
                x4used = [0, 0]
                for nch in range(8):
                    q = nch % 2
                    hs = h2qs[q]
                    if h2used[q]:
                        P.wait("y", "t", h2used[q])
                    htok = P.y(lambda e, hs=hs, nch=nch: e.dma_start(
                        hs[:, :], scr_h2q[nch, :, :]))
                    P.waitd("t", htok)
                    for ic in range(32):
                        bk = ic % 2
                        pf = ps_f1[bk]
                        if bank7[bk]:
                            P.wait("t", "s", bank7[bk])
                        if ic == 0 and gact_tfree:
                            P.wait("s", "t", gact_tfree)
                        for cc in range(CC):
                            P.t(lambda e, pf=pf, cc=cc, ic=ic, hs=hs:
                                e.matmul(
                                    pf[:, :],
                                    wf1_b[:, cc * DFF + 128 * ic:
                                          cc * DFF + 128 * ic + 128],
                                    hs[:, cc * 256:(cc + 1) * 256],
                                    start=(cc == 0), stop=(cc == CC - 1)),
                                inc=("t" if cc == CC - 1 else None))
                        P.wait("s", "t", P.cnt["t"])
                        P.s(lambda e, pf=pf, ic=ic: e.activation(
                            gact[:, 256 * ic:256 * ic + 256], pf[:, :],
                            AF.Gelu, bias=bf1c[:, ic:ic + 1], scale=1.0),
                            inc="s")
                        bank7[bk] = P.cnt["s"]
                    h2used[q] = P.cnt["t"]
                    P.wait("t", "s", P.cnt["s"])
                    for sub in range(2):
                        nt = 2 * nch + sub
                        q2 = nt % 2
                        x3a, xba, x4a = x3t7[q2], xbt7[q2], x4t[q2]
                        if x4used[q2]:
                            P.wait("y", "v", x4used[q2])
                        t3k = P.y(lambda e, x3a=x3a, nt=nt: e.dma_start(
                            x3a[:, :], scr_x3[256 + 128 * nt:
                                              256 + 128 * nt + 128, :]))
                        tbk = P.y(lambda e, xba=xba, nt=nt: e.dma_start(
                            xba[:, :],
                            scr_xb[128 * nt:128 * nt + 128, :]))
                        for oh in range(2):
                            pf2 = ps_f2[oh]
                            if bank8[oh]:
                                P.wait("t", "v", bank8[oh])
                            for ic in range(32):
                                P.t(lambda e, pf2=pf2, ic=ic, oh=oh,
                                    sub=sub: e.matmul(
                                        pf2[:, :],
                                        gact[:, 256 * ic + 128 * sub:
                                             256 * ic + 128 * sub + 128],
                                        wf2_b[:, ic * D + 512 * oh:
                                              ic * D + 512 * oh + 512],
                                        start=(ic == 0), stop=(ic == 31)),
                                    inc=("t" if ic == 31 else None))
                        gact_tfree = P.cnt["t"]
                        P.wait("v", "t", P.cnt["t"])
                        P.waitd("v", t3k)
                        P.waitd("v", tbk)
                        for oh in range(2):
                            pf2 = ps_f2[oh]
                            P.v(lambda e, pf2=pf2, x4a=x4a, oh=oh:
                                e.scalar_tensor_tensor(
                                    x4a[:, 512 * oh:512 * oh + 512],
                                    pf2[:, :], 1.0,
                                    bf2rep[:, 512 * oh:512 * oh + 512],
                                    ALU.mult, ALU.add),
                                reads=[bf2rep], writes=[x4a], inc="v")
                            bank8[oh] = P.cnt["v"]
                        P.v(lambda e, x4a=x4a, xba=xba: e.tensor_add(
                            x4a[:, :], x4a[:, :], xba[:, :]),
                            reads=[x4a, xba], writes=[x4a])
                        P.v(lambda e, x4a=x4a, x3a=x3a: e.tensor_add(
                            x4a[:, :], x4a[:, :], x3a[:, :]),
                            reads=[x4a, x3a], writes=[x4a], inc="v")
                        P.wait("g", "v", P.cnt["v"])
                        P.g(lambda e, x4a=x4a, nt=nt: e.dma_start(
                            scr_x4[128 * nt:128 * nt + 128, :], x4a[:, :]))
                        lnq_tile(st8[q2], x4a, sq7[q2], ln7[q2], n87[q2],
                                 aq7[q2], srec2, nt)
                        x4used[q2] = P.cnt["v"]
                        pst = ps_t7[q2]
                        transpose_1024(n87[q2], pst, sev7[q2])
                        if xq7g[q2] is not None:
                            P.waitd("s", xq7g[q2])
                        P.s(lambda e, q2=q2, pst=pst: e.copy(
                            xq7[q2][:, :], pst[:, :]), inc="s")
                        sev7[q2] = P.cnt["s"]
                        P.wait("g", "s", P.cnt["s"])
                        xq7g[q2] = P.g(lambda e, q2=q2, nt=nt:
                                       e.dma_start(scr_xnq2[nt, :, :],
                                                   xq7[q2][:, :]))

            if upto == "x4":
                with ExitStack() as FIN:
                    finish_from_scratch(nc, P, scr_x4, out, FIN, off=0)
                return nc, cvals

        with ExitStack() as P6A:
            ffn_A(ONT, scr_xnq2, scr_n8h2, srec2, dsc2, P6A, tg="f2")
        with ExitStack() as P6B:
            ffn_B(ONT, scr_n8h2, dsc2, scr_x4, out, P6B,
                  lambda nt, xo, q: None, tg="f2")
        for sem, n in P.dma_done_all("g"):
            P.wait("y", sem, n)
        for sem, n in P.dma_done_all("y"):
            P.wait("y", sem, n)
        emit_blocks(nc, P)
        return nc, cvals


def finish_from_scratch(nc, P, scr, out, stk, off=256):
    cpt = stk.enter_context(nc.sbuf_tensor("cpt_fin", [128, D], F32))
    for sem, n in P.dma_done_all("g"):
        P.wait("y", sem, n)
    prev = None
    for i in range(OWN // 128):
        if prev is not None:
            P.waitd("y", prev)
        tk = P.y(lambda e, i=i: e.dma_start(
            cpt[:, :], scr[off + 128 * i:off + 128 * i + 128, :]))
        P.waitd("y", tk)
        prev = P.y(lambda e, i=i: e.dma_start(
            out[128 * i:128 * i + 128, :], cpt[:, :]))
    for sem, n in P.dma_done_all("y"):
        P.wait("y", sem, n)
    emit_blocks(nc, P)


def emit_blocks(nc, P):
    with nc.Block() as block:
        @block.vector
        def _(eng):
            for f in P.ops["v"]:
                f(eng)

        @block.scalar
        def _(eng):
            for f in P.ops["s"]:
                f(eng)

        @block.tensor
        def _(eng):
            for f in P.ops["t"]:
                f(eng)

        @block.gpsimd
        def _(eng):
            for f in P.ops["g"]:
                f(eng)

        @block.sync
        def _(eng):
            for f in P.ops["y"]:
                f(eng)


def make_host_inputs(inp, core, cvals):
    b, h = core // 2, core % 2
    x = np.asarray(inp["L0_attn_a_x"][b], dtype=np.float32)
    xrot = np.roll(x, 2048, axis=0) if h == 0 else x
    t0 = 2048 * h

    def t3(a, chunks):
        a = np.asarray(a, dtype=np.float32)
        return np.ascontiguousarray(a.reshape(chunks, 128, a.shape[1]))

    m = {}
    m["xrot"] = np.ascontiguousarray(xrot)
    m["aWqT"] = t3(np.asarray(inp["L0_attn_a_Wq"]).T, CC)
    m["aWkvT"] = t3(np.asarray(inp["L0_attn_a_Wkv"]).T, CC)
    m["aWpT"] = t3(np.asarray(inp["L0_attn_a_Wproj"]).T, CC)
    m["gT"] = t3(np.asarray(inp["L0_ffn_w_gate"]).T, CC)
    m["uT"] = t3(np.asarray(inp["L0_ffn_w_up"]).T, CC)
    m["dT"] = t3(np.asarray(inp["L0_ffn_w_down"]).T, 32)
    m["bWq"] = t3(np.asarray(inp["L0_attn_b_Wq"]), CC)
    wk = np.asarray(inp["L0_attn_b_Wk"], dtype=np.float32)
    wkd = np.concatenate([np.concatenate([wk[:, 64 * g:64 * g + 64]] * 2,
                                         axis=1) for g in range(4)], axis=1)
    m["bWkd"] = t3(wkd, CC)
    m["bWv"] = t3(np.asarray(inp["L0_attn_b_Wv"]), CC)
    m["bWo"] = t3(np.asarray(inp["L0_attn_b_Wo"]), CC)
    m["bWf1"] = t3(np.asarray(inp["L0_attn_b_Wf1"]), CC)
    m["bWf2"] = t3(np.asarray(inp["L0_attn_b_Wf2"]), 32)
    brws = np.zeros((8, D), dtype=np.float32)
    brws[0] = inp["L0_attn_b_bo"]
    brws[1] = inp["L0_attn_b_g1"]
    brws[2] = inp["L0_attn_b_b1"]
    brws[3] = inp["L0_attn_b_g2"]
    brws[4] = inp["L0_attn_b_b2"]
    brws[5] = inp["L0_attn_a_bproj"]
    brws[6] = inp["L0_attn_b_bf2"]
    m["brows"] = brws
    bk = np.asarray(inp["L0_attn_b_bk"], dtype=np.float32)
    bkd = np.concatenate([np.concatenate([bk[64 * g:64 * g + 64]] * 2)
                          for g in range(4)])
    bkv = np.zeros((2, 512), dtype=np.float32)
    bkv[0] = bkd
    bkv[1, 0:256] = inp["L0_attn_b_bv"]
    m["bkvrow"] = bkv
    m["bqcol"] = np.ascontiguousarray(
        np.asarray(inp["L0_attn_b_bq"], dtype=np.float32).reshape(8, 128).T)
    m["bkdcol"] = np.ascontiguousarray((bkd / 8.0).reshape(4, 128).T)
    m["bf1col"] = np.ascontiguousarray(
        np.asarray(inp["L0_attn_b_bf1"], dtype=np.float32).reshape(32, 128).T)
    mk = np.zeros((16, 128, 384), dtype=np.float32)
    for qt in range(16):
        p = 256 + 128 * qt + np.arange(128)[:, None]
        n = 128 * qt + np.arange(384)[None, :]
        valid = (n <= p) & (p - n < 256)
        if h == 0:
            valid &= (n >= 256)
        mk[qt] = np.where(valid, np.float32(0.0), np.float32(-1e9))
    m["masks"] = mk
    m["posarr"] = (t0 - 256 + np.arange(R, dtype=np.float32)).reshape(1, R)
    th = np.zeros((128, 2), dtype=np.float32)
    j = np.arange(128) % 32
    th[:, 0] = (1.0 / 10000.0 ** (j / 32.0)).astype(np.float32)
    th[:, 1] = np.where((np.arange(128) % 64) < 32, -1.0, 1.0)
    m["thsg"] = th
    m["ident"] = np.eye(128, dtype=np.float32)
    pm = np.zeros((128, 128), dtype=np.float32)
    for mm in range(128):
        swap = (mm + 32) % 64 + 64 * (mm // 64)
        pm[swap, mm] = 1.0
    m["perm"] = pm
    cpv = np.zeros(16, dtype=np.float32)
    cpv[:len(cvals)] = cvals
    m["cpool"] = np.tile(cpv, (128, 1))
    return m


_cache = {}


def kernel(**inputs):
    if "nc" not in _cache:
        _cache["nc"], _cache["cvals"] = build_program("full")
    nc = _cache["nc"]
    in_maps = [make_host_inputs(inputs, c, _cache["cvals"]) for c in range(8)]
    res = run_bass_kernel_spmd(nc, in_maps, core_ids=list(range(8)))
    outp = np.empty((4, 4096, 1024), dtype=np.float32)
    for c in range(8):
        b, h = c // 2, c % 2
        outp[b, 2048 * h:2048 * h + 2048, :] = res.results[c]["out"]
    return outp


# version 02 (final)
# speedup vs baseline: 1.0659x; 1.0659x over previous
import numpy as np
from contextlib import ExitStack

import concourse.bass as bass
import concourse.mybir as mybir
from concourse.bass_utils import run_bass_kernel_spmd

F32 = mybir.dt.float32
BF16 = mybir.dt.bfloat16
AF = mybir.ActivationFunctionType
ALU = mybir.AluOpType
AX = mybir.AxisListType

MAGIC = float(np.float32(1.5 * 2 ** 23))
S = 4096
R = 2304
OWN = 2048
Q0 = S - R
D = 1024
CC = 8
NT = R // 128
ONT = OWN // 128
DFF = 4096
EPS = 1e-5
QCH = [(384 * i, 384) for i in range(6)]
PI = float(np.float32(np.pi))

ENGS = ("v", "s", "t", "g", "y")
NYL = 6
NGL = 4


class Prog:
    def __init__(self):
        self.ops = {e: [] for e in ENGS}
        self.nano = None
        self.drains = {}
        self.vops = {}
        self.cnt = {}
        self.lastw = {e: [] for e in ENGS}
        self.sem_h = {}
        self.rr = {"y": 0, "g": 0}

    def wait(self, e, sem, n):
        assert n <= self.cnt[sem], f"wait {sem}>={n}, emitted {self.cnt[sem]}"
        if n > 0:
            self.ops[e].append(
                lambda eng, sem=sem, n=n: eng.wait_ge(self.sem_h[sem], n))

    def waitd(self, e, tok):
        if tok is not None:
            self.wait(e, tok[0], tok[1])

    def _op(self, e, fn, inc, reads, writes):
        if e == "v" and any(r is w for r in reads for w in self.lastw["v"]):
            if self.nano is not None:
                nano = self.nano
                self.ops["v"].append(
                    lambda eng, nano=nano: eng.tensor_copy(
                        nano[0][:, 0:1], nano[1][:, 0:1]))
            else:
                self.ops["v"].append(lambda eng: eng.drain())
            self.drains[getattr(self, "phase", "?")] = \
                self.drains.get(getattr(self, "phase", "?"), 0) + 1
        if e == "v":
            self.vops[getattr(self, "phase", "?")] = \
                self.vops.get(getattr(self, "phase", "?"), 0) + 1
        self.lastw[e] = list(writes)
        if inc is not None:
            self.cnt[inc] = self.cnt.get(inc, 0) + 1

        def run(eng, fn=fn, inc=inc):
            inst = fn(eng)
            if inc is not None:
                inst.then_inc(self.sem_h[inc], 1)
        self.ops[e].append(run)
        return self.cnt.get(inc)

    def _dma(self, e, fn):
        nl = NYL if e == "y" else NGL
        lane = f"d{e}{self.rr[e] % nl}"
        self.rr[e] += 1
        prev = self.cnt[lane]
        if prev > 0:
            self.ops[e].append(
                lambda eng, lane=lane, prev=prev: eng.wait_ge(
                    self.sem_h[lane], prev))
        self.cnt[lane] = prev + 16
        n = self.cnt[lane]

        def run(eng, fn=fn, lane=lane):
            fn(eng).then_inc(self.sem_h[lane], 16)
        self.ops[e].append(run)
        return (lane, n)

    def dma_done_all(self, e):
        nl = NYL if e == "y" else NGL
        return [(f"d{e}{i}", self.cnt[f"d{e}{i}"]) for i in range(nl)
                if self.cnt[f"d{e}{i}"] > 0]

    def v(self, fn, inc=None, reads=(), writes=()):
        return self._op("v", fn, inc, reads, writes)

    def s(self, fn, inc=None):
        return self._op("s", fn, inc, (), ())

    def t(self, fn, inc=None):
        return self._op("t", fn, inc, (), ())

    def g(self, fn):
        return self._dma("g", fn)

    def y(self, fn):
        return self._dma("y", fn)

    def barrier(self):
        gt = self.dma_done_all("g")
        yt = self.dma_done_all("y")
        for e in ("v", "s", "t", "g", "y"):
            for sem in ("v", "s", "t"):
                self.wait(e, sem, self.cnt[sem])
            for sem, n in gt + yt:
                self.wait(e, sem, n)


def build_program(upto="full"):
    nc = bass.Bass("TRN2", target_bir_lowering=False)
    P = Prog()

    xrot = nc.dram_tensor("xrot", [S, D], F32, kind="ExternalInput")
    aWqT = nc.dram_tensor("aWqT", [CC, 128, D], F32, kind="ExternalInput")
    aWkvT = nc.dram_tensor("aWkvT", [CC, 128, 2 * D], F32, kind="ExternalInput")
    aWpT = nc.dram_tensor("aWpT", [CC, 128, D], F32, kind="ExternalInput")
    gT = nc.dram_tensor("gT", [CC, 128, DFF], F32, kind="ExternalInput")
    uT = nc.dram_tensor("uT", [CC, 128, DFF], F32, kind="ExternalInput")
    dT = nc.dram_tensor("dT", [32, 128, D], F32, kind="ExternalInput")
    bWq = nc.dram_tensor("bWq", [CC, 128, D], F32, kind="ExternalInput")
    bWkd = nc.dram_tensor("bWkd", [CC, 128, 512], F32, kind="ExternalInput")
    bWv = nc.dram_tensor("bWv", [CC, 128, 256], F32, kind="ExternalInput")
    bWo = nc.dram_tensor("bWo", [CC, 128, D], F32, kind="ExternalInput")
    bWf1 = nc.dram_tensor("bWf1", [CC, 128, DFF], F32, kind="ExternalInput")
    bWf2 = nc.dram_tensor("bWf2", [32, 128, D], F32, kind="ExternalInput")
    brows = nc.dram_tensor("brows", [8, D], F32, kind="ExternalInput")
    bkvrow = nc.dram_tensor("bkvrow", [2, 512], F32, kind="ExternalInput")
    bqcol = nc.dram_tensor("bqcol", [128, 8], F32, kind="ExternalInput")
    bkdcol = nc.dram_tensor("bkdcol", [128, 4], F32, kind="ExternalInput")
    bf1col = nc.dram_tensor("bf1col", [128, 32], F32, kind="ExternalInput")
    masks_d = nc.dram_tensor("masks", [16, 128, 384], F32, kind="ExternalInput")
    posarr = nc.dram_tensor("posarr", [1, R], F32, kind="ExternalInput")
    thsg = nc.dram_tensor("thsg", [128, 2], F32, kind="ExternalInput")
    ident = nc.dram_tensor("ident", [128, 128], F32, kind="ExternalInput")
    perm_d = nc.dram_tensor("perm", [128, 128], F32, kind="ExternalInput")
    cpool_d = nc.dram_tensor("cpool", [128, 16], F32, kind="ExternalInput")

    out = nc.dram_tensor("out", [OWN, D], F32, kind="ExternalOutput")

    scr_x2 = nc.dram_tensor("scr_x2", [R, D], F32, kind="Internal")
    scr_x3 = nc.dram_tensor("scr_x3", [R, D], F32, kind="Internal")
    scr_y = nc.dram_tensor("scr_y", [R, D], F32, kind="Internal")
    scr_xb = nc.dram_tensor("scr_xb", [OWN, D], F32, kind="Internal")
    scr_x4 = nc.dram_tensor("scr_x4", [OWN, D], F32, kind="Internal")
    scr_xnq1 = nc.dram_tensor("scr_xnq1", [NT, 128, D], BF16, kind="Internal")
    scr_xnq2 = nc.dram_tensor("scr_xnq2", [ONT, 128, D], BF16, kind="Internal")
    scr_n8h1 = nc.dram_tensor("scr_n8h1", [NT, 128, DFF], BF16, kind="Internal")
    scr_n8h2 = nc.dram_tensor("scr_n8h2", [ONT, 128, DFF], BF16,
                              kind="Internal")
    scr_tern = [nc.dram_tensor(f"scr_tern{i}", [128, 32768], BF16,
                               kind="Internal") for i in range(3)]
    scr_h2q = nc.dram_tensor("scr_h2q", [8, 128, CC * 256], BF16,
                             kind="Internal")
    scr_h1q = nc.dram_tensor("scr_h1q", [128, CC * R], BF16, kind="Internal")

    cvals = []

    def ccol(v):
        v = float(np.float32(v))
        if v not in cvals:
            cvals.append(v)
            assert len(cvals) <= 16
        return cvals.index(v)

    with ExitStack() as TOP:
        def sbuf(stk, name, shape, dtype):
            return stk.enter_context(nc.sbuf_tensor(name, shape, dtype))

        def psum(stk, name, shape, dtype=F32):
            return stk.enter_context(nc.psum_tensor(name, shape, dtype))

        sems = ["v", "s", "t"] + [f"dy{i}" for i in range(NYL)] + \
            [f"dg{i}" for i in range(NGL)]
        for snm in sems:
            P.sem_h[snm] = TOP.enter_context(nc.semaphore("sem_" + snm))
            P.cnt[snm] = 0

        cp = sbuf(TOP, "cp", [128, 16], F32)
        nanob = sbuf(TOP, "nanob", [128, 2], F32)
        P.nano = (nanob, nanob)
        id32 = sbuf(TOP, "id32", [128, 128], F32)
        id16 = sbuf(TOP, "id16", [128, 128], BF16)
        ones16 = sbuf(TOP, "ones16", [128, 128], BF16)
        ones32 = sbuf(TOP, "ones32", [128, 128], F32)
        perm32 = sbuf(TOP, "perm32", [128, 128], F32)
        mcol = sbuf(TOP, "mcol", [128, 8], F32)
        srec1 = sbuf(TOP, "srec1", [128, NT], F32)
        dsc1 = sbuf(TOP, "dsc1", [128, NT], F32)
        srec2 = sbuf(TOP, "srec2", [128, ONT], F32)
        dsc2 = sbuf(TOP, "dsc2", [128, ONT], F32)

        def cap(v):
            return cp[:, ccol(v):ccol(v) + 1]

        for t in [P.y(lambda e: e.dma_start(cp[:, :], cpool_d[:, :])),
                  P.y(lambda e: e.dma_start(id32[:, :], ident[:, :])),
                  P.y(lambda e: e.dma_start(perm32[:, :], perm_d[:, :]))]:
            P.waitd("v", t)
        P.v(lambda e: e.memset(ones16[:, :], 1.0), writes=[ones16])
        P.v(lambda e: e.memset(ones32[:, :], 1.0), writes=[ones32])
        P.v(lambda e: e.tensor_copy(id16[:, :], id32[:, :]), inc="v",
            reads=[id32], writes=[id16])

        def ln_stats(stt, xt, sq, mul):
            P.wait("s", "v", P.cnt["v"])
            P.s(lambda e: e.activation(sq[:, :], xt[:, :], AF.Square,
                                       accum_out=stt[:, 1:2]), inc="s")
            P.v(lambda e: e.reduce_sum(stt[:, 0:1], xt[:, :], AX.X),
                reads=[xt], writes=[stt])
            P.wait("v", "s", P.cnt["s"])
            P.v(lambda e: e.tensor_scalar(stt[:, 2:3], stt[:, 0:1], 1.0 / D,
                                          None, ALU.mult), reads=[stt],
                writes=[stt])
            P.v(lambda e: e.tensor_scalar(stt[:, 3:4], stt[:, 1:2], 1.0 / D,
                                          None, ALU.mult), reads=[stt],
                writes=[stt])
            P.v(lambda e: e.scalar_tensor_tensor(stt[:, 4:5], stt[:, 2:3],
                                                 stt[:, 2:3], stt[:, 3:4],
                                                 ALU.mult, ALU.subtract),
                reads=[stt], writes=[stt], inc="v")
            P.wait("s", "v", P.cnt["v"])
            P.s(lambda e: e.activation(stt[:, 5:6], stt[:, 4:5], AF.Sqrt,
                                       bias=cap(EPS), scale=-1.0), inc="s")
            P.wait("v", "s", P.cnt["s"])
            P.v(lambda e: e.reciprocal(stt[:, 6:7], stt[:, 5:6]),
                reads=[stt], writes=[stt])
            P.v(lambda e: e.tensor_scalar(stt[:, 7:8], stt[:, 6:7], mul, None,
                                          ALU.mult), reads=[stt], writes=[stt])

        def quant_chunks(dst, dst_off, src, n, w, finalmul, tmp,
                         lo=-128.0, hi=127.0):
            for j in range(n):
                b = tmp[j % 2]
                P.v(lambda e, j=j, b=b: e.tensor_scalar(
                    b[:, 0:w], src[:, j * w:(j + 1) * w], 16.0, MAGIC,
                    ALU.mult, ALU.add), reads=[src], writes=[b])
                P.v(lambda e, b=b: e.tensor_scalar(
                    b[:, 0:w], b[:, 0:w], MAGIC, hi, ALU.subtract, ALU.min),
                    reads=[b], writes=[b])
                P.v(lambda e, j=j, b=b: e.tensor_scalar(
                    dst[:, dst_off + j * w:dst_off + (j + 1) * w], b[:, 0:w],
                    lo, finalmul, ALU.max, ALU.mult),
                    reads=[b], writes=[dst], inc="v")

        def lnq_tile(stt, xo, sq, lnx, n8, aq, srec, nt):
            ln_stats(stt, xo, sq, 1.0)
            P.v(lambda e: e.tensor_scalar(
                lnx[:, :], xo[:, :], stt[:, 2:3], stt[:, 6:7],
                ALU.subtract, ALU.mult), reads=[xo, stt], writes=[lnx])
            P.v(lambda e: e.tensor_reduce(
                aq[:, 0:1], lnx[:, :], AX.X, ALU.max,
                apply_absolute_value=True), reads=[lnx], writes=[aq])
            P.v(lambda e: e.tensor_scalar(
                aq[:, 1:2], aq[:, 0:1], 1e-5, 1.0 / 127.0, ALU.max, ALU.mult),
                reads=[aq], writes=[aq])
            P.v(lambda e, nt=nt: e.tensor_copy(
                srec[:, nt:nt + 1], aq[:, 1:2]), reads=[aq], writes=[srec])
            P.v(lambda e: e.reciprocal(aq[:, 2:3], aq[:, 1:2]),
                reads=[aq], writes=[aq])
            P.v(lambda e: e.tensor_scalar(
                lnx[:, :], lnx[:, :], aq[:, 2:3], MAGIC, ALU.mult, ALU.add),
                reads=[lnx, aq], writes=[lnx])
            P.v(lambda e: e.tensor_scalar(
                n8[:, :], lnx[:, :], MAGIC, None, ALU.subtract),
                reads=[lnx], writes=[n8], inc="v")

        def transpose_1024(n8, pst, sevac_prev):
            P.wait("t", "v", P.cnt["v"])
            if sevac_prev:
                P.wait("t", "s", sevac_prev)
            for cc in range(CC):
                P.t(lambda e, cc=cc: e.transpose(
                    pst[:, 128 * cc:128 * cc + 128],
                    n8[:, 128 * cc:128 * cc + 128], id16[:, :]),
                    inc=("t" if cc == CC - 1 else None))
            P.wait("s", "t", P.cnt["t"])

        P.phase = "ph1"
        with ExitStack() as SC23:
            zq = sbuf(SC23, "zq", [128, CC * R], BF16)
            with ExitStack() as SC12:
                xqf = sbuf(SC12, "xqf", [128, CC * S], BF16)
                with ExitStack() as PH1:
                    xin = [sbuf(PH1, f"xin{i}", [128, D], F32) for i in range(2)]
                    sqt = [sbuf(PH1, f"sqt{i}", [128, D], F32) for i in range(2)]
                    lnt = [sbuf(PH1, f"lnt{i}", [128, D], F32) for i in range(2)]
                    qtm = [sbuf(PH1, f"qtm{i}", [128, D], BF16) for i in range(2)]
                    stt_ = [sbuf(PH1, f"stt{i}", [128, 8], F32) for i in range(2)]
                    pstr = [psum(PH1, f"pstr{i}", [128, 1024], BF16)
                            for i in range(2)]
                    sevac = [0, 0]
                    tdone = [0, 0]
                    for i in range(32):
                        q = i % 2
                        xt, sq, ln, qt, stt = (xin[q], sqt[q], lnt[q], qtm[q],
                                               stt_[q])
                        ps = pstr[q]
                        if tdone[q]:
                            P.wait("y", "t", tdone[q])
                        tk = P.y(lambda e, xt=xt, r0=128 * i: e.dma_start(
                            xt[:, :], xrot[r0:r0 + 128, :]))
                        P.waitd("v", tk)
                        P.waitd("s", tk)
                        ln_stats(stt, xt, sq, 16.0)
                        P.v(lambda e, ln=ln, xt=xt, stt=stt: e.tensor_scalar(
                            ln[:, :], xt[:, :], stt[:, 2:3], stt[:, 7:8],
                            ALU.subtract, ALU.mult), reads=[xt, stt],
                            writes=[ln])
                        P.v(lambda e, ln=ln: e.tensor_scalar(
                            ln[:, :], ln[:, :], MAGIC, MAGIC, ALU.add,
                            ALU.subtract), reads=[ln], writes=[ln])
                        P.v(lambda e, qt=qt, ln=ln: e.tensor_scalar(
                            qt[:, :], ln[:, :], 127.0, -128.0, ALU.min,
                            ALU.max), reads=[ln], writes=[qt], inc="v")
                        transpose_1024(qt, ps, sevac[q])
                        tdone[q] = P.cnt["t"]
                        for cc in range(CC):
                            P.s(lambda e, cc=cc, i=i, ps=ps: e.copy(
                                xqf[:, cc * S + 128 * i:cc * S + 128 * i + 128],
                                ps[:, 128 * cc:128 * cc + 128]),
                                inc=("s" if cc == CC - 1 else None))
                        sevac[q] = P.cnt["s"]

                P.phase = "ph2"
                with ExitStack() as PH2:
                    qq = sbuf(PH2, "qq", [128, CC * R], BF16)
                    P.barrier()
                    with ExitStack() as QW:
                        wqf = sbuf(QW, "wqf", [128, CC * D], F32)
                        wq16f = sbuf(QW, "wq16f", [128, CC * D], BF16)
                        wtmp = [sbuf(QW, f"wtmpA{i}", [128, 512], F32)
                                for i in range(2)]
                        evq = [sbuf(QW, f"evqA{i}", [128, 512], F32)
                               for i in range(2)]
                        ps_w = [psum(QW, f"ps_wA{i}", [128, 512])
                                for i in range(2)]
                        for t in [P.y(lambda e, cc=cc: e.dma_start(
                                wqf[:, cc * D:(cc + 1) * D], aWqT[cc, :, :]))
                                for cc in range(CC)]:
                            P.waitd("v", t)
                        quant_chunks(wq16f, 0, wqf, CC * 2, 512, 1.0 / 256.0,
                                     wtmp)
                        P.wait("t", "v", P.cnt["v"])
                        bankv = [0, 0]
                        idx = 0
                        for oc in range(CC):
                            for (c0, cw) in QCH:
                                bk = idx % 2
                                idx += 1
                                pw = ps_w[bk]
                                if bankv[bk]:
                                    P.wait("t", "v", bankv[bk])
                                for cc in range(CC):
                                    P.t(lambda e, pw=pw, cc=cc, oc=oc, c0=c0,
                                        cw=cw: e.matmul(
                                            pw[:, 0:cw],
                                            wq16f[:, cc * D + 128 * oc:
                                                  cc * D + 128 * oc + 128],
                                            xqf[:, cc * S + Q0 + c0:
                                                cc * S + Q0 + c0 + cw],
                                            start=(cc == 0),
                                            stop=(cc == CC - 1)),
                                        inc=("t" if cc == CC - 1 else None))
                                P.wait("v", "t", P.cnt["t"])
                                ev = evq[bk]
                                P.v(lambda e, pw=pw, ev=ev, cw=cw:
                                    e.tensor_scalar(
                                        ev[:, 0:cw], pw[:, 0:cw], 16.0, MAGIC,
                                        ALU.mult, ALU.add), writes=[ev],
                                    inc="v")
                                bankv[bk] = P.cnt["v"]
                                P.v(lambda e, ev=ev, cw=cw: e.tensor_scalar(
                                    ev[:, 0:cw], ev[:, 0:cw], MAGIC, 127.0,
                                    ALU.subtract, ALU.min), reads=[ev],
                                    writes=[ev])
                                P.v(lambda e, ev=ev, oc=oc, c0=c0, cw=cw:
                                    e.tensor_scalar(
                                        qq[:, oc * R + c0:oc * R + c0 + cw],
                                        ev[:, 0:cw], -128.0, 1.0,
                                        ALU.max, ALU.mult),
                                    reads=[ev], writes=[qq], inc="v")

                    if upto in ("xqf", "qq1"):
                        dump = xqf if upto == "xqf" else qq
                        P.wait("g", "s", P.cnt["s"])
                        P.wait("g", "v", P.cnt["v"])
                        for j in range(16):
                            P.g(lambda e, j=j, dump=dump: e.dma_start(
                                out[128 * j:128 * j + 128, :],
                                dump[:, j * D:(j + 1) * D]))
                        for sem, n in P.dma_done_all("g"):
                            P.wait("g", sem, n)
                        emit_blocks(nc, P)
                        return nc, cvals

                    P.barrier()
                    ebuf = sbuf(PH2, "ebuf", [128, 32 * 384], BF16)
                    kpair = sbuf(PH2, "kpair", [128, S], BF16)
                    vpair = sbuf(PH2, "vpair", [128, 32 * 128], BF16)
                    wldf = sbuf(PH2, "wldf", [128, CC * 128], F32)
                    wp16 = sbuf(PH2, "wp16", [128, CC * 128], BF16)
                    wtmp2 = [sbuf(PH2, f"wtmpB{i}", [128, 512], F32)
                             for i in range(2)]
                    evp = [sbuf(PH2, f"evpB{i}", [128, 512], F32)
                           for i in range(2)]
                    q8c = [sbuf(PH2, f"q8c{i}", [128, 384], F32)
                           for i in range(2)]
                    zrf = sbuf(PH2, "zrf", [128, 384], F32)
                    zr16 = sbuf(PH2, "zr16", [128, 384], BF16)
                    p16f = [sbuf(PH2, f"p16f{i}", [128, 384], BF16)
                            for i in range(2)]
                    p16 = [sbuf(PH2, f"p16{i}", [128, 384], BF16)
                           for i in range(2)]
                    ps_s = [psum(PH2, f"ps_s{i}", [128, 384]) for i in range(2)]
                    ps_z = psum(PH2, "ps_z", [128, 384])
                    ps_avA = psum(PH2, "ps_avA", [128, 384])
                    ps_avB = psum(PH2, "ps_avB", [128, 384])
                    ps_w2 = [psum(PH2, f"ps_wB{i}", [128, 512])
                             for i in range(2)]

                    sid = [0, 0]
                    zrv_last = 0
                    p16v_last = 0
                    av_free = {0: 0, 64: 0}
                    bankv2 = [0, 0]
                    wldf_free_v = 0

                    def kv_cols(col0, store, finalmul, wid):
                        nonlocal wldf_free_v, bankv2
                        toks = []
                        if wldf_free_v:
                            P.wait("y", "v", wldf_free_v)
                        for cc in range(CC):
                            toks.append(P.y(lambda e, cc=cc: e.dma_start(
                                wldf[:, cc * 128:cc * 128 + 128],
                                aWkvT[cc, :, col0:col0 + 128])))
                        for t in toks:
                            P.waitd("v", t)
                        quant_chunks(wp16, 0, wldf, CC, 128, 1.0 / 256.0,
                                     wtmp2)
                        wldf_free_v = P.cnt["v"]
                        P.wait("t", "v", P.cnt["v"])
                        if wid == 512:
                            for nch in range(8):
                                bk = nch % 2
                                pw = ps_w2[bk]
                                if bankv2[bk]:
                                    P.wait("t", "v", bankv2[bk])
                                for cc in range(CC):
                                    P.t(lambda e, pw=pw, cc=cc, nch=nch:
                                        e.matmul(
                                            pw[:, :],
                                            wp16[:, 128 * cc:128 * cc + 128],
                                            xqf[:, cc * S + 512 * nch:
                                                cc * S + 512 * nch + 512],
                                            start=(cc == 0),
                                            stop=(cc == CC - 1)),
                                        inc=("t" if cc == CC - 1 else None))
                                P.wait("v", "t", P.cnt["t"])
                                ev = evp[bk]
                                P.v(lambda e, pw=pw, ev=ev: e.tensor_scalar(
                                    ev[:, :], pw[:, :], 16.0, MAGIC, ALU.mult,
                                    ALU.add), writes=[ev], inc="v")
                                bankv2[bk] = P.cnt["v"]
                                P.v(lambda e, ev=ev: e.tensor_scalar(
                                    ev[:, :], ev[:, :], MAGIC, 127.0,
                                    ALU.subtract, ALU.min), reads=[ev],
                                    writes=[ev])
                                P.v(lambda e, ev=ev, nch=nch, store=store,
                                    finalmul=finalmul: e.tensor_scalar(
                                        store[:, 512 * nch:512 * nch + 512],
                                        ev[:, :], -128.0, finalmul,
                                        ALU.max, ALU.mult),
                                    reads=[ev], writes=[store], inc="v")
                        else:
                            for ntl in range(32):
                                bk = ntl % 2
                                pw = ps_w2[bk]
                                if bankv2[bk]:
                                    P.wait("t", "v", bankv2[bk])
                                for cc in range(CC):
                                    P.t(lambda e, pw=pw, cc=cc, ntl=ntl:
                                        e.matmul(
                                            pw[:, 0:128],
                                            xqf[:, cc * S + 128 * ntl:
                                                cc * S + 128 * ntl + 128],
                                            wp16[:, 128 * cc:128 * cc + 128],
                                            start=(cc == 0),
                                            stop=(cc == CC - 1)),
                                        inc=("t" if cc == CC - 1 else None))
                                P.wait("v", "t", P.cnt["t"])
                                ev = evp[bk]
                                P.v(lambda e, pw=pw, ev=ev: e.tensor_scalar(
                                    ev[:, 0:128], pw[:, 0:128], 16.0, MAGIC,
                                    ALU.mult, ALU.add), writes=[ev], inc="v")
                                bankv2[bk] = P.cnt["v"]
                                P.v(lambda e, ev=ev: e.tensor_scalar(
                                    ev[:, 0:128], ev[:, 0:128], MAGIC, 127.0,
                                    ALU.subtract, ALU.min), reads=[ev],
                                    writes=[ev])
                                P.v(lambda e, ev=ev, ntl=ntl, store=store,
                                    finalmul=finalmul: e.tensor_scalar(
                                        store[:, 128 * ntl:128 * ntl + 128],
                                        ev[:, 0:128], -128.0, finalmul,
                                        ALU.max, ALU.mult),
                                    reads=[ev], writes=[store], inc="v")

                    for hp in range(CC):
                        kv_cols(128 * hp, kpair, 1.0 / 128.0, 512)
                        if upto == "kp0" and hp == 0:
                            P.wait("g", "s", P.cnt["s"])
                            P.wait("g", "v", P.cnt["v"])
                            for j in range(4):
                                P.g(lambda e, j=j: e.dma_start(
                                    out[128 * j:128 * j + 128, :],
                                    kpair[:, j * D:(j + 1) * D]))
                            for sem, n in P.dma_done_all("g"):
                                P.wait("g", sem, n)
                            emit_blocks(nc, P)
                            return nc, cvals
                        kv_cols(D + 128 * hp, vpair, 1.0 / 16.0, 128)
                        if upto == "kpv" and hp == 0:
                            P.wait("g", "s", P.cnt["s"])
                            P.wait("g", "v", P.cnt["v"])
                            for j in range(4):
                                P.g(lambda e, j=j: e.dma_start(
                                    out[128 * j:128 * j + 128, :],
                                    kpair[:, j * D:(j + 1) * D]))
                            for sem, n in P.dma_done_all("g"):
                                P.wait("g", sem, n)
                            emit_blocks(nc, P)
                            return nc, cvals
                        kv_done_v = P.cnt["v"]
                        for hh in range(2):
                            base = 64 * hh
                            ps_av = ps_avA if hh == 0 else ps_avB
                            avlo, avhi = base, base + 64
                            for (c0, cw) in QCH:
                                P.wait("t", "v", kv_done_v)
                                if p16v_last:
                                    P.wait("s", "v", p16v_last)
                                for kt in range(32):
                                    pss = ps_s[kt % 2]
                                    if sid[kt % 2]:
                                        P.wait("t", "s", sid[kt % 2])
                                    P.t(lambda e, pss=pss, kt=kt, base=base,
                                        hp=hp, c0=c0, cw=cw: e.matmul(
                                            pss[:, 0:cw],
                                            kpair[base:base + 64,
                                                  128 * kt:128 * kt + 128],
                                            qq[base:base + 64,
                                               hp * R + c0:hp * R + c0 + cw],
                                            start=True, stop=True,
                                            tile_position=(base, 0)), inc="t")
                                    P.wait("s", "t", P.cnt["t"])
                                    qc = q8c[kt % 2]
                                    P.s(lambda e, qc=qc, pss=pss, cw=cw:
                                        e.activation(
                                            qc[:, 0:cw], pss[:, 0:cw],
                                            AF.Identity, bias=cap(MAGIC),
                                            scale=1.0), inc="s")
                                    sid[kt % 2] = P.cnt["s"]
                                    P.s(lambda e, qc=qc, kt=kt, cw=cw:
                                        e.activation(
                                            ebuf[:, 384 * kt:384 * kt + cw],
                                            qc[:, 0:cw], AF.Exp,
                                            bias=cap(-(8.0 + MAGIC / 16.0)),
                                            scale=1.0 / 16.0), inc="s")
                                exp_done = P.cnt["s"]
                                P.wait("t", "s", exp_done)
                                if zrv_last:
                                    P.wait("t", "v", zrv_last)
                                for kt in range(32):
                                    P.t(lambda e, kt=kt, cw=cw: e.matmul(
                                        ps_z[:, 0:cw], ones16[:, :],
                                        ebuf[:, 384 * kt:384 * kt + cw],
                                        start=(kt == 0), stop=(kt == 31)),
                                        inc=("t" if kt == 31 else None))
                                P.wait("v", "t", P.cnt["t"])
                                P.v(lambda e, cw=cw: e.reciprocal(
                                    zrf[:, 0:cw], ps_z[:, 0:cw]),
                                    writes=[zrf], inc="v")
                                zrv_last = P.cnt["v"]
                                P.v(lambda e, cw=cw: e.tensor_scalar(
                                    zr16[:, 0:cw], zrf[:, 0:cw], 16.0, None,
                                    ALU.mult), reads=[zrf], writes=[zr16],
                                    inc="v")
                                if av_free[base]:
                                    P.wait("t", "v", av_free[base])
                                for kt in range(32):
                                    pf, pq = p16f[kt % 2], p16[kt % 2]
                                    P.v(lambda e, pf=pf, kt=kt, cw=cw:
                                        e.tensor_tensor(
                                            pf[:, 0:cw],
                                            ebuf[:, 384 * kt:384 * kt + cw],
                                            zr16[:, 0:cw], ALU.mult),
                                        reads=[ebuf, zr16], writes=[pf])
                                    P.v(lambda e, pf=pf, pq=pq, cw=cw:
                                        e.tensor_scalar(
                                            pq[:, 0:cw], pf[:, 0:cw], MAGIC,
                                            MAGIC, ALU.add, ALU.subtract),
                                        reads=[pf], writes=[pq], inc="v")
                                    P.wait("t", "v", P.cnt["v"])
                                    P.t(lambda e, pq=pq, kt=kt, base=base,
                                        cw=cw, ps_av=ps_av, avlo=avlo,
                                        avhi=avhi: e.matmul(
                                            ps_av[avlo:avhi, 0:cw],
                                            vpair[:, 128 * kt + base:
                                                  128 * kt + base + 64],
                                            pq[:, 0:cw], start=(kt == 0),
                                            stop=(kt == 31),
                                            tile_position=(0, avlo)),
                                        inc=("t" if kt == 31 else None))
                                p16v_last = P.cnt["v"]
                                P.wait("v", "t", P.cnt["t"])
                                ev = evp[0]
                                P.v(lambda e, ps_av=ps_av, ev=ev, avlo=avlo,
                                    avhi=avhi, cw=cw: e.tensor_scalar(
                                        ev[avlo:avhi, 0:cw],
                                        ps_av[avlo:avhi, 0:cw], MAGIC, MAGIC,
                                        ALU.add, ALU.subtract),
                                    writes=[ev], inc="v")
                                av_free[base] = P.cnt["v"]
                                P.v(lambda e, ev=ev, hp=hp, c0=c0, cw=cw,
                                    avlo=avlo, avhi=avhi: e.tensor_scalar(
                                        zq[avlo:avhi,
                                           hp * R + c0:hp * R + c0 + cw],
                                        ev[avlo:avhi, 0:cw], 127.0, -128.0,
                                        ALU.min, ALU.max),
                                    reads=[ev], writes=[zq], inc="v")
                    if upto == "kpa" and False:
                        P.wait("g", "s", P.cnt["s"])
                        P.wait("g", "v", P.cnt["v"])
                        for j in range(4):
                            P.g(lambda e, j=j: e.dma_start(
                                out[128 * j:128 * j + 128, :],
                                kpair[:, j * D:(j + 1) * D]))
                        for sem, n in P.dma_done_all("g"):
                            P.wait("g", sem, n)
                        emit_blocks(nc, P)
                        return nc, cvals
                    if upto in ("zq1", "kp1", "eb1"):
                        dump = {"zq1": zq, "kp1": kpair, "eb1": ebuf}[upto]
                        P.wait("g", "s", P.cnt["s"])
                        P.wait("g", "v", P.cnt["v"])
                        for j in range(min(16, dump.shape[1] // D)):
                            P.g(lambda e, j=j, dump=dump: e.dma_start(
                                out[128 * j:128 * j + 128, :],
                                dump[:, j * D:(j + 1) * D]))
                        for sem, n in P.dma_done_all("g"):
                            P.wait("g", sem, n)
                        emit_blocks(nc, P)
                        return nc, cvals

            P.phase = "ph3"
            with ExitStack() as PH3:
                P.barrier()
                wpf = sbuf(PH3, "wpf", [128, CC * D], F32)
                wp16p = sbuf(PH3, "wp16p", [128, CC * D], BF16)
                wtmp3 = [sbuf(PH3, f"w3tmp{i}", [128, 512], F32)
                         for i in range(2)]
                bprep = sbuf(PH3, "bprep", [128, D], F32)
                brow = sbuf(PH3, "brow", [1, 8 * D], F32)
                x1t = [sbuf(PH3, f"x1t{i}", [128, D], F32) for i in range(2)]
                x2t = [sbuf(PH3, f"x2t{i}", [128, D], F32) for i in range(2)]
                sqt3 = [sbuf(PH3, f"sqt3{i}", [128, D], F32) for i in range(2)]
                lnt3 = [sbuf(PH3, f"lnt3{i}", [128, D], F32) for i in range(2)]
                n8t = [sbuf(PH3, f"n8t{i}", [128, D], BF16) for i in range(2)]
                xnqt = [sbuf(PH3, f"xnqt{i}", [128, D], BF16) for i in range(2)]
                stt3 = [sbuf(PH3, f"stt3{i}", [128, 8], F32) for i in range(2)]
                aqs = [sbuf(PH3, f"aqs{i}", [128, 4], F32) for i in range(2)]
                ps_o = [psum(PH3, f"ps_o{i}", [128, 512]) for i in range(2)]
                ps_b3 = psum(PH3, "ps_b3", [128, 1024])
                ps_t3 = [psum(PH3, f"ps_t3{i}", [128, 1024], BF16)
                         for i in range(2)]

                toks = [P.y(lambda e, cc=cc: e.dma_start(
                    wpf[:, cc * D:(cc + 1) * D], aWpT[cc, :, :]))
                    for cc in range(CC)]
                tbr = P.y(lambda e: e.dma_start(
                    brow[0:1, :], brows[:, :].rearrange("r d -> () (r d)")))
                for t in toks:
                    P.waitd("v", t)
                quant_chunks(wp16p, 0, wpf, CC * 2, 512, 1.0 / 256.0, wtmp3)
                P.waitd("t", tbr)
                P.wait("t", "v", P.cnt["v"])
                for half in range(2):
                    P.t(lambda e, half=half: e.matmul(
                        ps_b3[:, 512 * half:512 * half + 512], ones32[0:1, :],
                        brow[0:1, 5 * D + 512 * half:5 * D + 512 * half + 512],
                        start=True, stop=True),
                        inc=("t" if half else None))
                P.wait("v", "t", P.cnt["t"])
                P.v(lambda e: e.tensor_copy(bprep[:, :], ps_b3[:, :]),
                    writes=[bprep], inc="v")

                x2done = [0, 0]
                x1_used = [0, 0]
                sevac3 = [0, 0]
                x2g = [None, None]
                xqg = [None, None]
                for nt in range(NT):
                    q = nt % 2
                    xa, xb_ = x1t[q], x2t[q]
                    sq3, ln3, n8, xq3, stt, aq = (sqt3[q], lnt3[q], n8t[q],
                                                  xnqt[q], stt3[q], aqs[q])
                    if x1_used[q]:
                        P.wait("y", "v", x1_used[q])
                    tk1 = P.y(lambda e, xa=xa, nt=nt: e.dma_start(
                        xa[:, :], xrot[Q0 + 128 * nt:Q0 + 128 * nt + 128, :]))
                    for oh in range(2):
                        po = ps_o[oh]
                        if x2done[oh]:
                            P.wait("t", "v", x2done[oh])
                        for cc in range(CC):
                            P.t(lambda e, po=po, cc=cc, nt=nt, oh=oh: e.matmul(
                                po[:, :],
                                zq[:, cc * R + 128 * nt:cc * R + 128 * nt + 128],
                                wp16p[:, cc * D + 512 * oh:
                                      cc * D + 512 * oh + 512],
                                start=(cc == 0), stop=(cc == CC - 1)),
                                inc=("t" if cc == CC - 1 else None))
                    P.wait("v", "t", P.cnt["t"])
                    P.waitd("v", tk1)
                    if x2g[q] is not None:
                        P.waitd("v", x2g[q])
                    for oh in range(2):
                        po = ps_o[oh]
                        P.v(lambda e, po=po, xb_=xb_, oh=oh:
                            e.scalar_tensor_tensor(
                                xb_[:, 512 * oh:512 * oh + 512], po[:, :], 1.0,
                                bprep[:, 512 * oh:512 * oh + 512],
                                ALU.mult, ALU.add),
                            reads=[bprep], writes=[xb_], inc="v")
                        x2done[oh] = P.cnt["v"]
                    P.v(lambda e, xb_=xb_, xa=xa: e.tensor_add(
                        xb_[:, :], xb_[:, :], xa[:, :]),
                        reads=[xb_, xa], writes=[xb_], inc="v")
                    x1_used[q] = P.cnt["v"]
                    x2_v = P.cnt["v"]
                    P.wait("g", "v", x2_v)
                    x2g[q] = P.g(lambda e, xb_=xb_, nt=nt: e.dma_start(
                        scr_x2[128 * nt:128 * nt + 128, :], xb_[:, :]))
                    lnq_tile(stt, xb_, sq3, ln3, n8, aq, srec1, nt)
                    ps3 = ps_t3[q]
                    transpose_1024(n8, ps3, sevac3[q])
                    if xqg[q] is not None:
                        P.waitd("s", xqg[q])
                    P.s(lambda e, xq3=xq3, ps3=ps3: e.copy(xq3[:, :],
                                                           ps3[:, :]),
                        inc="s")
                    sevac3[q] = P.cnt["s"]
                    P.wait("g", "s", P.cnt["s"])
                    xqg[q] = P.g(lambda e, xq3=xq3, nt=nt: e.dma_start(
                        scr_xnq1[nt, :, :], xq3[:, :]))

                if upto == "x2":
                    finish_from_scratch(nc, P, scr_x2, out, PH3)
                    return nc, cvals

        def ffn_A(nt_count, scr_xnq, scr_n8h, srec, dsc, stk, tg=''):
            P.barrier()
            terng_sb = sbuf(stk, "terng_sb" + tg, [128, 32768], BF16)
            ternu_sb = sbuf(stk, "ternu_sb" + tg, [128, 32768], BF16)
            xnt = [sbuf(stk, f"xnt{i}" + tg, [128, D], BF16) for i in range(2)]
            sigt = sbuf(stk, "sigt" + tg, [128, 512], F32)
            tt1 = sbuf(stk, "tt1" + tg, [128, 512], F32)
            prod = sbuf(stk, "prod" + tg, [128, DFF], F32)
            n8h = sbuf(stk, "n8h" + tg, [128, DFF], BF16)
            n8hT = [sbuf(stk, f"n8hT{i}" + tg, [128, DFF], BF16) for i in range(2)]
            gsc = sbuf(stk, "gsc" + tg, [128, 2], F32)
            pmv = sbuf(stk, "pmv" + tg, [128, 4], F32)
            ps_g = [psum(stk, f"ps_g{i}" + tg, [128, 512]) for i in range(2)]
            ps_u = [psum(stk, f"ps_u{i}" + tg, [128, 512]) for i in range(2)]
            ps_t4 = psum(stk, "ps_t4" + tg, [128, 1024], BF16)
            for t in ([P.y(lambda e, j=j: e.dma_start(
                    terng_sb[:, 4096 * j:4096 * (j + 1)],
                    scr_tern[0][:, 4096 * j:4096 * (j + 1)]))
                    for j in range(8)] +
                    [P.y(lambda e, j=j: e.dma_start(
                        ternu_sb[:, 4096 * j:4096 * (j + 1)],
                        scr_tern[1][:, 4096 * j:4096 * (j + 1)]))
                     for j in range(8)]):
                P.waitd("t", t)
            xused = [0, 0]
            bank_g = [0, 0]
            bank_u = [0, 0]
            sevac4 = 0
            n8g = [None, None]
            sig_last = 0
            for nt in range(nt_count):
                q = nt % 2
                xn = xnt[q]
                if xused[q]:
                    P.wait("y", "t", xused[q])
                xtok = P.y(lambda e, xn=xn, nt=nt: e.dma_start(
                    xn[:, :], scr_xnq[nt, :, :]))
                P.waitd("t", xtok)
                P.waitd("v", xtok)
                if sig_last:
                    P.wait("v", "s", sig_last)
                P.v(lambda e, nt=nt: e.tensor_scalar(
                    gsc[:, 0:1], srec[:, nt:nt + 1], mcol[:, 0:1], None,
                    ALU.mult), reads=[srec, mcol], writes=[gsc])
                P.v(lambda e, nt=nt: e.tensor_scalar(
                    gsc[:, 1:2], srec[:, nt:nt + 1], mcol[:, 2:3], None,
                    ALU.mult), reads=[srec, mcol], writes=[gsc], inc="v")
                P.wait("s", "v", P.cnt["v"])
                for oc in range(CC):
                    bk = oc % 2
                    pg, pu = ps_g[bk], ps_u[bk]
                    if bank_g[bk]:
                        P.wait("t", "v", bank_g[bk])
                    for cc in range(CC):
                        P.t(lambda e, pg=pg, cc=cc, oc=oc, xn=xn: e.matmul(
                            pg[:, :], xn[:, 128 * cc:128 * cc + 128],
                            terng_sb[:, cc * DFF + 512 * oc:
                                     cc * DFF + 512 * oc + 512],
                            start=(cc == 0), stop=(cc == CC - 1)),
                            inc=("t" if cc == CC - 1 else None))
                    tg_done = P.cnt["t"]
                    if bank_u[bk]:
                        P.wait("t", "v", bank_u[bk])
                    for cc in range(CC):
                        P.t(lambda e, pu=pu, cc=cc, oc=oc, xn=xn: e.matmul(
                            pu[:, :], xn[:, 128 * cc:128 * cc + 128],
                            ternu_sb[:, cc * DFF + 512 * oc:
                                     cc * DFF + 512 * oc + 512],
                            start=(cc == 0), stop=(cc == CC - 1)),
                            inc=("t" if cc == CC - 1 else None))
                    tu_done = P.cnt["t"]
                    P.wait("s", "t", tg_done)
                    P.s(lambda e, pg=pg: e.activation(
                        sigt[:, :], pg[:, :], AF.Sigmoid,
                        scale=gsc[:, 0:1]), inc="s")
                    sig_last = P.cnt["s"]
                    P.wait("v", "s", P.cnt["s"])
                    P.wait("v", "t", tu_done)
                    P.v(lambda e, pg=pg: e.scalar_tensor_tensor(
                        tt1[:, :], pg[:, :], gsc[:, 0:1], sigt[:, :],
                        ALU.mult, ALU.mult), reads=[sigt, gsc],
                        writes=[tt1], inc="v")
                    bank_g[bk] = P.cnt["v"]
                    P.v(lambda e, pu=pu, oc=oc: e.scalar_tensor_tensor(
                        prod[:, 512 * oc:512 * oc + 512], pu[:, :],
                        gsc[:, 1:2], tt1[:, :], ALU.mult, ALU.mult),
                        reads=[tt1, gsc], writes=[prod], inc="v")
                    bank_u[bk] = P.cnt["v"]
                P.v(lambda e: e.tensor_reduce(
                    pmv[:, 0:1], prod[:, :], AX.X, ALU.max,
                    apply_absolute_value=True), reads=[prod], writes=[pmv])
                P.v(lambda e: e.tensor_scalar(
                    pmv[:, 1:2], pmv[:, 0:1], 1e-5, 1.0 / 127.0,
                    ALU.max, ALU.mult), reads=[pmv], writes=[pmv])
                P.v(lambda e, nt=nt: e.tensor_scalar(
                    dsc[:, nt:nt + 1], pmv[:, 1:2], mcol[:, 4:5], None,
                    ALU.mult), reads=[pmv, mcol], writes=[dsc])
                P.v(lambda e: e.reciprocal(pmv[:, 2:3], pmv[:, 1:2]),
                    reads=[pmv], writes=[pmv])
                P.v(lambda e: e.tensor_scalar(
                    prod[:, :], prod[:, :], pmv[:, 2:3], MAGIC,
                    ALU.mult, ALU.add), reads=[prod, pmv], writes=[prod])
                P.v(lambda e: e.tensor_scalar(
                    n8h[:, :], prod[:, :], MAGIC, None, ALU.subtract),
                    reads=[prod], writes=[n8h], inc="v")
                P.wait("t", "v", P.cnt["v"])
                no = n8hT[q]
                if n8g[q] is not None:
                    P.waitd("s", n8g[q])
                for grp in range(4):
                    if sevac4:
                        P.wait("t", "s", sevac4)
                    for cc in range(8):
                        P.t(lambda e, grp=grp, cc=cc: e.transpose(
                            ps_t4[:, 128 * cc:128 * cc + 128],
                            n8h[:, 1024 * grp + 128 * cc:
                                1024 * grp + 128 * cc + 128],
                            id16[:, :]),
                            inc=("t" if cc == 7 else None))
                    P.wait("s", "t", P.cnt["t"])
                    P.s(lambda e, grp=grp, no=no: e.copy(
                        no[:, 1024 * grp:1024 * (grp + 1)], ps_t4[:, :]),
                        inc="s")
                    sevac4 = P.cnt["s"]
                xused[q] = P.cnt["t"]
                P.wait("g", "s", P.cnt["s"])
                n8g[q] = P.g(lambda e, no=no, nt=nt: e.dma_start(
                    scr_n8h[nt, :, :], no[:, :]))

        def ffn_B(nt_count, scr_n8h, dsc, x_in_scr, x_out, stk, post_fn, tg=''):
            P.barrier()
            ternd_sb = sbuf(stk, "ternd_sb" + tg, [128, 32768], BF16)
            nht = [sbuf(stk, f"nht{i}" + tg, [128, DFF], BF16) for i in range(2)]
            xit = [sbuf(stk, f"xit{i}" + tg, [128, D], F32) for i in range(2)]
            xot = [sbuf(stk, f"xot{i}" + tg, [128, D], F32) for i in range(2)]
            ps_d = [psum(stk, f"ps_d{i}" + tg, [128, 512]) for i in range(2)]
            for t in [P.y(lambda e, j=j: e.dma_start(
                    ternd_sb[:, 4096 * j:4096 * (j + 1)],
                    scr_tern[2][:, 4096 * j:4096 * (j + 1)]))
                    for j in range(8)]:
                P.waitd("t", t)
            nused = [0, 0]
            xofree = [0, 0]
            bank_d = [0, 0]
            for nt in range(nt_count):
                q = nt % 2
                nh, xi, xo = nht[q], xit[q], xot[q]
                if nused[q]:
                    P.wait("y", "t", nused[q])
                tn = P.y(lambda e, nh=nh, nt=nt: e.dma_start(
                    nh[:, :], scr_n8h[nt, :, :]))
                if xofree[q]:
                    P.wait("y", "v", xofree[q])
                tx = P.y(lambda e, xi=xi, nt=nt: e.dma_start(
                    xi[:, :], x_in_scr[128 * nt:128 * nt + 128, :]))
                P.waitd("t", tn)
                for oh in range(2):
                    pd = ps_d[oh]
                    if bank_d[oh]:
                        P.wait("t", "v", bank_d[oh])
                    for ic in range(32):
                        P.t(lambda e, pd=pd, ic=ic, oh=oh, nh=nh: e.matmul(
                            pd[:, :], nh[:, 128 * ic:128 * ic + 128],
                            ternd_sb[:, ic * D + 512 * oh:
                                     ic * D + 512 * oh + 512],
                            start=(ic == 0), stop=(ic == 31)),
                            inc=("t" if ic == 31 else None))
                nused[q] = P.cnt["t"]
                P.wait("v", "t", P.cnt["t"])
                P.waitd("v", tx)
                for oh in range(2):
                    pd = ps_d[oh]
                    P.v(lambda e, pd=pd, xo=xo, xi=xi, oh=oh, nt=nt:
                        e.scalar_tensor_tensor(
                            xo[:, 512 * oh:512 * oh + 512], pd[:, :],
                            dsc[:, nt:nt + 1],
                            xi[:, 512 * oh:512 * oh + 512],
                            ALU.mult, ALU.add),
                        reads=[xi, dsc], writes=[xo], inc="v")
                    bank_d[oh] = P.cnt["v"]
                xo_v = P.cnt["v"]
                P.wait("g", "v", xo_v)
                P.g(lambda e, xo=xo, nt=nt: e.dma_start(
                    x_out[128 * nt:128 * nt + 128, :], xo[:, :]))
                P.wait("s", "v", xo_v)
                post_fn(nt, xo, q)
                xofree[q] = P.cnt["v"]

        P.phase = "tern"
        with ExitStack() as PT:
            P.barrier()
            wbig = sbuf(PT, "wbig", [128, 32768], F32)
            tstg = [sbuf(PT, f"tstg{i}", [128, 4096], BF16) for i in range(2)]
            wtmp4 = [sbuf(PT, f"w4tmp{i}", [128, 4096], F32) for i in range(2)]
            acol = sbuf(PT, "acol", [128, 1], F32)
            mtiny = sbuf(PT, "mtiny", [1, 16], F32)
            ps_m = psum(PT, "ps_m", [128, 128])
            ps_mb = psum(PT, "ps_mb", [128, 2])
            wbig_free = 0
            for wi, (src3, nch3) in enumerate([(gT, CC), (uT, CC), (dT, 32)]):
                wd = 32768 // nch3
                if wbig_free:
                    P.wait("y", "v", wbig_free)
                toks = [P.y(lambda e, j=j, src3=src3, wd=wd: e.dma_start(
                    wbig[:, j * wd:(j + 1) * wd], src3[j, :, :]))
                    for j in range(nch3)]
                for t in toks:
                    P.waitd("v", t)
                P.v(lambda e: e.reduce_sum(acol[:, 0:1], wbig[:, :], AX.X,
                                           apply_absolute_value=True),
                    reads=[wbig], writes=[acol], inc="v")
                P.wait("t", "v", P.cnt["v"])
                P.t(lambda e: e.transpose(ps_m[0:1, 0:128], acol[:, 0:1],
                                          id32[:, :]), inc="t")
                P.wait("v", "t", P.cnt["t"])
                P.v(lambda e, wi=wi: e.reduce_sum(
                    mtiny[0:1, 4 * wi:4 * wi + 1], ps_m[0:1, 0:128], AX.X),
                    writes=[mtiny], inc="v")
                P.v(lambda e, wi=wi: e.tensor_scalar(
                    mtiny[0:1, 4 * wi + 1:4 * wi + 2],
                    mtiny[0:1, 4 * wi:4 * wi + 1],
                    1.0 / 4194304.0, 1e-5, ALU.mult, ALU.max),
                    reads=[mtiny], writes=[mtiny])
                P.v(lambda e, wi=wi: e.reciprocal(
                    mtiny[0:1, 4 * wi + 2:4 * wi + 3],
                    mtiny[0:1, 4 * wi + 1:4 * wi + 2]),
                    reads=[mtiny], writes=[mtiny], inc="v")
                P.wait("t", "v", P.cnt["v"])
                P.t(lambda e, wi=wi: e.matmul(
                    ps_mb[:, 0:2], ones32[0:1, :],
                    mtiny[0:1, 4 * wi + 1:4 * wi + 3],
                    start=True, stop=True), inc="t")
                P.wait("v", "t", P.cnt["t"])
                P.v(lambda e, wi=wi: e.tensor_copy(
                    mcol[:, 2 * wi:2 * wi + 2], ps_mb[:, 0:2]),
                    writes=[mcol], inc="v")
                stok = [None, None]
                for j in range(8):
                    b = tstg[j % 2]
                    wt = wtmp4[j % 2]
                    if stok[j % 2] is not None:
                        P.waitd("v", stok[j % 2])
                    P.v(lambda e, j=j, wt=wt, wi=wi: e.tensor_scalar(
                        wt[:, :], wbig[:, 4096 * j:4096 * (j + 1)],
                        mcol[:, 2 * wi + 1:2 * wi + 2], MAGIC,
                        ALU.mult, ALU.add), reads=[wbig, mcol], writes=[wt])
                    P.v(lambda e, wt=wt: e.tensor_scalar(
                        wt[:, :], wt[:, :], MAGIC, 1.0, ALU.subtract, ALU.min),
                        reads=[wt], writes=[wt])
                    P.v(lambda e, b=b, wt=wt: e.tensor_scalar(
                        b[:, :], wt[:, :], -1.0, 1.0, ALU.max, ALU.mult),
                        reads=[wt], writes=[b], inc="v")
                    P.wait("g", "v", P.cnt["v"])
                    stok[j % 2] = P.g(lambda e, b=b, j=j, wi=wi: e.dma_start(
                        scr_tern[wi][:, 4096 * j:4096 * (j + 1)], b[:, :]))
                wbig_free = P.cnt["v"]

        P.phase = "ffnA1"
        with ExitStack() as P4A:
            ffn_A(NT, scr_xnq1, scr_n8h1, srec1, dsc1, P4A)

        with ExitStack() as S5:
            bf2rep = sbuf(S5, "bf2rep", [128, D], F32)
            bf1c = sbuf(S5, "bf1c", [128, 32], F32)
            A2 = S5.enter_context(ExitStack())
            g1rep = sbuf(A2, "g1rep", [128, D], F32)
            b1rep = sbuf(A2, "b1rep", [128, D], F32)
            g2rep = sbuf(A2, "g2rep", [128, D], F32)
            b2rep = sbuf(A2, "b2rep", [128, D], F32)
            borep = sbuf(A2, "borep", [128, D], F32)
            with ExitStack() as BRP:
                P.barrier()
                brow4 = sbuf(BRP, "brow4", [1, 8 * D], F32)
                ps_bb = psum(BRP, "ps_bb", [128, 512])
                tbr = P.y(lambda e: e.dma_start(
                    brow4[0:1, :], brows[:, :].rearrange("r d -> () (r d)")))
                P.waitd("t", tbr)
                for rep, row in [(g1rep, 1), (b1rep, 2), (g2rep, 3),
                                 (b2rep, 4), (borep, 0), (bf2rep, 6)]:
                    for half in range(2):
                        P.t(lambda e, rep=rep, row=row, half=half: e.matmul(
                            ps_bb[:, :], ones32[0:1, :],
                            brow4[0:1, row * D + 512 * half:
                                  row * D + 512 * half + 512],
                            start=True, stop=True), inc="t")
                        P.wait("v", "t", P.cnt["t"])
                        P.v(lambda e, rep=rep, half=half: e.tensor_copy(
                            rep[:, 512 * half:512 * half + 512], ps_bb[:, :]),
                            writes=[rep], inc="v")
                        P.wait("t", "v", P.cnt["v"])

            P.phase = "ffnB1"
            with ExitStack() as P4B:
                sq5 = [sbuf(P4B, f"sq5{i}", [128, D], F32) for i in range(2)]
                yt5 = [sbuf(P4B, f"yt5{i}", [128, D], F32) for i in range(2)]
                h1t = [sbuf(P4B, f"h1t{i}", [128, D], F32) for i in range(2)]
                h1b = [sbuf(P4B, f"h1b{i}", [128, D], BF16) for i in range(2)]
                st5 = [sbuf(P4B, f"st5{i}", [128, 8], F32) for i in range(2)]
                st6 = [sbuf(P4B, f"st6{i}", [128, 8], F32) for i in range(2)]
                h1o = [sbuf(P4B, f"h1o{i}", [128, D], BF16)
                       for i in range(2)]
                ps_t5 = [psum(P4B, f"ps_t5{i}", [128, 1024], BF16)
                         for i in range(2)]
                sevac5 = [0, 0]
                yg = [None, None]
                h1g = [None, None]

                def post1(nt, xo, q):
                    sq, yt, ht, hb = sq5[q], yt5[q], h1t[q], h1b[q]
                    stt, stt2 = st5[q], st6[q]
                    ln_stats(stt, xo, sq, 1.0)
                    if yg[q] is not None:
                        P.waitd("v", yg[q])
                    P.v(lambda e, yt=yt, xo=xo, stt=stt: e.tensor_scalar(
                        yt[:, :], xo[:, :], stt[:, 2:3], stt[:, 6:7],
                        ALU.subtract, ALU.mult), reads=[xo, stt],
                        writes=[yt], inc="v")
                    P.wait("g", "v", P.cnt["v"])
                    yg[q] = P.g(lambda e, yt=yt, nt=nt: e.dma_start(
                        scr_y[128 * nt:128 * nt + 128, :], yt[:, :]))
                    ln_stats(stt2, yt, sq, 1.0)
                    P.v(lambda e, ht=ht, yt=yt, stt2=stt2: e.tensor_scalar(
                        ht[:, :], yt[:, :], stt2[:, 2:3], stt2[:, 6:7],
                        ALU.subtract, ALU.mult), reads=[yt, stt2],
                        writes=[ht])
                    P.v(lambda e, ht=ht: e.tensor_tensor(
                        ht[:, :], ht[:, :], g1rep[:, :], ALU.mult),
                        reads=[ht, g1rep], writes=[ht])
                    P.v(lambda e, hb=hb, ht=ht: e.tensor_tensor(
                        hb[:, :], ht[:, :], b1rep[:, :], ALU.add),
                        reads=[ht, b1rep], writes=[hb], inc="v")
                    pst = ps_t5[q]
                    transpose_1024(hb, pst, sevac5[q])
                    ho = h1o[q]
                    if h1g[q] is not None:
                        for tkh in h1g[q]:
                            P.waitd("s", tkh)
                    P.s(lambda e, ho=ho, pst=pst: e.copy(ho[:, :], pst[:, :]),
                        inc="s")
                    sevac5[q] = P.cnt["s"]
                    P.wait("g", "s", P.cnt["s"])
                    h1g[q] = [P.g(lambda e, ho=ho, cc=cc, nt=nt: e.dma_start(
                        scr_h1q[:, cc * R + 128 * nt:
                                cc * R + 128 * nt + 128],
                        ho[:, 128 * cc:128 * cc + 128])) for cc in range(CC)]

                ffn_B(NT, scr_n8h1, dsc1, scr_x2, scr_x3, P4B, post1)

            if upto == "x3":
                with ExitStack() as FIN:
                    finish_from_scratch(nc, P, scr_x3, out, FIN)
                return nc, cvals
            if upto == "yy1":
                with ExitStack() as FIN:
                    finish_from_scratch(nc, P, scr_y, out, FIN, off=256)
                return nc, cvals
            if upto in ("h11", "h17"):
                with ExitStack() as FIN:
                    htmp = sbuf(FIN, "htmp", [128, D], F32)
                    for sem, n_ in P.dma_done_all("g"):
                        P.wait("y", sem, n_)
                    base = 7 * R if upto == "h17" else 0
                    nj = 2 if upto == "h17" else 16
                    prev = None
                    for j in range(nj):
                        if prev is not None:
                            P.waitd("g", prev)
                        tk_ = P.g(lambda e, j=j: e.dma_start(
                            htmp[:, :],
                            scr_h1q[:, base + j * D:base + (j + 1) * D]))
                        P.waitd("y", tk_)
                        prev = P.y(lambda e, j=j: e.dma_start(
                            out[128 * j:128 * j + 128, :], htmp[:, :]))
                    for sem, n_ in P.dma_done_all("y"):
                        P.wait("y", sem, n_)
                    emit_blocks(nc, P)
                return nc, cvals

            P.phase = "rope"
            with ExitStack() as PH5:
                P.barrier()
                zf = sbuf(PH5, "zf", [128, CC * OWN], BF16)
                with ExitStack() as A25:
                    masksb = sbuf(A25, "masksb", [128, 16 * 384], BF16)
                    qbf = sbuf(A25, "qbf", [128, CC * OWN], BF16)
                    kdf = sbuf(A25, "kdf", [128, 4 * R], BF16)
                    vtm = sbuf(A25, "vtm", [128, NT * 256], BF16)
                    mtok = [P.g(lambda e, qt=qt: e.dma_start(
                        masksb[:, 384 * qt:384 * qt + 384],
                        masks_d[qt, :, :])) for qt in range(16)]
                    with ExitStack() as RP2:
                        cosF = sbuf(RP2, "cosF", [128, R], F32)
                        sinFS = sbuf(RP2, "sinFS", [128, R], F32)
                        bqc = sbuf(RP2, "bqc", [128, 8], F32)
                        bkc = sbuf(RP2, "bkc", [128, 4], F32)
                        bvrep = sbuf(RP2, "bvrep", [128, 256], F32)
                        RP3 = RP2.enter_context(ExitStack())
                        thcol = sbuf(RP3, "thcol", [128, 2], F32)
                        posr = sbuf(RP3, "posr", [1, R], F32)
                        bvrow = sbuf(RP3, "bvrow", [1, 512], F32)
                        frq = [sbuf(RP3, f"frq{i}", [128, 384], F32)
                               for i in range(4)]
                        ps_r5 = psum(RP3, "ps_r5", [128, 384])
                        for t in [P.y(lambda e: e.dma_start(thcol[:, :],
                                                            thsg[:, :])),
                                  P.y(lambda e: e.dma_start(posr[0:1, :],
                                                            posarr[:, :])),
                                  P.y(lambda e: e.dma_start(bqc[:, :],
                                                            bqcol[:, :])),
                                  P.y(lambda e: e.dma_start(bkc[:, :],
                                                            bkdcol[:, :])),
                                  P.y(lambda e: e.dma_start(bf1c[:, :],
                                                            bf1col[:, :])),
                                  P.y(lambda e: e.dma_start(bvrow[0:1, :],
                                                            bkvrow[1:2, :]))]:
                            P.waitd("v", t)
                            P.waitd("t", t)
                            P.waitd("s", t)
                        P.t(lambda e: e.matmul(ps_r5[:, 0:256], ones32[0:1, :],
                                               bvrow[0:1, 0:256], start=True,
                                               stop=True), inc="t")
                        P.wait("v", "t", P.cnt["t"])
                        P.v(lambda e: e.tensor_copy(bvrep[:, :],
                                                    ps_r5[:, 0:256]),
                            writes=[bvrep], inc="v")
                        P.wait("t", "v", P.cnt["v"])
                        for ch in range(6):
                            P.t(lambda e, ch=ch: e.matmul(
                                ps_r5[:, :], ones32[0:1, :],
                                posr[0:1, 384 * ch:384 * ch + 384],
                                start=True, stop=True), inc="t")
                            P.wait("v", "t", P.cnt["t"])
                            f0, f1, f2, f3 = frq
                            P.v(lambda e: e.tensor_scalar(
                                f0[:, :], ps_r5[:, :], thcol[:, 0:1], None,
                                ALU.mult), reads=[thcol], writes=[f0],
                                inc="v")
                            P.wait("t", "v", P.cnt["v"])
                            P.v(lambda e: e.tensor_scalar(
                                f1[:, :], f0[:, :],
                                float(1.0 / (2 * np.pi)), MAGIC,
                                ALU.mult, ALU.add), reads=[f0], writes=[f1])
                            P.v(lambda e: e.tensor_scalar(
                                f1[:, :], f1[:, :], MAGIC, None,
                                ALU.subtract), reads=[f1], writes=[f1])
                            P.v(lambda e: e.scalar_tensor_tensor(
                                f2[:, :], f1[:, :], float(-2 * np.pi),
                                f0[:, :], ALU.mult, ALU.add),
                                reads=[f1, f0], writes=[f2], inc="v")
                            P.v(lambda e: e.tensor_scalar(
                                f3[:, :], f0[:, :],
                                float(1.0 / (2 * np.pi)), 0.25,
                                ALU.mult, ALU.add), reads=[f0], writes=[f3])
                            P.v(lambda e: e.tensor_scalar(
                                f3[:, :], f3[:, :], MAGIC, MAGIC, ALU.add,
                                ALU.subtract), reads=[f3], writes=[f3])
                            P.v(lambda e: e.scalar_tensor_tensor(
                                f3[:, :], f3[:, :], float(-2 * np.pi),
                                f0[:, :], ALU.mult, ALU.add),
                                reads=[f3, f0], writes=[f3], inc="v")
                            P.wait("s", "v", P.cnt["v"])
                            P.s(lambda e, ch=ch: e.activation(
                                sinFS[:, 384 * ch:384 * ch + 384], f2[:, :],
                                AF.Sin), inc="s")
                            P.s(lambda e, ch=ch: e.activation(
                                cosF[:, 384 * ch:384 * ch + 384], f3[:, :],
                                AF.Sin, bias=cap(PI / 2), scale=1.0),
                                inc="s")
                            P.wait("v", "s", P.cnt["s"])
                            P.v(lambda e, ch=ch: e.tensor_scalar(
                                sinFS[:, 384 * ch:384 * ch + 384],
                                sinFS[:, 384 * ch:384 * ch + 384],
                                thcol[:, 1:2], None, ALU.mult),
                                reads=[sinFS, thcol], writes=[sinFS],
                                inc="v")

                        RP3.close()
                        for eng in ("y", "g"):
                            for sem in ("v", "s", "t"):
                                P.wait(eng, sem, P.cnt[sem])
                        if upto == "h27":
                            htmp = sbuf(RP2, "htmp", [128, D], F32)
                            P.wait("g", "v", P.cnt["v"])
                            P.wait("g", "s", P.cnt["s"])
                            prev = None
                            for j in range(2):
                                if prev is not None:
                                    P.waitd("g", prev)
                                tk_ = P.g(lambda e, j=j: e.dma_start(
                                    htmp[:, :],
                                    scr_h1q[:, 7 * R + j * D:
                                            7 * R + (j + 1) * D]))
                                P.waitd("y", tk_)
                                prev = P.y(lambda e, j=j: e.dma_start(
                                    out[128 * j:128 * j + 128, :],
                                    htmp[:, :]))
                            for sem, n_ in P.dma_done_all("y"):
                                P.wait("y", sem, n_)
                            emit_blocks(nc, P)
                            return nc, cvals
                        dbgq = sbuf(RP2, "dbgq", [128, 1024], F32) \
                            if upto in ("qp1", "hw1") else None
                        if upto == "rp1":
                            P.wait("g", "s", P.cnt["s"])
                            P.wait("g", "v", P.cnt["v"])
                            for j in range(2):
                                P.g(lambda e, j=j: e.dma_start(
                                    out[128 * j:128 * j + 128, :],
                                    cosF[:, j * D:(j + 1) * D]))
                                P.g(lambda e, j=j: e.dma_start(
                                    out[256 + 128 * j:256 + 128 * j + 128, :],
                                    sinFS[:, j * D:(j + 1) * D]))
                            for sem, n_ in P.dma_done_all("g"):
                                P.wait("g", sem, n_)
                            emit_blocks(nc, P)
                            return nc, cvals
                        P.phase = "5a_qkv"
                        with ExitStack() as QKV:
                            hwin = [sbuf(QKV, f"hwin{i}", [128, CC * 512],
                                         BF16) for i in range(2)]
                            qf32 = [sbuf(QKV, f"qf32{i}", [128, 512], F32)
                                    for i in range(2)]
                            qsw = [sbuf(QKV, f"qsw{i}", [128, 512], F32)
                                   for i in range(2)]
                            rt1 = sbuf(QKV, "rt1", [128, 512], F32)
                            rt2 = sbuf(QKV, "rt2", [128, 512], F32)
                            ps_q = [psum(QKV, f"ps_q{i}", [128, 512])
                                    for i in range(2)]
                            ps_sw = [psum(QKV, f"ps_sw{i}", [128, 512])
                                     for i in range(2)]
                            hw_used = [0, 0]

                            def qk_rope(n_oc, wsb, wwid, bias_ap, dst,
                                        dst_cols, chs, src_off, esc=1.0):
                                bank5 = [0, 0]
                                sevq = [0, 0]
                                for chi, (c0, cw) in enumerate(chs):
                                    hb_ = chi % 2
                                    hw = hwin[hb_]
                                    if hw_used[hb_]:
                                        P.wait("y", "t", hw_used[hb_])
                                    htoks = [P.y(lambda e, cc=cc, c0=c0,
                                                 cw=cw, hw=hw:
                                                 e.dma_start(
                                                     hw[:, cc * cw:
                                                        (cc + 1) * cw],
                                                     scr_h1q[:, cc * R +
                                                             src_off + c0:
                                                             cc * R +
                                                             src_off + c0 +
                                                             cw]))
                                             for cc in range(CC)]
                                    for t in htoks:
                                        P.waitd("t", t)
                                    for oc in range(n_oc):
                                        bk = (chi * n_oc + oc) % 2
                                        pq, psw = ps_q[bk], ps_sw[bk]
                                        qf, qs = qf32[bk], qsw[bk]
                                        if bank5[bk]:
                                            P.wait("t", "v", bank5[bk])
                                            P.wait("t", "s", sevq[bk])
                                        for cc in range(CC):
                                            P.t(lambda e, pq=pq, cc=cc,
                                                oc=oc, cw=cw, wsb=wsb,
                                                wwid=wwid, hw=hw: e.matmul(
                                                    pq[:, 0:cw],
                                                    wsb[:, cc * wwid +
                                                        128 * oc:
                                                        cc * wwid +
                                                        128 * oc + 128],
                                                    hw[:, cc * cw:
                                                       cc * cw + cw],
                                                    start=(cc == 0),
                                                    stop=(cc == CC - 1)),
                                                inc=("t" if cc == CC - 1
                                                     else None))
                                        P.wait("s", "t", P.cnt["t"])
                                        P.s(lambda e, qf=qf, pq=pq, cw=cw,
                                            oc=oc, bias_ap=bias_ap, esc=esc:
                                            e.activation(
                                                qf[:, 0:cw], pq[:, 0:cw],
                                                AF.Identity,
                                                bias=bias_ap[:, oc:oc + 1],
                                                scale=esc), inc="s")
                                        if dbgq is not None and oc == 0 \
                                                and chi == 0 and \
                                                dst_cols == OWN:
                                            P.wait("v", "s", P.cnt["s"])
                                            if upto == "qp1":
                                                P.v(lambda e, qf=qf, cw=cw:
                                                    e.tensor_copy(
                                                        dbgq[:, 0:cw],
                                                        qf[:, 0:cw]),
                                                    writes=[dbgq], inc="v")
                                            else:
                                                P.v(lambda e, hw=hw:
                                                    e.tensor_copy(
                                                        dbgq[:, 0:512],
                                                        hw[:, 2 * 512:
                                                           2 * 512 + 512]),
                                                    writes=[dbgq])
                                                P.v(lambda e, hw=hw:
                                                    e.tensor_copy(
                                                        dbgq[:, 512:1024],
                                                        hw[:, 7 * 512:
                                                           7 * 512 + 512]),
                                                    writes=[dbgq], inc="v")
                                        sevq[bk] = P.cnt["s"]
                                        P.wait("t", "s", P.cnt["s"])
                                        P.t(lambda e, psw=psw, qf=qf, cw=cw:
                                            e.matmul(
                                                psw[:, 0:cw], perm32[:, :],
                                                qf[:, 0:cw], start=True,
                                                stop=True), inc="t")
                                        P.wait("s", "t", P.cnt["t"])
                                        P.s(lambda e, qs=qs, psw=psw, cw=cw:
                                            e.copy(qs[:, 0:cw],
                                                   psw[:, 0:cw]), inc="s")
                                        P.wait("v", "s", P.cnt["s"])
                                        P.v(lambda e, qf=qf, c0=c0, cw=cw:
                                            e.tensor_tensor(
                                                rt1[:, 0:cw], qf[:, 0:cw],
                                                cosF[:, src_off + c0:
                                                     src_off + c0 + cw],
                                                ALU.mult),
                                            reads=[qf, cosF], writes=[rt1])
                                        P.v(lambda e, qs=qs, c0=c0, cw=cw:
                                            e.tensor_tensor(
                                                rt2[:, 0:cw], qs[:, 0:cw],
                                                sinFS[:, src_off + c0:
                                                      src_off + c0 + cw],
                                                ALU.mult),
                                            reads=[qs, sinFS], writes=[rt2])
                                        P.v(lambda e, oc=oc, c0=c0, cw=cw,
                                            dst=dst, dst_cols=dst_cols:
                                            e.tensor_tensor(
                                                dst[:, oc * dst_cols + c0:
                                                    oc * dst_cols + c0 + cw],
                                                rt1[:, 0:cw], rt2[:, 0:cw],
                                                ALU.add),
                                            reads=[rt1, rt2], writes=[dst],
                                            inc="v")
                                        bank5[bk] = P.cnt["v"]
                                    hw_used[hb_] = P.cnt["t"]

                            OCH = [(512 * i, 512) for i in range(4)]
                            with ExitStack() as WQS:
                                wq_b = sbuf(WQS, "wq_b", [128, CC * D], BF16)
                                for t in [P.g(lambda e, cc=cc: e.dma_start(
                                        wq_b[:, cc * D:(cc + 1) * D],
                                        bWq[cc, :, :])) for cc in range(CC)]:
                                    P.waitd("t", t)
                                qk_rope(CC, wq_b, D, bqc, qbf, OWN, OCH, 256)
                                P.barrier()
                            with ExitStack() as WKS:
                                wkd_b = sbuf(WKS, "wkd_b", [128, CC * 512],
                                             BF16)
                                for t in [P.g(lambda e, cc=cc: e.dma_start(
                                        wkd_b[:, cc * 512:(cc + 1) * 512],
                                        bWkd[cc, :, :])) for cc in range(CC)]:
                                    P.waitd("t", t)
                                qk_rope(4, wkd_b, 512, bkc, kdf, R, QCH, 0,
                                        esc=0.125)
                                P.barrier()
                            wv_b = sbuf(QKV, "wv_b", [128, CC * 256], BF16)
                            for t in [P.g(lambda e, cc=cc: e.dma_start(
                                    wv_b[:, cc * 256:(cc + 1) * 256],
                                    bWv[cc, :, :])) for cc in range(CC)]:
                                P.waitd("t", t)
                            bank5v = [0, 0]
                            for nt in range(NT):
                                hb_ = nt % 2
                                hw = hwin[hb_]
                                if hw_used[hb_]:
                                    P.wait("y", "t", hw_used[hb_])
                                htoks = [P.y(lambda e, cc=cc, nt=nt, hw=hw:
                                             e.dma_start(
                                                 hw[:, cc * 128:
                                                    (cc + 1) * 128],
                                                 scr_h1q[:, cc * R +
                                                         128 * nt:
                                                         cc * R +
                                                         128 * nt + 128]))
                                         for cc in range(CC)]
                                for t in htoks:
                                    P.waitd("t", t)
                                bk = nt % 2
                                pq = ps_q[bk]
                                if bank5v[bk]:
                                    P.wait("t", "v", bank5v[bk])
                                for cc in range(CC):
                                    P.t(lambda e, pq=pq, cc=cc, hw=hw:
                                        e.matmul(
                                            pq[:, 0:256],
                                            hw[:, cc * 128:cc * 128 + 128],
                                            wv_b[:, cc * 256:
                                                 (cc + 1) * 256],
                                            start=(cc == 0),
                                            stop=(cc == CC - 1)),
                                        inc=("t" if cc == CC - 1 else None))
                                hw_used[hb_] = P.cnt["t"]
                                P.wait("v", "t", P.cnt["t"])
                                P.v(lambda e, pq=pq, nt=nt: e.tensor_tensor(
                                    vtm[:, 256 * nt:256 * nt + 256],
                                    pq[:, 0:256], bvrep[:, :], ALU.add),
                                    reads=[bvrep], writes=[vtm], inc="v")
                                bank5v[bk] = P.cnt["v"]

                    def dump16(t_, n=16):
                        P.wait("g", "s", P.cnt["s"])
                        P.wait("g", "v", P.cnt["v"])
                        for j in range(min(n, t_.shape[1] // D)):
                            P.g(lambda e, j=j, t_=t_: e.dma_start(
                                out[128 * j:128 * j + 128, :],
                                t_[:, j * D:(j + 1) * D]))
                        for sem, n_ in P.dma_done_all("g"):
                            P.wait("g", sem, n_)
                        emit_blocks(nc, P)

                    if upto == "qb1":
                        dump16(qbf)
                        return nc, cvals
                    if upto in ("qp1", "hw1"):
                        dump16(dbgq, 1)
                        return nc, cvals
                    if upto == "kd1":
                        dump16(kdf)
                        return nc, cvals
                    if upto == "vt1":
                        dump16(vtm)
                        return nc, cvals

                    P.phase = "5b_att"
                    with ExitStack() as ATT:
                        P.barrier()
                        for t in mtok:
                            P.waitd("v", t)
                        dbgs = sbuf(ATT, "dbgs", [128, 2048], F32) \
                            if upto == "sm1" else None
                        smt = sbuf(ATT, "smt", [128, 4 * 384], F32)
                        pex = sbuf(ATT, "pex", [128, 4 * 384], F32)
                        pb = sbuf(ATT, "pb", [128, 4 * 384], BF16)
                        pbt = sbuf(ATT, "pbt", [128, 12 * 128], BF16)
                        nmx = sbuf(ATT, "nmx", [128, 4], F32)
                        zz = sbuf(ATT, "zz", [128, 8], F32)
                        ps_sc = psum(ATT, "ps_sc", [128, 4 * 512])
                        ps_pt = psum(ATT, "ps_pt", [128, 12 * 128], BF16)
                        ps_z1 = psum(ATT, "ps_z1", [128, 128])
                        ps_z2 = psum(ATT, "ps_z2", [128, 128])
                        sc_free_v = 0
                        pt_free_v = 0
                        z_free_v = 0
                        for g in range(4):
                            kb = 64 * g
                            for qt in range(16):
                                if sc_free_v:
                                    P.wait("t", "v", sc_free_v)
                                for hq in range(4):
                                    h = 4 * g + hq
                                    qb = 64 * (h % 2)
                                    P.t(lambda e, h=h, hq=hq, qt=qt, qb=qb,
                                        g=g: e.matmul(
                                            ps_sc[:, 512 * hq:
                                                  512 * hq + 384],
                                            qbf[qb:qb + 64,
                                                (h // 2) * OWN + 128 * qt:
                                                (h // 2) * OWN +
                                                128 * qt + 128],
                                            kdf[qb:qb + 64,
                                                g * R + 128 * qt:
                                                g * R + 128 * qt + 384],
                                            start=True, stop=True,
                                            tile_position=(qb, 0)),
                                        inc=("t" if hq == 3 else None))
                                P.wait("v", "t", P.cnt["t"])
                                m3 = masksb[:, 384 * qt:384 * qt + 384]
                                P.v(lambda e, m3=m3: e.tensor_tensor(
                                    smt[:, :].rearrange(
                                        "p (h w) -> p h w", h=4),
                                    ps_sc[:, :].rearrange(
                                        "p (h w) -> p h w", h=4)[:, :, 0:384],
                                    m3.rearrange("p (o w) -> p o w", o=1)
                                    .broadcast_to((128, 4, 384)), ALU.add),
                                    reads=[masksb], writes=[smt], inc="v")
                                if dbgs is not None and g == 0 and qt == 0:
                                    P.v(lambda e: e.tensor_copy(
                                        dbgs[:, 0:1536], smt[:, :]),
                                        reads=[smt], writes=[dbgs], inc="v")
                                sc_free_v = P.cnt["v"]
                                P.v(lambda e: e.tensor_reduce(
                                    nmx[:, :],
                                    smt[:, :].rearrange(
                                        "p (h w) -> p h w", h=4),
                                    AX.X, ALU.max, negate=True),
                                    reads=[smt], writes=[nmx], inc="v")
                                P.wait("s", "v", P.cnt["v"])
                                for hq in range(4):
                                    P.s(lambda e, hq=hq: e.activation(
                                        pex[:, 384 * hq:384 * hq + 384],
                                        smt[:, 384 * hq:384 * hq + 384],
                                        AF.Exp, bias=nmx[:, hq:hq + 1],
                                        scale=1.0,
                                        accum_out=zz[:, hq:hq + 1]),
                                        inc=("s" if hq == 3 else None))
                                P.wait("v", "s", P.cnt["s"])
                                P.v(lambda e: e.reciprocal(zz[:, 4:8],
                                                           zz[:, 0:4]),
                                    writes=[zz], inc="v")
                                for hq in range(4):
                                    P.v(lambda e, hq=hq: e.tensor_scalar(
                                        pb[:, 384 * hq:384 * hq + 384],
                                        pex[:, 384 * hq:384 * hq + 384],
                                        zz[:, 4 + hq:5 + hq], None,
                                        ALU.mult),
                                        reads=[pex, zz], writes=[pb],
                                        inc=("v" if hq == 3 else None))
                                P.wait("t", "v", P.cnt["v"])
                                if pt_free_v:
                                    P.wait("t", "v", pt_free_v)
                                for blk in range(12):
                                    P.t(lambda e, blk=blk: e.transpose(
                                        ps_pt[:, 128 * blk:128 * blk + 128],
                                        pb[:, 128 * blk:128 * blk + 128],
                                        id16[:, :]),
                                        inc=("t" if blk == 11 else None))
                                P.wait("v", "t", P.cnt["t"])
                                P.v(lambda e: e.tensor_copy(pbt[:, :],
                                                            ps_pt[:, :]),
                                    writes=[pbt], inc="v")
                                pt_free_v = P.cnt["v"]
                                P.wait("t", "v", P.cnt["v"])
                                if z_free_v:
                                    P.wait("t", "v", z_free_v)
                                for rnd in range(2):
                                    for sub in range(2):
                                        hq = 2 * rnd + sub
                                        psz = ps_z1 if sub == 0 else ps_z2
                                        zlo = 64 * (hq % 2)
                                        for j in range(3):
                                            P.t(lambda e, hq=hq, j=j, qt=qt,
                                                kb=kb, psz=psz, zlo=zlo:
                                                e.matmul(
                                                    psz[zlo:zlo + 64, 0:128],
                                                    vtm[:, 256 * (qt + j) +
                                                        kb:
                                                        256 * (qt + j) +
                                                        kb + 64],
                                                    pbt[:, 384 * hq +
                                                        128 * j:
                                                        384 * hq +
                                                        128 * j + 128],
                                                    start=(j == 0),
                                                    stop=(j == 2),
                                                    tile_position=(0, zlo)),
                                                inc=("t" if j == 2
                                                     else None))
                                    P.wait("v", "t", P.cnt["t"])
                                    for sub in range(2):
                                        hq = 2 * rnd + sub
                                        psz = ps_z1 if sub == 0 else ps_z2
                                        zlo = 64 * (hq % 2)
                                        h = 4 * g + hq
                                        P.v(lambda e, psz=psz, zlo=zlo, h=h,
                                            qt=qt: e.tensor_copy(
                                                zf[zlo:zlo + 64,
                                                   (h // 2) * OWN +
                                                   128 * qt:
                                                   (h // 2) * OWN +
                                                   128 * qt + 128],
                                                psz[zlo:zlo + 64, 0:128]),
                                            writes=[zf],
                                            inc=("v" if sub == 1
                                                 else None))
                                    z_free_v = P.cnt["v"]
                                    if rnd == 0:
                                        P.wait("t", "v", z_free_v)
                        if upto == "sm1":
                            P.wait("g", "s", P.cnt["s"])
                            P.wait("g", "v", P.cnt["v"])
                            for j in range(2):
                                P.g(lambda e, j=j: e.dma_start(
                                    out[128 * j:128 * j + 128, :],
                                    dbgs[:, j * D:(j + 1) * D]))
                            for sem, n_ in P.dma_done_all("g"):
                                P.wait("g", sem, n_)
                            emit_blocks(nc, P)
                            return nc, cvals

                if upto == "zf1":
                    P.wait("g", "s", P.cnt["s"])
                    P.wait("g", "v", P.cnt["v"])
                    for j in range(16):
                        P.g(lambda e, j=j: e.dma_start(
                            out[128 * j:128 * j + 128, :],
                            zf[:, j * D:(j + 1) * D]))
                    for sem, n_ in P.dma_done_all("g"):
                        P.wait("g", sem, n_)
                    emit_blocks(nc, P)
                    return nc, cvals

                P.phase = "5c_prj"
                with ExitStack() as PRJ:
                    P.barrier()
                    wo_b = sbuf(PRJ, "wo_b", [128, CC * D], BF16)
                    yt = [sbuf(PRJ, f"yt6{i}", [128, D], F32)
                          for i in range(2)]
                    xbt = [sbuf(PRJ, f"xbt{i}", [128, D], F32)
                           for i in range(2)]
                    sq6 = [sbuf(PRJ, f"sq6{i}", [128, D], F32)
                           for i in range(2)]
                    h2t = [sbuf(PRJ, f"h2t{i}", [128, D], F32)
                           for i in range(2)]
                    h2b = [sbuf(PRJ, f"h2b{i}", [128, D], BF16)
                           for i in range(2)]
                    h2o = [sbuf(PRJ, f"h2o{i}", [128, D], BF16)
                           for i in range(2)]
                    st7 = [sbuf(PRJ, f"st7{i}", [128, 8], F32)
                           for i in range(2)]
                    ps_o5 = [psum(PRJ, f"ps_o5{i}", [128, 512])
                             for i in range(2)]
                    ps_t6 = [psum(PRJ, f"ps_t6{i}", [128, 1024], BF16)
                             for i in range(2)]
                    for t in [P.g(lambda e, cc=cc: e.dma_start(
                            wo_b[:, cc * D:(cc + 1) * D], bWo[cc, :, :]))
                            for cc in range(CC)]:
                        P.waitd("t", t)
                    bank6 = [0, 0]
                    sev6 = [0, 0]
                    h2g = [None, None]
                    xbused = [0, 0]
                    for nt in range(ONT):
                        q = nt % 2
                        ya, xb_ = yt[q], xbt[q]
                        if xbused[q]:
                            P.wait("y", "v", xbused[q])
                        tk = P.y(lambda e, ya=ya, nt=nt: e.dma_start(
                            ya[:, :], scr_y[256 + 128 * nt:
                                            256 + 128 * nt + 128, :]))
                        for oh in range(2):
                            po = ps_o5[oh]
                            if bank6[oh]:
                                P.wait("t", "v", bank6[oh])
                            for cc in range(CC):
                                P.t(lambda e, po=po, cc=cc, nt=nt, oh=oh:
                                    e.matmul(
                                        po[:, :],
                                        zf[:, cc * OWN + 128 * nt:
                                           cc * OWN + 128 * nt + 128],
                                        wo_b[:, cc * D + 512 * oh:
                                             cc * D + 512 * oh + 512],
                                        start=(cc == 0),
                                        stop=(cc == CC - 1)),
                                    inc=("t" if cc == CC - 1 else None))
                        P.wait("v", "t", P.cnt["t"])
                        P.waitd("v", tk)
                        for oh in range(2):
                            po = ps_o5[oh]
                            P.v(lambda e, po=po, xb_=xb_, oh=oh:
                                e.scalar_tensor_tensor(
                                    xb_[:, 512 * oh:512 * oh + 512],
                                    po[:, :], 1.0,
                                    borep[:, 512 * oh:512 * oh + 512],
                                    ALU.mult, ALU.add),
                                reads=[borep], writes=[xb_], inc="v")
                            bank6[oh] = P.cnt["v"]
                        P.v(lambda e, xb_=xb_, ya=ya: e.tensor_add(
                            xb_[:, :], xb_[:, :], ya[:, :]),
                            reads=[xb_, ya], writes=[xb_], inc="v")
                        P.wait("g", "v", P.cnt["v"])
                        P.g(lambda e, xb_=xb_, nt=nt: e.dma_start(
                            scr_xb[128 * nt:128 * nt + 128, :], xb_[:, :]))
                        stt = st7[q]
                        ln_stats(stt, xb_, sq6[q], 1.0)
                        P.v(lambda e, q=q, xb_=xb_, stt=stt: e.tensor_scalar(
                            h2t[q][:, :], xb_[:, :], stt[:, 2:3],
                            stt[:, 6:7], ALU.subtract, ALU.mult),
                            reads=[xb_, stt], writes=[h2t[q]])
                        P.v(lambda e, q=q: e.tensor_tensor(
                            h2t[q][:, :], h2t[q][:, :], g2rep[:, :],
                            ALU.mult),
                            reads=[h2t[q], g2rep], writes=[h2t[q]])
                        P.v(lambda e, q=q: e.tensor_tensor(
                            h2b[q][:, :], h2t[q][:, :], b2rep[:, :],
                            ALU.add),
                            reads=[h2t[q], b2rep], writes=[h2b[q]],
                            inc="v")
                        xbused[q] = P.cnt["v"]
                        pst = ps_t6[q]
                        transpose_1024(h2b[q], pst, sev6[q])
                        if h2g[q] is not None:
                            P.waitd("s", h2g[q])
                        P.s(lambda e, q=q, pst=pst: e.copy(
                            h2o[q][:, :], pst[:, :]), inc="s")
                        sev6[q] = P.cnt["s"]
                        P.wait("g", "s", P.cnt["s"])
                        h2g[q] = P.g(lambda e, q=q, nt=nt: e.dma_start(
                            scr_h2q[nt // 2, :, :]
                            .rearrange("p (c w) -> p c w", c=CC)
                            [:, :, 128 * (nt % 2):128 * (nt % 2) + 128],
                            h2o[q][:, :].rearrange(
                                "p (c w) -> p c w", c=CC)))

            A2.close()
            if upto == "xb":
                with ExitStack() as FIN:
                    finish_from_scratch(nc, P, scr_xb, out, FIN, off=0)
                return nc, cvals

            P.phase = "5d_mlp"
            with ExitStack() as MLP:
                P.barrier()
                wf1_b = sbuf(MLP, "wf1_b", [128, CC * DFF], BF16)
                wf2_b = sbuf(MLP, "wf2_b", [128, 32 * D], BF16)
                h2qs = [sbuf(MLP, f"h2qs{i}", [128, CC * 256], BF16)
                        for i in range(2)]
                gact = sbuf(MLP, "gact", [128, 32 * 256], BF16)
                x3t7 = [sbuf(MLP, f"x3t7{i}", [128, D], F32)
                        for i in range(2)]
                xbt7 = [sbuf(MLP, f"xbt7{i}", [128, D], F32)
                        for i in range(2)]
                x4t = [sbuf(MLP, f"x4t{i}", [128, D], F32) for i in range(2)]
                sq7 = [sbuf(MLP, f"sq7{i}", [128, D], F32) for i in range(2)]
                ln7 = [sbuf(MLP, f"ln7{i}", [128, D], F32) for i in range(2)]
                n87 = [sbuf(MLP, f"n87{i}", [128, D], BF16)
                       for i in range(2)]
                xq7 = [sbuf(MLP, f"xq7{i}", [128, D], BF16)
                       for i in range(2)]
                st8 = [sbuf(MLP, f"st8{i}", [128, 8], F32) for i in range(2)]
                aq7 = [sbuf(MLP, f"aq7{i}", [128, 4], F32) for i in range(2)]
                ps_f1 = [psum(MLP, f"ps_f1{i}", [128, 256])
                         for i in range(2)]
                ps_f2 = [psum(MLP, f"ps_f2{i}", [128, 512])
                         for i in range(2)]
                ps_t7 = [psum(MLP, f"ps_t7{i}", [128, 1024], BF16)
                         for i in range(2)]
                wtk = [P.g(lambda e, cc=cc: e.dma_start(
                    wf1_b[:, cc * DFF:(cc + 1) * DFF], bWf1[cc, :, :]))
                    for cc in range(CC)]
                wtk += [P.g(lambda e, ic=ic: e.dma_start(
                    wf2_b[:, ic * D:(ic + 1) * D], bWf2[ic, :, :]))
                    for ic in range(32)]
                for t in wtk:
                    P.waitd("t", t)
                h2used = [0, 0]
                bank7 = [0, 0]
                bank8 = [0, 0]
                sev7 = [0, 0]
                xq7g = [None, None]
                gact_tfree = 0
                x4used = [0, 0]
                for nch in range(8):
                    q = nch % 2
                    hs = h2qs[q]
                    if h2used[q]:
                        P.wait("y", "t", h2used[q])
                    htok = P.y(lambda e, hs=hs, nch=nch: e.dma_start(
                        hs[:, :], scr_h2q[nch, :, :]))
                    P.waitd("t", htok)
                    for ic in range(32):
                        bk = ic % 2
                        pf = ps_f1[bk]
                        if bank7[bk]:
                            P.wait("t", "s", bank7[bk])
                        if ic == 0 and gact_tfree:
                            P.wait("s", "t", gact_tfree)
                        for cc in range(CC):
                            P.t(lambda e, pf=pf, cc=cc, ic=ic, hs=hs:
                                e.matmul(
                                    pf[:, :],
                                    wf1_b[:, cc * DFF + 128 * ic:
                                          cc * DFF + 128 * ic + 128],
                                    hs[:, cc * 256:(cc + 1) * 256],
                                    start=(cc == 0), stop=(cc == CC - 1)),
                                inc=("t" if cc == CC - 1 else None))
                        P.wait("s", "t", P.cnt["t"])
                        P.s(lambda e, pf=pf, ic=ic: e.activation(
                            gact[:, 256 * ic:256 * ic + 256], pf[:, :],
                            AF.Gelu, bias=bf1c[:, ic:ic + 1], scale=1.0),
                            inc="s")
                        bank7[bk] = P.cnt["s"]
                    h2used[q] = P.cnt["t"]
                    P.wait("t", "s", P.cnt["s"])
                    for sub in range(2):
                        nt = 2 * nch + sub
                        q2 = nt % 2
                        x3a, xba, x4a = x3t7[q2], xbt7[q2], x4t[q2]
                        if x4used[q2]:
                            P.wait("y", "v", x4used[q2])
                        t3k = P.y(lambda e, x3a=x3a, nt=nt: e.dma_start(
                            x3a[:, :], scr_x3[256 + 128 * nt:
                                              256 + 128 * nt + 128, :]))
                        tbk = P.y(lambda e, xba=xba, nt=nt: e.dma_start(
                            xba[:, :],
                            scr_xb[128 * nt:128 * nt + 128, :]))
                        for oh in range(2):
                            pf2 = ps_f2[oh]
                            if bank8[oh]:
                                P.wait("t", "v", bank8[oh])
                            for ic in range(32):
                                P.t(lambda e, pf2=pf2, ic=ic, oh=oh,
                                    sub=sub: e.matmul(
                                        pf2[:, :],
                                        gact[:, 256 * ic + 128 * sub:
                                             256 * ic + 128 * sub + 128],
                                        wf2_b[:, ic * D + 512 * oh:
                                              ic * D + 512 * oh + 512],
                                        start=(ic == 0), stop=(ic == 31)),
                                    inc=("t" if ic == 31 else None))
                        gact_tfree = P.cnt["t"]
                        P.wait("v", "t", P.cnt["t"])
                        P.waitd("v", t3k)
                        P.waitd("v", tbk)
                        for oh in range(2):
                            pf2 = ps_f2[oh]
                            P.v(lambda e, pf2=pf2, x4a=x4a, oh=oh:
                                e.scalar_tensor_tensor(
                                    x4a[:, 512 * oh:512 * oh + 512],
                                    pf2[:, :], 1.0,
                                    bf2rep[:, 512 * oh:512 * oh + 512],
                                    ALU.mult, ALU.add),
                                reads=[bf2rep], writes=[x4a], inc="v")
                            bank8[oh] = P.cnt["v"]
                        P.v(lambda e, x4a=x4a, xba=xba: e.tensor_add(
                            x4a[:, :], x4a[:, :], xba[:, :]),
                            reads=[x4a, xba], writes=[x4a])
                        P.v(lambda e, x4a=x4a, x3a=x3a: e.tensor_add(
                            x4a[:, :], x4a[:, :], x3a[:, :]),
                            reads=[x4a, x3a], writes=[x4a], inc="v")
                        P.wait("g", "v", P.cnt["v"])
                        P.g(lambda e, x4a=x4a, nt=nt: e.dma_start(
                            scr_x4[128 * nt:128 * nt + 128, :], x4a[:, :]))
                        lnq_tile(st8[q2], x4a, sq7[q2], ln7[q2], n87[q2],
                                 aq7[q2], srec2, nt)
                        x4used[q2] = P.cnt["v"]
                        pst = ps_t7[q2]
                        transpose_1024(n87[q2], pst, sev7[q2])
                        if xq7g[q2] is not None:
                            P.waitd("s", xq7g[q2])
                        P.s(lambda e, q2=q2, pst=pst: e.copy(
                            xq7[q2][:, :], pst[:, :]), inc="s")
                        sev7[q2] = P.cnt["s"]
                        P.wait("g", "s", P.cnt["s"])
                        xq7g[q2] = P.g(lambda e, q2=q2, nt=nt:
                                       e.dma_start(scr_xnq2[nt, :, :],
                                                   xq7[q2][:, :]))

            if upto == "x4":
                with ExitStack() as FIN:
                    finish_from_scratch(nc, P, scr_x4, out, FIN, off=0)
                return nc, cvals

        P.phase = "ffn2"
        with ExitStack() as P6A:
            ffn_A(ONT, scr_xnq2, scr_n8h2, srec2, dsc2, P6A, tg="f2")
        with ExitStack() as P6B:
            ffn_B(ONT, scr_n8h2, dsc2, scr_x4, out, P6B,
                  lambda nt, xo, q: None, tg="f2")
        for sem, n in P.dma_done_all("g"):
            P.wait("y", sem, n)
        for sem, n in P.dma_done_all("y"):
            P.wait("y", sem, n)
        emit_blocks(nc, P)
        return nc, cvals


def finish_from_scratch(nc, P, scr, out, stk, off=256):
    cpt = stk.enter_context(nc.sbuf_tensor("cpt_fin", [128, D], F32))
    for sem, n in P.dma_done_all("g"):
        P.wait("y", sem, n)
    prev = None
    for i in range(OWN // 128):
        if prev is not None:
            P.waitd("y", prev)
        tk = P.y(lambda e, i=i: e.dma_start(
            cpt[:, :], scr[off + 128 * i:off + 128 * i + 128, :]))
        P.waitd("y", tk)
        prev = P.y(lambda e, i=i: e.dma_start(
            out[128 * i:128 * i + 128, :], cpt[:, :]))
    for sem, n in P.dma_done_all("y"):
        P.wait("y", sem, n)
    emit_blocks(nc, P)


def emit_blocks(nc, P):
    with nc.Block() as block:
        @block.vector
        def _(eng):
            for f in P.ops["v"]:
                f(eng)

        @block.scalar
        def _(eng):
            for f in P.ops["s"]:
                f(eng)

        @block.tensor
        def _(eng):
            for f in P.ops["t"]:
                f(eng)

        @block.gpsimd
        def _(eng):
            for f in P.ops["g"]:
                f(eng)

        @block.sync
        def _(eng):
            for f in P.ops["y"]:
                f(eng)


def make_host_inputs(inp, core, cvals):
    b, h = core // 2, core % 2
    x = np.asarray(inp["L0_attn_a_x"][b], dtype=np.float32)
    xrot = np.roll(x, 2048, axis=0) if h == 0 else x
    t0 = 2048 * h

    def t3(a, chunks):
        a = np.asarray(a, dtype=np.float32)
        return np.ascontiguousarray(a.reshape(chunks, 128, a.shape[1]))

    m = {}
    m["xrot"] = np.ascontiguousarray(xrot)
    m["aWqT"] = t3(np.asarray(inp["L0_attn_a_Wq"]).T, CC)
    m["aWkvT"] = t3(np.asarray(inp["L0_attn_a_Wkv"]).T, CC)
    m["aWpT"] = t3(np.asarray(inp["L0_attn_a_Wproj"]).T, CC)
    m["gT"] = t3(np.asarray(inp["L0_ffn_w_gate"]).T, CC)
    m["uT"] = t3(np.asarray(inp["L0_ffn_w_up"]).T, CC)
    m["dT"] = t3(np.asarray(inp["L0_ffn_w_down"]).T, 32)
    m["bWq"] = t3(np.asarray(inp["L0_attn_b_Wq"]), CC)
    wk = np.asarray(inp["L0_attn_b_Wk"], dtype=np.float32)
    wkd = np.concatenate([np.concatenate([wk[:, 64 * g:64 * g + 64]] * 2,
                                         axis=1) for g in range(4)], axis=1)
    m["bWkd"] = t3(wkd, CC)
    m["bWv"] = t3(np.asarray(inp["L0_attn_b_Wv"]), CC)
    m["bWo"] = t3(np.asarray(inp["L0_attn_b_Wo"]), CC)
    m["bWf1"] = t3(np.asarray(inp["L0_attn_b_Wf1"]), CC)
    m["bWf2"] = t3(np.asarray(inp["L0_attn_b_Wf2"]), 32)
    brws = np.zeros((8, D), dtype=np.float32)
    brws[0] = inp["L0_attn_b_bo"]
    brws[1] = inp["L0_attn_b_g1"]
    brws[2] = inp["L0_attn_b_b1"]
    brws[3] = inp["L0_attn_b_g2"]
    brws[4] = inp["L0_attn_b_b2"]
    brws[5] = inp["L0_attn_a_bproj"]
    brws[6] = inp["L0_attn_b_bf2"]
    m["brows"] = brws
    bk = np.asarray(inp["L0_attn_b_bk"], dtype=np.float32)
    bkd = np.concatenate([np.concatenate([bk[64 * g:64 * g + 64]] * 2)
                          for g in range(4)])
    bkv = np.zeros((2, 512), dtype=np.float32)
    bkv[0] = bkd
    bkv[1, 0:256] = inp["L0_attn_b_bv"]
    m["bkvrow"] = bkv
    m["bqcol"] = np.ascontiguousarray(
        np.asarray(inp["L0_attn_b_bq"], dtype=np.float32).reshape(8, 128).T)
    m["bkdcol"] = np.ascontiguousarray((bkd / 8.0).reshape(4, 128).T)
    m["bf1col"] = np.ascontiguousarray(
        np.asarray(inp["L0_attn_b_bf1"], dtype=np.float32).reshape(32, 128).T)
    mk = np.zeros((16, 128, 384), dtype=np.float32)
    for qt in range(16):
        p = 256 + 128 * qt + np.arange(128)[:, None]
        n = 128 * qt + np.arange(384)[None, :]
        valid = (n <= p) & (p - n < 256)
        if h == 0:
            valid &= (n >= 256)
        mk[qt] = np.where(valid, np.float32(0.0), np.float32(-1e9))
    m["masks"] = mk
    m["posarr"] = (t0 - 256 + np.arange(R, dtype=np.float32)).reshape(1, R)
    th = np.zeros((128, 2), dtype=np.float32)
    j = np.arange(128) % 32
    th[:, 0] = (1.0 / 10000.0 ** (j / 32.0)).astype(np.float32)
    th[:, 1] = np.where((np.arange(128) % 64) < 32, -1.0, 1.0)
    m["thsg"] = th
    m["ident"] = np.eye(128, dtype=np.float32)
    pm = np.zeros((128, 128), dtype=np.float32)
    for mm in range(128):
        swap = (mm + 32) % 64 + 64 * (mm // 64)
        pm[swap, mm] = 1.0
    m["perm"] = pm
    cpv = np.zeros(16, dtype=np.float32)
    cpv[:len(cvals)] = cvals
    m["cpool"] = np.tile(cpv, (128, 1))
    return m


_cache = {}


def kernel(**inputs):
    if "nc" not in _cache:
        _cache["nc"], _cache["cvals"] = build_program("full")
    nc = _cache["nc"]
    in_maps = [make_host_inputs(inputs, c, _cache["cvals"]) for c in range(8)]
    res = run_bass_kernel_spmd(nc, in_maps, core_ids=list(range(8)))
    outp = np.empty((4, 4096, 1024), dtype=np.float32)
    for c in range(8):
        b, h = c // 2, c % 2
        outp[b, 2048 * h:2048 * h + 2048, :] = res.results[c]["out"]
    return outp
